# version 05
# baseline (speedup 1.0000x reference)
from contextlib import ExitStack

import numpy as np

import concourse.bass as bass
import concourse.mybir as mybir
from concourse.bass_utils import run_bass_kernel_spmd

F32 = mybir.dt.float32
BF16 = mybir.dt.bfloat16
AF = mybir.ActivationFunctionType
ALU = mybir.AluOpType
AX = mybir.AxisListType

ENGINES = ("sync", "gpsimd", "tensor", "vector", "scalar")
CHAINED = ("vector", "scalar", "gpsimd")

D = 1024
H = 16
HD = 64
I3 = 3072
FI = 4096
FF = 2048
NE = 8
EPS = 1e-5
GELU_C = 0.7978845608028654
GELU_A = 0.044715
SCALE_A = 0.125
SCALE_B = (2 * 1024) ** -0.5
MASK_NEG = -1.0e9
CONST_VALS = (0.0, 1.0, EPS, -0.5, -1.0, 0.5, 2.0, SCALE_A, SCALE_B, GELU_C,
              GELU_A, 1.0 / D, 1.0 / 3.0)


class Em:

    def __init__(self, nc, es):
        self.nc = nc
        self.es = es
        self.q = {e: [] for e in ENGINES}
        self.sems = {}
        self.cnt = {}
        self.mark = {e: {} for e in ENGINES}
        self.last = {e: None for e in ENGINES}

    def sem(self, name):
        if name not in self.sems:
            self.sems[name] = self.es.enter_context(self.nc.semaphore(name))
            self.cnt[name] = 0
        return self.sems[name]

    def op(self, eng, fn, waits=(), sem=None, inc=0):
        if eng in CHAINED:
            self.wait(eng, self.last[eng])
            if not inc:
                sem, inc = "c_" + eng[0], 1
        for t in waits:
            self.wait(eng, t)
        if inc:
            s = self.sem(sem)
            self.cnt[sem] += inc
            val = self.cnt[sem]

            def emit(h, fn=fn, s=s, inc=inc):
                fn(h).then_inc(s, inc)

            self.q[eng].append(emit)
            tok = (sem, val)
            if eng in CHAINED and inc == 1:
                self.last[eng] = tok
            return tok
        self.q[eng].append(fn)
        return None

    def wait(self, eng, tok):
        if tok is None:
            return
        name, val = tok
        if name.startswith("d_") and val != self.cnt[name]:
            val = self.cnt[name]
        if self.mark[eng].get(name, 0) >= val:
            return
        self.mark[eng][name] = val
        s = self.sems[name]
        self.q[eng].append(lambda h, s=s, val=val: h.wait_ge(s, val))

    def dma(self, queue, out, in_, waits=(), sem=None, **kw):
        assert sem is not None

        def fn(h, out=out, in_=in_, kw=kw):
            return h.dma_start(out=out, in_=in_, **kw)

        if queue == "gpsimd":
            for t in waits:
                self.wait(queue, t)
            s = self.sem(sem)
            self.cnt[sem] += 16
            val = self.cnt[sem]

            def emit(h, fn=fn, s=s):
                fn(h).then_inc(s, 16)

            self.q[queue].append(emit)
            return (sem, val)
        return self.op(queue, fn, waits=waits, sem=sem, inc=16)

    def stream_tok(self, sem):
        self.sem(sem)
        return (sem, self.cnt[sem])


class Psum:

    def __init__(self, nc, es):
        self.t = {i: es.enter_context(nc.psum_tensor(f"pb{i}", [128, 512], F32))
                  for i in range(6)}
        self.t["bf"] = es.enter_context(nc.psum_tensor("pbf", [128, 1024], BF16))
        self.t["bf2"] = es.enter_context(nc.psum_tensor("pbf2", [128, 1024], BF16))
        self.free = {k: [] for k in self.t}

    def get(self, i):
        w = self.free[i]
        self.free[i] = []
        return self.t[i], w

    def consumed(self, i, tok):
        if tok is not None:
            self.free[i].append(tok)


def host_constants():
    import ml_dtypes
    bf = ml_dtypes.bfloat16
    c = {}
    c["c_ident_bf"] = np.eye(128, dtype=bf)
    c["c_ident_f32"] = np.eye(128, dtype=np.float32)
    c["c_ones_row_f32"] = np.ones((1, 128), dtype=np.float32)
    c["c_ones_col_bf"] = np.ones((128, 1), dtype=bf)
    pool = np.tile(np.asarray(CONST_VALS, dtype=np.float32)[None, :], (128, 1))
    c["c_pool"] = pool
    c["c_pool_bf"] = pool.astype(bf)
    dm = np.zeros((128, 640), dtype=np.float32)
    qq = np.arange(128)[:, None]
    kk = np.arange(128)[None, :]
    dm[:, 512:640] = np.where(kk > qq, MASK_NEG, 0.0).astype(np.float32)
    c["c_dmask"] = dm
    oh = np.zeros((NE, NE, 128), dtype=np.float32)
    for e in range(NE):
        oh[e, e, :] = 1.0
    c["c_onehot"] = np.ascontiguousarray(oh.transpose(1, 0, 2).reshape(NE, NE * 128))
    return c


class Builder:
    def __init__(self, T=4096, taps=(), n_sub=4):
        self.T = T
        self.TT = T // 512
        self.taps = {}
        self.tap_req = list(taps)
        self.n_sub = n_sub

    def di(self, name, shape, dt=F32):
        ap = self.nc.dram_tensor(name, list(shape), dt, kind="ExternalInput").ap()
        self.inp[name] = ap
        return ap

    def scratch(self, name, shape, dt=F32):
        return self.nc.dram_tensor(name, list(shape), dt, kind="Internal").ap()

    def sb(self, name, p, f, dt=F32):
        return self.es.enter_context(self.nc.sbuf_tensor(name, [p, f], dt))

    def declare(self):
        T = self.T
        self.hs = self.di("L0_attn_a_hs", (T, D))
        for nm in ("ln1_g", "ln1_b", "ln2_g", "ln2_b"):
            self.di(f"L0_attn_a_{nm}", (D,))
        for fam, (o, i) in (("ca", (I3, D)), ("cp", (D, D)), ("cf", (FI, D)),
                            ("cm", (D, FI))):
            self.di(f"L0_attn_a_{fam}_sub_w", (2, o, i))
            self.di(f"L0_attn_a_{fam}_Wl_w", (1, 2))
            self.di(f"L0_attn_a_{fam}_W_w", (o, i))
            self.di(f"L0_attn_a_{fam}_B_w", (o, i))
            self.di(f"L0_attn_a_{fam}_sub_b", (2, o))
            self.di(f"L0_attn_a_{fam}_Wl_b", (1, 2))
            self.di(f"L0_attn_a_{fam}_W_b", (o,))
            self.di(f"L0_attn_a_{fam}_B_b", (o,))
        self.di("L0_attn_b_W_qkv", (D, I3))
        self.di("L0_attn_b_b_qkv", (I3,))
        self.di("L0_attn_b_W_out", (D, D))
        self.di("L0_attn_b_b_out", (D,))
        self.di("L0_ffn_Wg", (D, NE))
        self.di("L0_ffn_Ws1", (D, FF))
        self.di("L0_ffn_Ws3", (D, FF))
        self.di("L0_ffn_Ws2", (FF, D))
        self.di("L0_ffn_We1", (NE, D, FF))
        self.di("L0_ffn_We3", (NE, D, FF))
        self.di("L0_ffn_We2", (NE, FF, D))
        self.di("c_ident_bf", (128, 128), BF16)
        self.di("c_ident_f32", (128, 128))
        self.di("c_ones_row_f32", (1, 128))
        self.di("c_ones_col_bf", (128, 1), BF16)
        self.di("c_pool", (128, len(CONST_VALS)))
        self.di("c_pool_bf", (128, len(CONST_VALS)), BF16)
        self.di("c_dmask", (128, 640))
        self.di("c_onehot", (NE, NE * 128))

        self.out = self.nc.dram_tensor("out", [T, D], F32, kind="ExternalOutput").ap()
        for (name, shape, dt) in self.tap_req:
            self.taps[name] = self.nc.dram_tensor(
                name, list(shape), dt, kind="ExternalOutput").ap()

        self.x_hbm = self.scratch("x_hbm", (D, T))
        self.q_hbm = self.scratch("q_hbm", (D, T), BF16)
        self.k_hbm = self.scratch("k_hbm", (D, T), BF16)
        self.v_hbm = self.scratch("v_hbm", (T, D), BF16)
        self.o_hbm = self.scratch("o_hbm", (D, T), BF16)
        self.xn_hbm = self.scratch("xn_hbm", (D, T), BF16)
        self.r_hbm = self.scratch("r_hbm", (D, T))
        self.x2_hbm = self.scratch("x2_hbm", (D, T))
        self.acc_hbm = self.scratch("acc_hbm", (D, T))
        self.g_hbm = self.scratch("g_hbm", (FI, T), BF16)

    def alloc_persistent(self):
        em = self.em
        self.ident_bf = self.sb("ident_bf", 128, 128, BF16)
        self.ident_f = self.sb("ident_f", 128, 128)
        self.ones_row = self.sb("ones_row", 1, 128)
        self.ones_row_bf = self.sb("ones_row_bf", 1, 128, BF16)
        self.ones_col_bf = self.sb("ones_col_bf", 128, 1, BF16)
        self.sum_row = self.sb("sum_row", 1, self.T)
        self.ssq_row = self.sb("ssq_row", 1, self.T)
        self.rA = self.sb("rA", 1, 512)
        self.rB = self.sb("rB", 1, 512)
        self.rt1 = self.sb("rt1", 1, 512)
        self.rt2 = self.sb("rt2", 1, 512)
        self.rt3 = self.sb("rt3", 1, 512)

        em.dma("sync", self.ident_bf[:], self.inp["c_ident_bf"], sem="d_const")
        em.dma("sync", self.ident_f[:], self.inp["c_ident_f32"], sem="d_const")
        em.dma("sync", self.ones_row[:], self.inp["c_ones_row_f32"], sem="d_const")
        em.dma("sync", self.ones_col_bf[:], self.inp["c_ones_col_bf"], sem="d_const")
        self.cpool = self.sb("cpool", 128, len(CONST_VALS))
        self.cpool_bf = self.sb("cpool_bf", 128, len(CONST_VALS), BF16)
        em.dma("sync", self.cpool[:], self.inp["c_pool"], sem="d_const")
        em.dma("sync", self.cpool_bf[:], self.inp["c_pool_bf"], sem="d_const")
        for j, val in enumerate(CONST_VALS):
            self.nc.const_aps.register(F32, val, self.cpool[:, j:j + 1])
            self.nc.const_aps.register(BF16, val, self.cpool_bf[:, j:j + 1])
        self.tok_const = em.stream_tok("d_const")
        for e in CHAINED:
            em.last[e] = self.tok_const
        self.tok_onesbf = self.v(
            lambda h: h.tensor_copy(self.ones_row_bf[:], self.ones_row[:]),
            waits=[self.tok_const])

    def v(self, fn, waits=()):
        return self.em.op("vector", fn, waits=waits)

    def s(self, fn, waits=()):
        return self.em.op("scalar", fn, waits=waits)

    def t(self, fn, waits=()):
        return self.em.op("tensor", fn, waits=waits, sem="c_t", inc=1)

    def g(self, fn, waits=()):
        return self.em.op("gpsimd", fn, waits=waits)

    def stats_tile(self, src_fn, W, col0, waits, banks=(2, 0)):
        prev = getattr(self, "_stats_prev_mm", [])
        rows = []
        all_src_toks = []
        for j in range(2):
            cast_toks = []
            for ic in range(8):
                hi = self.stb[:, ic * W:(ic + 1) * W]
                lo = self.stq[:, ic * W:(ic + 1) * W]
                if j == 0:
                    t1 = self.s(lambda h, s=src_fn(ic), hi=hi:
                                h.activation(hi, s, AF.Copy),
                                waits=list(waits) + prev)
                    t2 = self.v(lambda h, s=src_fn(ic), hi=hi, lo=lo:
                                h.scalar_tensor_tensor(lo, hi, -1.0, s,
                                                       ALU.mult, ALU.add),
                                waits=[t1] + prev)
                else:
                    t0 = self.s(lambda h, s=src_fn(ic):
                                h.activation(self.stf[:, 0:W], s, AF.Square),
                                waits=list(waits) + prev +
                                getattr(self, "_stf_prev", []))
                    t1 = self.s(lambda h, hi=hi:
                                h.activation(hi, self.stf[:, 0:W], AF.Copy))
                    t2 = self.v(lambda h, hi=hi, lo=lo:
                                h.scalar_tensor_tensor(lo, hi, -1.0,
                                                       self.stf[:, 0:W],
                                                       ALU.mult, ALU.add),
                                waits=[t1])
                    self._stf_prev = [t2]
                cast_toks += [t1, t2]
            all_src_toks += cast_toks
            pb, bw = self.ps.get(banks[j])
            mm = None
            for k in range(2):
                buf = self.stb if k == 0 else self.stq
                for ic in range(8):
                    mm = self.t(
                        lambda h, r=buf[:, ic * W:(ic + 1) * W], o=pb[0:1, 0:W],
                               st=(k == 0 and ic == 0), sp=(k == 1 and ic == 7):
                        h.matmul(o, self.ones_col_bf[:, 0:1], r, start=st, stop=sp),
                        waits=cast_toks + bw + [self.tok_onesbf])
            dst = (self.sum_row if j == 0 else self.ssq_row)[0:1, col0:col0 + W]
            ev = self.v(lambda h, s=pb[0:1, 0:W], d=dst: h.tensor_copy(d, s),
                        waits=[mm])
            self.ps.consumed(banks[j], ev)
            rows.append(ev)
            prev = [mm]
            self._stats_prev_mm = [mm]
        return rows, all_src_toks

    def row_math_outer(self, col0, W, waits):
        sr = self.sum_row[0:1, col0:col0 + W]
        qr = self.ssq_row[0:1, col0:col0 + W]
        m = self.rt1[0:1, 0:W]
        vv = self.rt2[0:1, 0:W]
        t1 = self.v(lambda h: h.tensor_scalar(m, sr, 1.0 / D, None, ALU.mult),
                    waits=waits)
        t2 = self.v(lambda h: h.tensor_mul(self.rt3[0:1, 0:W], m, m))
        t3 = self.v(lambda h: h.scalar_tensor_tensor(
            vv, qr, 1.0 / D, self.rt3[0:1, 0:W], ALU.mult, ALU.subtract), waits=waits)
        t4 = self.s(lambda h: h.activation(self.rt3[0:1, 0:W], vv, AF.Ln, bias=EPS),
                    waits=[t3])
        t5 = self.s(lambda h: h.activation(self.rA[0:1, 0:W], self.rt3[0:1, 0:W],
                                           AF.Exp, scale=-0.5))
        t6 = self.v(lambda h: h.scalar_tensor_tensor(
            self.rB[0:1, 0:W], m, -1.0, self.rA[0:1, 0:W], ALU.mult, ALU.mult),
            waits=[t5])
        return [t5, t6]

    def row_math_fused_inner(self, W, waits):
        vv = self.rt2[0:1, 0:W]
        m = self.rt1[0:1, 0:W]
        t1 = self.v(lambda h: h.tensor_mul(self.rt3[0:1, 0:W], self.rA[0:1, 0:W],
                                           self.rA[0:1, 0:W]), waits=waits)
        t2 = self.v(lambda h: h.tensor_mul(self.rt3[0:1, 0:W], self.rt3[0:1, 0:W], vv))
        t3 = self.s(lambda h: h.activation(self.rt3[0:1, 0:W], self.rt3[0:1, 0:W],
                                           AF.Ln, bias=EPS), waits=[t2])
        t4 = self.s(lambda h: h.activation(self.rt3[0:1, 0:W], self.rt3[0:1, 0:W],
                                           AF.Exp, scale=-0.5))
        t5 = self.v(lambda h: h.tensor_mul(self.rA2[0:1, 0:W], self.rA[0:1, 0:W],
                                           self.rt3[0:1, 0:W]), waits=[t4])
        t6 = self.v(lambda h: h.scalar_tensor_tensor(
            self.rB2[0:1, 0:W], m, -1.0, self.rA2[0:1, 0:W], ALU.mult, ALU.mult))
        return [t5, t6]

    def rep_rows(self, arow_ap, brow_ap, W, waits, banks=(4, 5)):
        pa, wa = self.ps.get(banks[0])
        pb, wb = self.ps.get(banks[1])
        ta = self.t(lambda h: h.matmul(pa[:, 0:W], self.ones_row[0:1, 0:128],
                                       arow_ap, start=True, stop=True),
                    waits=list(waits) + wa + [self.tok_const])
        tb = self.t(lambda h: h.matmul(pb[:, 0:W], self.ones_row[0:1, 0:128],
                                       brow_ap, start=True, stop=True),
                    waits=list(waits) + wb + [self.tok_const])
        return pa, pb, [ta, tb]

    def normalize_tile(self, x_fn, out_fn, W, Aps, Bps, rep_toks, banks,
                       waits=(), gcol=None, bcol=None, scratch=None):
        toks = []
        psum_cons = []
        for ic in range(8):
            if gcol is None:
                t1 = self.v(lambda h, x=x_fn(ic): h.scalar_tensor_tensor(
                    scratch[:, 0:W], x, 1.0, Aps[:, 0:W], ALU.mult, ALU.mult),
                    waits=list(waits) + rep_toks)
                t2 = self.v(lambda h, o=out_fn(ic): h.scalar_tensor_tensor(
                    o, scratch[:, 0:W], 1.0, Bps[:, 0:W], ALU.mult, ALU.add))
                psum_cons += [t1, t2]
            else:
                t0 = self.v(lambda h, ic=ic: h.tensor_scalar(
                    self.nrmA[:, 0:W], Aps[:, 0:W], gcol[:, ic:ic + 1], None,
                    ALU.mult), waits=list(waits) + rep_toks)
                t0b = self.v(lambda h, ic=ic: h.tensor_scalar(
                    self.nrmB[:, 0:W], Bps[:, 0:W], gcol[:, ic:ic + 1],
                    bcol[:, ic:ic + 1], ALU.mult, ALU.add))
                t1 = self.v(lambda h, x=x_fn(ic): h.scalar_tensor_tensor(
                    scratch[:, 0:W], x, 1.0, self.nrmA[:, 0:W], ALU.mult, ALU.mult))
                t2 = self.v(lambda h, o=out_fn(ic): h.scalar_tensor_tensor(
                    o, scratch[:, 0:W], 1.0, self.nrmB[:, 0:W], ALU.mult, ALU.add))
                psum_cons += [t0, t0b]
            toks.append(t2)
        for b in banks:
            for c in psum_cons:
                self.ps.consumed(b, c)
        return toks

    def _set_fence(self):
        em = self.em
        self.fence = [(s, em.cnt[s]) for s in list(em.cnt.keys())]
        for attr in ("_stats_prev_mm", "_stf_prev", "_zprev", "_ocp_prev",
                     "_g_prev", "_m_prev", "_xn_prev", "_bq_m_prev",
                     "_wem_prev", "_ffn_uprev", "_ffn_vprev", "_gate_prev"):
            setattr(self, attr, list(self.fence))
        self._pfree = None

    def push_phase(self):
        self.phase_stack.append(ExitStack())

    def flush_block(self):
        em = self.em
        if not any(em.q[e] for e in ENGINES):
            return
        with self.nc.Block() as block:
            for eng in ENGINES:
                if em.q[eng]:
                    getattr(block, eng)(self._replayer(eng))
        for e in ENGINES:
            em.q[e] = []

    def pop_phase(self):
        self.flush_block()
        self.phase_stack.pop().close()
        self._set_fence()

    def new_phase(self):
        if self.phase_stack:
            self.pop_phase()
        self.push_phase()

    @property
    def phase_es(self):
        return self.phase_stack[-1] if self.phase_stack else None

    def sbp(self, name, p, f, dt=F32):
        return self.phase_stack[-1].enter_context(
            self.nc.sbuf_tensor(name, [p, f], dt))

    def tap_hbm(self, name, src_ap, waits):
        if name in self.taps:
            self.em.dma("sync", self.taps[name], src_ap, waits=waits, sem="d_tap")

    def load_scalar_pair(self, wl_name, cols, waits):
        em = self.em
        self._lsp = getattr(self, "_lsp", 0) + 1
        row = self.sbp(f"wlr{self._lsp}", 1, 2)
        t0 = em.dma("scalar", row[:], self.inp[wl_name], waits=waits, sem="d_small")
        pb, bw = self.ps.get(2)
        mms = []
        for j in range(2):
            mms.append(self.t(lambda h, j=j: h.matmul(
                pb[:, j * 128:j * 128 + 1], self.ones_row[0:1, 0:128],
                row[0:1, j:j + 1], start=True, stop=True),
                waits=[t0, self.tok_const] + bw))
        e1 = self.s(lambda h: h.activation(cols[:, 0:1], pb[:, 0:1], AF.Copy),
                    waits=mms)
        e2 = self.s(lambda h: h.activation(cols[:, 1:2], pb[:, 128:129], AF.Copy))
        self.ps.consumed(2, e1)
        self.ps.consumed(2, e2)
        return [e1, e2]

    def gen_bias_cols(self, fam, o_dim, dst_cols, waits):
        em = self.em
        nm = f"L0_attn_a_{fam}"
        n_oc = o_dim // 128
        wl = self.sbp(f"bwl_{fam}", 128, 2)
        ctoks = self.load_scalar_pair(f"{nm}_Wl_b", wl, waits)
        bufs = [self.sbp(f"bg{k}_{fam}", 128, n_oc) for k in range(4)]
        ldt = []
        for k, src in enumerate((self.inp[f"{nm}_sub_b"][0, :],
                                 self.inp[f"{nm}_sub_b"][1, :],
                                 self.inp[f"{nm}_W_b"],
                                 self.inp[f"{nm}_B_b"])):
            for oc in range(n_oc):
                ldt.append(em.dma("scalar", bufs[k][:, oc:oc + 1],
                                  src[oc * 128:(oc + 1) * 128],
                                  waits=waits, sem="d_small"))
        ls = em.stream_tok("d_small")
        t1 = self.v(lambda h: h.tensor_scalar(dst_cols[:, 0:n_oc], bufs[0][:, 0:n_oc],
                                              wl[:, 0:1], None, ALU.mult),
                    waits=[ls] + ctoks)
        t2 = self.v(lambda h: h.scalar_tensor_tensor(
            dst_cols[:, 0:n_oc], bufs[1][:, 0:n_oc], wl[:, 1:2],
            dst_cols[:, 0:n_oc], ALU.mult, ALU.add))
        t3 = self.s(lambda h: h.activation(dst_cols[:, 0:n_oc], dst_cols[:, 0:n_oc],
                                           AF.Tanh), waits=[t2])
        t4 = self.v(lambda h: h.tensor_mul(dst_cols[:, 0:n_oc], dst_cols[:, 0:n_oc],
                                           bufs[2][:, 0:n_oc]), waits=[t3])
        t5 = self.v(lambda h: h.tensor_add(dst_cols[:, 0:n_oc], dst_cols[:, 0:n_oc],
                                           bufs[3][:, 0:n_oc]))
        return [t5]

    def gen_bias_row(self, fam, lo, hi, dst_row, waits):
        em = self.em
        nm = f"L0_attn_a_{fam}"
        W = hi - lo
        wl = self.sbp(f"rwl_{fam}", 128, 2)
        ctoks = self.load_scalar_pair(f"{nm}_Wl_b", wl, waits)
        bufs = [self.sbp(f"rg{k}_{fam}", 1, W) for k in range(4)]
        ldt = []
        for k, src in enumerate((self.inp[f"{nm}_sub_b"][0, lo:hi],
                                 self.inp[f"{nm}_sub_b"][1, lo:hi],
                                 self.inp[f"{nm}_W_b"][lo:hi],
                                 self.inp[f"{nm}_B_b"][lo:hi])):
            ldt.append(em.dma("scalar", bufs[k][0:1, :], src, waits=waits,
                              sem="d_small"))
        ls = em.stream_tok("d_small")
        t1 = self.v(lambda h: h.tensor_scalar(dst_row[0:1, 0:W], bufs[0][0:1, :],
                                              wl[0:1, 0:1], None, ALU.mult),
                    waits=[ls] + ctoks)
        t2 = self.v(lambda h: h.scalar_tensor_tensor(
            dst_row[0:1, 0:W], bufs[1][0:1, :], wl[0:1, 1:2], dst_row[0:1, 0:W],
            ALU.mult, ALU.add))
        t3 = self.s(lambda h: h.activation(dst_row[0:1, 0:W], dst_row[0:1, 0:W],
                                           AF.Tanh), waits=[t2])
        t4 = self.v(lambda h: h.tensor_mul(dst_row[0:1, 0:W], dst_row[0:1, 0:W],
                                           bufs[2][0:1, :]), waits=[t3])
        t5 = self.v(lambda h: h.tensor_add(dst_row[0:1, 0:W], dst_row[0:1, 0:W],
                                           bufs[3][0:1, :]))
        return [t5]

    def gen_weight_fam(self, fam, o_dim, i_dim, dst_sb, waits=()):
        em = self.em
        nm = f"L0_attn_a_{fam}"
        wl = self.sbp(f"gwl_{fam}", 128, 2)
        ctoks = self.load_scalar_pair(f"{nm}_Wl_w", wl, waits)
        GW = 512
        gw = [[self.sbp(f"gw{j}{k}_{fam}", 128, GW) for k in range(4)]
              for j in range(2)]
        gtmp = [self.sbp(f"gt{j}_{fam}", 128, GW) for j in range(2)]
        gout = [self.sbp(f"go{j}_{fam}", 128, GW, BF16) for j in range(2)]
        gfree = {0: list(waits), 1: list(waits)}
        sub = self.inp[f"{nm}_sub_w"]
        Wt = self.inp[f"{nm}_W_w"]
        Bt = self.inp[f"{nm}_B_w"]
        n_ot = o_dim // 128
        n_ig = i_dim // GW
        done = []
        pc = 0
        for ot in range(n_ot):
            for ig in range(n_ig):
                j = pc % 2
                pc += 1
                b0, b1, bW, bB = gw[j]
                osl = slice(ot * 128, (ot + 1) * 128)
                isl = slice(ig * GW, (ig + 1) * GW)
                em.dma("gpsimd", b0[:, :], sub[0, osl, isl], waits=gfree[j],
                       sem=f"d_gen{j}")
                em.dma("gpsimd", b1[:, :], sub[1, osl, isl], sem=f"d_gen{j}")
                em.dma("gpsimd", bW[:, :], Wt[osl, isl], sem=f"d_gen{j}")
                em.dma("gpsimd", bB[:, :], Bt[osl, isl], sem=f"d_gen{j}")
                ldt = em.stream_tok(f"d_gen{j}")
                t1 = self.v(lambda h, b0=b0, j=j: h.tensor_scalar(
                    gtmp[j][:, :], b0[:, :], wl[:, 0:1], None, ALU.mult),
                    waits=[ldt] + ctoks)
                t2 = self.v(lambda h, b1=b1, j=j: h.scalar_tensor_tensor(
                    gtmp[j][:, :], b1[:, :], wl[:, 1:2], gtmp[j][:, :],
                    ALU.mult, ALU.add))
                t3 = self.s(lambda h, j=j: h.activation(gtmp[j][:, :], gtmp[j][:, :],
                                                        AF.Tanh), waits=[t2])
                t4 = self.v(lambda h, bW=bW, j=j: h.tensor_mul(
                    gtmp[j][:, :], gtmp[j][:, :], bW[:, :]), waits=[t3])
                t5 = self.v(lambda h, bB=bB, j=j: h.tensor_add(
                    gout[j][:, :], gtmp[j][:, :], bB[:, :]), waits=gfree[j])
                gfree[j] = [t5]
                for icl in range(GW // 128):
                    ic = ig * (GW // 128) + icl
                    pbf, bw = self.ps.get("bf")
                    ttr = self.t(lambda h, s=gout[j][:, icl * 128:(icl + 1) * 128],
                                        p=pbf[:, 0:128]:
                                 h.transpose(p, s, self.ident_bf[:]),
                                 waits=[t5, self.tok_const] + bw)
                    dst = dst_sb[:, ic * o_dim + ot * 128: ic * o_dim + ot * 128 + 128]
                    tev = self.v(lambda h, p=pbf[:, 0:128], d=dst: h.tensor_copy(d, p),
                                 waits=[ttr])
                    self.ps.consumed("bf", tev)
                    gfree[j].append(tev)
                    done.append(tev)
        return done

    def load_cols(self, src_1d, n_oc, dst_cols, waits):
        toks = []
        for oc in range(n_oc):
            toks.append(self.em.dma("scalar", dst_cols[:, oc:oc + 1],
                                    src_1d[oc * 128:(oc + 1) * 128],
                                    waits=waits, sem="d_small"))
        return toks

    def attn_alloc(self, G, sfx):
        T = self.T
        em = self.em
        self.atG = G
        self.at_dmask = self.sbp(f"at_dmask{sfx}", 128, 640)
        em.dma("sync", self.at_dmask[:], self.inp["c_dmask"],
               waits=self.fence, sem="d_atc")
        self.at_ctok = em.stream_tok("d_atc")
        self.at_S = [self.sbp(f"at_S{i}{sfx}", 128, T)
                     for i in range(getattr(self, "at_sbufs", 2))]
        self.at_P = [self.sbp(f"at_P{i}{sfx}", 128, T, BF16) for i in range(G)]
        self.at_PT = [self.sbp(f"at_PT{i}{sfx}", 128, G * 128, BF16)
                      for i in range(2)]
        self.at_mrow = self.sbp(f"at_mrow{sfx}", 128, 2)
        self.at_negm = self.sbp(f"at_negm{sfx}", 128, G)
        self.at_zparts = self.sbp(f"at_zparts{sfx}", 128, G * 8)
        self.at_zscr = self.sbp(f"at_zscr{sfx}", 128, 8)
        self.at_zsum = self.sbp(f"at_zsum{sfx}", 128, 1)
        self.at_zc = self.sbp(f"at_zc{sfx}", 128, 2 * G)
        self.at_zrow = self.sbp(f"at_zrow{sfx}", 1, 2 * G * 128)
        self.at_zrep = self.sbp(f"at_zrep{sfx}", 64, G * 128)
        self.at_sfree = {(p, w): list(self.fence)
                         for p in range(len(self.at_S)) for w in range(T // 512)}
        _, bfw0 = self.ps.get("bf")
        _, bfw1 = self.ps.get("bf2")
        self.at_bfw0 = bfw0 + bfw1 + list(self.fence)
        self.at_bfseq = []
        self.at_pfree = {i: list(self.fence) for i in range(G)}
        self.at_ptfree = {i: list(self.fence) for i in range(2)}
        self.at_negmfree = {i: list(self.fence) for i in range(G)}
        self.at_zfree = list(self.fence)
        self.at_zcfree = {0: list(self.fence), 1: list(self.fence)}
        self.at_zrowfree = {0: list(self.fence), 1: list(self.fence)}
        self.at_zsumfree = list(self.fence)

    def attn_core2(self, q_all, k_all, v_tok, scale, dst_fn, done_fn,
                   tq, tk, tv):
        em = self.em
        T = self.T
        G = self.atG
        NT = T // 128
        ng_groups = NT // G
        qk_cons = []
        v_cons = []

        def emit_S(g, tiles=None, ttrs=None):
            ttrs = {} if ttrs is None else ttrs
            for tl in (range(G) if tiles is None else tiles):
                t = g * G + tl
                sp = t % len(self.at_S)
                S = self.at_S[sp]
                kspan = (t + 1) * 128
                nwin = (kspan + 511) // 512
                for w in range(nwin):
                    N = min(512, kspan - w * 512)
                    bank = w % 2
                    spb, sbw = self.ps.get(bank)
                    mm = self.t(lambda h, t=t, w=w, N=N, spb=spb: h.matmul(
                        spb[:, 0:N], q_all[0:64, t * 128:(t + 1) * 128],
                        k_all[0:64, w * 512:w * 512 + N], start=True, stop=True),
                        waits=[tq, tk] + sbw)
                    qk_cons.append(mm)
                    last = (w == nwin - 1)
                    in1 = (self.at_dmask[:, 640 - N:640] if last
                           else self.at_dmask[:, 0:N])
                    ttr = self.v(lambda h, w=w, N=N, in1=in1, spb=spb, S=S:
                                 h.tensor_tensor(
                                     S[:, w * 512:w * 512 + N],
                                     spb[:, 0:N], in1, ALU.add),
                                 waits=[mm, self.at_ctok] + self.at_sfree[(sp, w)])
                    self.at_sfree[(sp, w)] = []
                    self.ps.consumed(bank, ttr)
                    if w == 0:
                        self.v(lambda h, w=w, N=N, S=S: h.reduce_max(
                            self.at_mrow[:, 0:1],
                            S[:, w * 512:w * 512 + N], axis=AX.X))
                    else:
                        self.v(lambda h, w=w, N=N, S=S: h.reduce_max(
                            self.at_mrow[:, 1:2],
                            S[:, w * 512:w * 512 + N], axis=AX.X))
                        self.v(lambda h: h.tensor_max(
                            self.at_mrow[:, 0:1], self.at_mrow[:, 0:1],
                            self.at_mrow[:, 1:2]))
                    ttrs[(tl, w)] = ttr
                ngt = self.v(lambda h, tl=tl: h.tensor_scalar(
                    self.at_negm[:, tl:tl + 1], self.at_mrow[:, 0:1],
                    -scale, None, ALU.mult), waits=self.at_negmfree[tl])
                self.at_negmfree[tl] = []
                ttrs[("ng", tl)] = ngt
            return ttrs

        def emit_E(g, ttrs, tiles=None, exp_toks=None):
            par = g % 2
            exp_toks = {} if exp_toks is None else exp_toks
            for tl in (range(G) if tiles is None else tiles):
                t = g * G + tl
                sp = t % len(self.at_S)
                S = self.at_S[sp]
                kspan = (t + 1) * 128
                nwin = (kspan + 511) // 512
                for w in range(nwin):
                    N = min(512, kspan - w * 512)
                    pe = self.s(lambda h, tl=tl, w=w, N=N, S=S: h.activation(
                        self.at_P[tl][:, w * 512:w * 512 + N],
                        S[:, w * 512:w * 512 + N],
                        AF.Exp, bias=self.at_negm[:, tl:tl + 1], scale=scale,
                        accum_out=self.at_zparts[:, tl * 8 + w:tl * 8 + w + 1]),
                        waits=[ttrs[(tl, w)], ttrs[("ng", tl)]]
                              + (self.at_pfree[tl] if w == 0 else []))
                    if w == 0:
                        self.at_pfree[tl] = []
                    self.at_sfree[(sp, w)].append(pe)
                    self.at_negmfree[tl].append(pe)
                    exp_toks[(tl, w)] = pe
                zs = self.s(lambda h, tl=tl, nw=nwin: h.activation(
                    self.at_zscr[:, 0:nw], self.at_zparts[:, tl * 8:tl * 8 + nw],
                    AF.Copy, accum_out=self.at_zsum[:, 0:1]),
                    waits=[exp_toks[(tl, w2)] for w2 in range(nwin)]
                          + self.at_zsumfree)
                rc = self.v(lambda h, tl=tl, par=par: h.reciprocal(
                    self.at_zc[:, par * G + tl:par * G + tl + 1],
                    self.at_zsum[:, 0:1]),
                    waits=[zs] + self.at_zcfree[par])
                self.at_zsumfree = [rc]
                self.at_zcfree[par] = []
                pz, pzw = self.ps.get(3)
                ztr = self.t(lambda h, tl=tl, par=par, pz=pz: h.transpose(
                    pz[0:1, 0:128],
                    self.at_zc[:, par * G + tl:par * G + tl + 1],
                    self.ident_f[:]),
                    waits=[rc, self.tok_const] + pzw)
                self.at_zcfree[par].append(ztr)
                ze = self.s(lambda h, tl=tl, par=par, pz=pz: h.activation(
                    self.at_zrow[0:1, (par * G + tl) * 128:(par * G + tl + 1) * 128],
                    pz[0:1, 0:128], AF.Copy),
                    waits=[ztr] + self.at_zrowfree[par])
                self.at_zrowfree[par] = []
                self.ps.consumed(3, ze)
                exp_toks[("z", tl)] = ze
            return exp_toks

        def emit_BC(g, exp_toks):
            par = g % 2
            nkc = g * G + G
            po, pow_ = self.ps.get(2)
            mmo = None
            for kc in range(nkc):
                j = kc % 2
                pt = self.at_PT[j]
                pt_toks = []
                for tl in range(G):
                    t = g * G + tl
                    if kc <= t:
                        seq = len(self.at_bfseq)
                        bname = "bf" if seq % 2 == 0 else "bf2"
                        pbf = self.ps.t[bname]
                        stag = (self.at_bfw0 if seq < 2
                                else [self.at_bfseq[seq - 2]])
                        ttr2 = self.t(lambda h, kc=kc, tl=tl, pbf=pbf:
                                      h.transpose(
                                          pbf[:, 0:128],
                                          self.at_P[tl][:, kc * 128:(kc + 1) * 128],
                                          self.ident_bf[:]),
                                      waits=[exp_toks[(tl, kc // 4)],
                                             self.tok_const] + stag)
                        self.at_pfree[tl].append(ttr2)
                        if seq % 2 == 0:
                            ev = self.s(lambda h, tl=tl, pt=pt, pbf=pbf:
                                        h.activation(pt[:, tl * 128:(tl + 1) * 128],
                                                     pbf[:, 0:128], AF.Copy),
                                        waits=[ttr2] + self.at_ptfree[j])
                        else:
                            ev = self.v(lambda h, tl=tl, pt=pt, pbf=pbf:
                                        h.tensor_copy(pt[:, tl * 128:(tl + 1) * 128],
                                                      pbf[:, 0:128]),
                                        waits=[ttr2] + self.at_ptfree[j])
                        self.at_bfseq.append(ev)
                    else:
                        ev = self.v(lambda h, tl=tl, pt=pt: h.memset(
                            pt[:, tl * 128:(tl + 1) * 128], 0.0),
                            waits=self.at_ptfree[j])
                    self.at_ptfree[j] = []
                    pt_toks.append(ev)
                mmo = self.t(lambda h, kc=kc, pt=pt, po=po,
                                    st=(kc == 0), sp=(kc == nkc - 1):
                             h.matmul(po[0:64, 0:G * 128],
                                      v_tok[:, kc * 64:(kc + 1) * 64],
                                      pt[:, 0:G * 128], start=st, stop=sp),
                             waits=pt_toks + [tv] + (pow_ if kc == 0 else []))
                v_cons.append(mmo)
                self.at_ptfree[j] = [mmo]
            pzr, pzrw = self.ps.get(4)
            zm = self.t(lambda h, par=par, pzr=pzr: h.matmul(
                pzr[0:64, 0:G * 128], self.ones_row[0:1, 0:64],
                self.at_zrow[0:1, par * G * 128:(par + 1) * G * 128],
                start=True, stop=True),
                waits=[exp_toks[("z", tl)] for tl in range(G)]
                      + [self.tok_const] + pzrw)
            self.at_zrowfree[par].append(zm)
            ze2 = self.s(lambda h, pzr=pzr: h.activation(
                self.at_zrep[:, 0:G * 128], pzr[0:64, 0:G * 128], AF.Copy),
                waits=[zm] + self.at_zfree)
            self.ps.consumed(4, ze2)
            dap, dwaits = dst_fn(g)
            oe = self.v(lambda h, po=po, dap=dap: h.tensor_mul(
                dap, po[0:64, 0:G * 128], self.at_zrep[:, 0:G * 128]),
                waits=[ze2, mmo] + list(dwaits))
            self.ps.consumed(2, oe)
            self.at_zfree = [oe]
            done_fn(g, oe)

        if len(self.at_S) >= G:
            ttrs = {0: emit_S(0)}
            exps = {0: emit_E(0, ttrs[0])}
            for g in range(ng_groups):
                if g + 1 < ng_groups:
                    ttrs[g + 1] = emit_S(g + 1)
                emit_BC(g, exps[g])
                if g + 1 < ng_groups:
                    exps[g + 1] = emit_E(g + 1, ttrs[g + 1])
        else:
            for g in range(ng_groups):
                ttrs = {}
                exps = {}
                for tl in range(G):
                    emit_S(g, tiles=[tl], ttrs=ttrs)
                    emit_E(g, ttrs, tiles=[tl], exp_toks=exps)
                emit_BC(g, exps)
        return qk_cons, v_cons

    def p_attn_a(self):
        em = self.em
        TT = self.TT
        T = self.T

        self.new_phase()
        wcaT = self.sbp("wcaT", 128, 8 * I3, BF16)
        self.nrmA = self.sbp("nrmA_a", 128, 512)
        self.nrmB = self.sbp("nrmB_a", 128, 512)
        self.rA2 = self.sbp("rA2", 1, 512)
        self.rB2 = self.sbp("rB2", 1, 512)
        bca_cols = self.sbp("bca_cols", 128, 16)
        bvrow = self.sbp("bvrow", 1, 1024)
        bvrep = self.sbp("bvrep", 128, 1024)
        ln1g = self.sbp("ln1g", 128, 8)
        ln1b = self.sbp("ln1b", 128, 8)
        tok_w = self.gen_weight_fam("ca", I3, D, wcaT, waits=self.fence)
        tok_bc = self.gen_bias_cols("ca", 2048, bca_cols, self.fence)
        tok_bv = self.gen_bias_row("ca", 2048, 3072, bvrow, self.fence)
        rep_toks = []
        for half in range(2):
            pb, bw = self.ps.get(2)
            tm = self.t(lambda h, half=half: h.matmul(
                pb[:, 0:512], self.ones_row[0:1, 0:128],
                bvrow[0:1, half * 512:(half + 1) * 512], start=True, stop=True),
                waits=tok_bv + bw + [self.tok_const])
            te = self.s(lambda h, half=half: h.activation(
                bvrep[:, half * 512:(half + 1) * 512], pb[:, 0:512], AF.Copy),
                waits=[tm])
            self.ps.consumed(2, te)
            rep_toks.append(te)
        tok_g = self.load_cols(self.inp["L0_attn_a_ln1_g"], 8, ln1g, self.fence)
        tok_b = self.load_cols(self.inp["L0_attn_a_ln1_b"], 8, ln1b, self.fence)
        ln1_tok = em.stream_tok("d_small")
        w_done = tok_w[-1]

        xa = [self.sbp("a1x0", 128, 8 * 512)]
        h1 = [self.sbp(f"a1h{i}", 128, 8 * 512, BF16) for i in range(2)]
        nsc = self.sbp("a1nsc", 128, 512)
        qst = [self.sbp(f"a1q{i}", 128, 512, BF16) for i in range(4)]
        vst = [self.sbp(f"a1v{i}", 128, 1024, BF16) for i in range(2)]
        xfree = {0: [], 1: []}
        hfree = {0: [], 1: []}
        qfree = {i: [] for i in range(4)}
        vfree = {0: [], 1: []}
        prev_rep = []
        for tt in range(TT):
            b = tt % 2
            x = xa[0]
            hb = h1[b]
            tok_x = em.dma("sync", x.rearrange("p (c t) -> p c t", c=8),
                           self.x_hbm.rearrange("(c p) t -> p c t", p=128)[
                               :, :, tt * 512:(tt + 1) * 512],
                           waits=xfree[0] + xfree[1] + self.tok_x_ready,
                           sem="d_a1x0")
            rtoks = self.row_math_outer(tt * 512, 512, [self.tok_stats_ready] + prev_rep)
            rtoks2 = self.row_math_fused_inner(512, rtoks)
            Aps, Bps, rep, = self.rep_rows(self.rA2[0:1, 0:512], self.rB2[0:1, 0:512],
                                           512, rtoks2, banks=(4, 5))
            prev_rep = rep
            htoks = self.normalize_tile(
                lambda ic, x=x: x[:, ic * 512:(ic + 1) * 512],
                lambda ic, hb=hb: hb[:, ic * 512:(ic + 1) * 512],
                512, Aps, Bps, rep, (4, 5),
                waits=[tok_x, ln1_tok] + hfree[b], gcol=ln1g, bcol=ln1b, scratch=nsc)
            xfree[b] = []
            hfree[b] = []
            for oc in range(16):
                bank = oc % 2
                pb, bw = self.ps.get(bank)
                mm = None
                for ic in range(8):
                    mm = self.t(lambda h, ic=ic, oc=oc, hb=hb, pb=pb,
                                       st=(ic == 0), sp=(ic == 7):
                                h.matmul(pb[:, 0:512],
                                         wcaT[:, ic * I3 + oc * 128: ic * I3 + oc * 128 + 128],
                                         hb[:, ic * 512:(ic + 1) * 512],
                                         start=st, stop=sp),
                                waits=htoks + bw + [w_done])
                q = qst[oc % 4]
                ev = self.s(lambda h, pb=pb, q=q, oc=oc: h.activation(
                    q[:], pb[:, 0:512], AF.Identity, bias=bca_cols[:, oc:oc + 1]),
                    waits=[mm] + qfree[oc % 4] + tok_bc)
                self.ps.consumed(bank, ev)
                dst = (self.q_hbm if oc < 8 else self.k_hbm)
                ocr = oc if oc < 8 else oc - 8
                td = em.dma("sync", dst[ocr * 128:(ocr + 1) * 128,
                                        tt * 512:(tt + 1) * 512], q[:],
                            waits=[ev], sem=f"d_qk{oc % 4}")
                qfree[oc % 4] = [td]
                hfree[b].append(mm)
            for ts4 in range(4):
                vb = ts4 % 2
                vs = vst[vb]
                evs = []
                for half in range(2):
                    bank = 2 + half
                    pb, bw = self.ps.get(bank)
                    mm = None
                    for ic in range(8):
                        mm = self.t(lambda h, ic=ic, ts4=ts4, half=half, hb=hb, pb=pb,
                                           st=(ic == 0), sp=(ic == 7):
                                    h.matmul(pb[:, 0:512],
                                             hb[:, ic * 512 + ts4 * 128: ic * 512 + ts4 * 128 + 128],
                                             wcaT[:, ic * I3 + 2048 + half * 512:
                                                  ic * I3 + 2048 + (half + 1) * 512],
                                             start=st, stop=sp),
                                    waits=htoks + bw + [w_done])
                    ev = self.v(lambda h, pb=pb, vs=vs, half=half: h.scalar_tensor_tensor(
                        vs[:, half * 512:(half + 1) * 512], pb[:, 0:512], 1.0,
                        bvrep[:, half * 512:(half + 1) * 512], ALU.mult, ALU.add),
                        waits=[mm] + vfree[vb] + rep_toks)
                    self.ps.consumed(bank, ev)
                    evs.append(ev)
                    hfree[b].append(mm)
                td = em.dma("sync", self.v_hbm[tt * 512 + ts4 * 128:
                                               tt * 512 + (ts4 + 1) * 128, :], vs[:],
                            waits=evs, sem=f"d_v{vb}")
                vfree[vb] = [td]
            xfree[b] = htoks
        tok_qk = [em.stream_tok(f"d_qk{i}") for i in range(4)]
        tok_v = [em.stream_tok(f"d_v{i}") for i in range(2)]
        self.tap_hbm("tap_q", self.q_hbm, tok_qk)
        self.tap_hbm("tap_k", self.k_hbm, tok_qk)
        self.tap_hbm("tap_v", self.v_hbm, tok_v)

        self.new_phase()
        G = 4
        self.at_sbufs = 4
        self.attn_alloc(G, "a")
        qh = [self.sbp(f"qh{i}", 64, T, BF16) for i in range(2)]
        kh = [self.sbp(f"kh{i}", 64, T, BF16) for i in range(2)]
        vth = [self.sbp(f"vth{i}", 128, (T // 128) * 64, BF16) for i in range(2)]
        ost = [self.sbp(f"ost{i}", 64, G * 128, BF16) for i in range(2)]
        qfree2 = {0: list(self.fence), 1: list(self.fence)}
        vfree2 = {0: list(self.fence), 1: list(self.fence)}
        ofree = {i: list(self.fence) for i in range(2)}
        for hd in range(H):
            b = hd % 2
            tq = em.dma("sync", qh[b][:], self.q_hbm[hd * 64:(hd + 1) * 64, :],
                        waits=qfree2[b] + tok_qk, sem=f"d_at_q{b}")
            tk = em.dma("sync", kh[b][:], self.k_hbm[hd * 64:(hd + 1) * 64, :],
                        waits=qfree2[b] + tok_qk, sem=f"d_at_q{b}")
            tv = em.dma("gpsimd",
                        vth[b].rearrange("p (k c) -> p k c", c=64),
                        self.v_hbm.rearrange("(k p) f -> p k f", p=128)[
                            :, :, hd * 64:(hd + 1) * 64],
                        waits=vfree2[b] + tok_v, sem=f"d_at_v{b}")
            qfree2[b] = []
            vfree2[b] = []

            def dst_fn(g, hd=hd):
                oi = g % 2
                w = ofree[oi]
                ofree[oi] = []
                return ost[oi][:], w

            def done_fn(g, oe, hd=hd):
                oi = g % 2
                td = em.dma("sync", self.o_hbm[hd * 64:(hd + 1) * 64,
                                               g * G * 128:(g + 1) * G * 128],
                            ost[oi][:], waits=[oe], sem=f"d_o{oi}")
                ofree[oi] = [td]

            qk_cons, v_cons = self.attn_core2(
                qh[b], kh[b], vth[b], SCALE_A, dst_fn, done_fn, tq, tk, tv)
            qfree2[b] += qk_cons
            vfree2[b] += v_cons
        tok_o = [em.stream_tok(f"d_o{i}") for i in range(2)]
        self.tap_hbm("tap_o", self.o_hbm, tok_o)

        self.new_phase()
        wcpT = self.sbp("wcpT", 128, 8 * D, BF16)
        bcp_cols = self.sbp("bcp_cols", 128, 8)
        tok_wcp = self.gen_weight_fam("cp", D, D, wcpT, waits=self.fence)
        tok_bcp = self.gen_bias_cols("cp", D, bcp_cols, self.fence)
        wcp_done = tok_wcp[-1]
        self.stb = self.sbp("c1stb", 128, 8 * 512, BF16)
        self.stq = self.sbp("c1stq", 128, 8 * 512, BF16)
        self.stf = self.sbp("c1stf", 128, 512)
        ot_b = [self.sbp(f"c1o{i}", 128, 8 * 512, BF16) for i in range(2)]
        x_b = [self.sbp(f"c1x{i}", 128, 8 * 512) for i in range(2)]
        ocp_b = self.sbp("c1ocp", 128, 512)
        r_b = [self.sbp(f"c1r{i}", 128, 8 * 512) for i in range(2)]
        nsc1 = self.sbp("c1nsc", 128, 512)
        ofree1 = {0: [], 1: []}
        xfree1 = {0: [], 1: []}
        rfree1 = {0: [], 1: []}
        prev_rep = []
        for tt in range(TT):
            b = tt % 2
            ob = ot_b[b]
            xb = x_b[b]
            rb = r_b[b]
            tok_o_ld = em.dma("sync", ob.rearrange("p (c t) -> p c t", c=8),
                              self.o_hbm.rearrange("(c p) t -> p c t", p=128)[
                                  :, :, tt * 512:(tt + 1) * 512],
                              waits=ofree1[b] + tok_o + self.fence, sem=f"d_c1o{b}")
            tok_x_ld = em.dma("sync", xb.rearrange("p (c t) -> p c t", c=8),
                              self.x_hbm.rearrange("(c p) t -> p c t", p=128)[
                                  :, :, tt * 512:(tt + 1) * 512],
                              waits=xfree1[b] + self.tok_x_ready + self.fence,
                              sem=f"d_c1x{b}")
            rtoks = self.row_math_outer(tt * 512, 512, [self.tok_stats_ready] + prev_rep)
            Aps, Bps, rep = self.rep_rows(self.rA[0:1, 0:512], self.rB[0:1, 0:512],
                                          512, rtoks, banks=(4, 5))
            prev_rep = rep
            rstat_toks = []
            for oc in range(8):
                bank = oc % 2
                pb, bw = self.ps.get(bank)
                mm = None
                for ic in range(8):
                    mm = self.t(lambda h, ic=ic, oc=oc, ob=ob, pb=pb,
                                       st=(ic == 0), sp=(ic == 7):
                                h.matmul(pb[:, 0:512],
                                         wcpT[:, ic * D + oc * 128: ic * D + oc * 128 + 128],
                                         ob[:, ic * 512:(ic + 1) * 512],
                                         start=st, stop=sp),
                                waits=[tok_o_ld, wcp_done] + bw)
                ev = self.s(lambda h, pb=pb, oc=oc: h.activation(
                    ocp_b[:], pb[:, 0:512], AF.Identity, bias=bcp_cols[:, oc:oc + 1]),
                    waits=[mm] + tok_bcp + getattr(self, "_ocp_prev", []))
                self.ps.consumed(bank, ev)
                u1 = self.v(lambda h, oc=oc, xb=xb: h.scalar_tensor_tensor(
                    nsc1[:, 0:512], xb[:, oc * 512:(oc + 1) * 512], 1.0,
                    Aps[:, 0:512], ALU.mult, ALU.mult),
                    waits=[tok_x_ld] + rep + rfree1[b])
                u2 = self.v(lambda h: h.scalar_tensor_tensor(
                    nsc1[:, 0:512], nsc1[:, 0:512], 1.0, Bps[:, 0:512],
                    ALU.mult, ALU.add))
                u3 = self.v(lambda h, oc=oc, rb=rb: h.tensor_add(
                    rb[:, oc * 512:(oc + 1) * 512], nsc1[:, 0:512], ocp_b[:]),
                    waits=[ev])
                self._ocp_prev = [u3]
                u4 = self.v(lambda h, oc=oc, xb=xb, rb=rb: h.tensor_add(
                    xb[:, oc * 512:(oc + 1) * 512], xb[:, oc * 512:(oc + 1) * 512],
                    rb[:, oc * 512:(oc + 1) * 512]))
                self.ps.consumed(4, u2)
                self.ps.consumed(5, u2)
                rstat_toks.append(u4)
            td_r = em.dma("sync", self.r_hbm.rearrange("(c p) t -> p c t", p=128)[
                :, :, tt * 512:(tt + 1) * 512],
                rb.rearrange("p (c t) -> p c t", c=8), waits=rstat_toks, sem=f"d_r{b}")
            td_x2 = em.dma("sync", self.x2_hbm.rearrange("(c p) t -> p c t", p=128)[
                :, :, tt * 512:(tt + 1) * 512],
                xb.rearrange("p (c t) -> p c t", c=8), waits=rstat_toks,
                sem=f"d_x2{b}")
            rows, casts = self.stats_tile(
                lambda ic, rb=rb: rb[:, ic * 512:(ic + 1) * 512], 512, tt * 512,
                rstat_toks)
            ofree1[b] = rstat_toks
            xfree1[b] = rstat_toks + [td_x2]
            rfree1[b] = casts + [td_r]
        tok_r = [em.stream_tok("d_r0"), em.stream_tok("d_r1")]
        tok_x2 = [em.stream_tok("d_x20"), em.stream_tok("d_x21")]
        self.tap_hbm("tap_r", self.r_hbm, tok_r)

        self.new_phase()
        wcfT = self.sbp("wcfT", 128, 8 * FI, BF16)
        self.nrmA = self.sbp("nrmA_5", 128, 512)
        self.nrmB = self.sbp("nrmB_5", 128, 512)
        bcf_cols = self.sbp("bcf_cols", 128, 32)
        ln2g = self.sbp("ln2g", 128, 8)
        ln2b = self.sbp("ln2b", 128, 8)
        tok_wcf = self.gen_weight_fam("cf", FI, D, wcfT, waits=self.fence)
        tok_bcf = self.gen_bias_cols("cf", FI, bcf_cols, self.fence)
        tok_g2 = self.load_cols(self.inp["L0_attn_a_ln2_g"], 8, ln2g, self.fence)
        tok_b2 = self.load_cols(self.inp["L0_attn_a_ln2_b"], 8, ln2b, self.fence)
        ln2_tok = em.stream_tok("d_small")
        wcf_done = tok_wcf[-1]
        r5 = [self.sbp(f"a5r{i}", 128, 8 * 512) for i in range(2)]
        h2 = [self.sbp(f"a5h{i}", 128, 8 * 512, BF16) for i in range(2)]
        nsc5 = self.sbp("a5nsc", 128, 512)
        gsc1 = self.sbp("a5g1", 128, 512)
        gsc2 = self.sbp("a5g2", 128, 512)
        gst = [self.sbp(f"a5gs{i}", 128, 512, BF16) for i in range(4)]
        rfree5 = {0: [], 1: []}
        hfree5 = {0: [], 1: []}
        gfree5 = {i: [] for i in range(4)}
        prev_rep = []
        for tt in range(TT):
            b = tt % 2
            rb = r5[b]
            hb = h2[b]
            tok_r_ld = em.dma("sync", rb.rearrange("p (c t) -> p c t", c=8),
                              self.r_hbm.rearrange("(c p) t -> p c t", p=128)[
                                  :, :, tt * 512:(tt + 1) * 512],
                              waits=rfree5[b] + tok_r + self.fence, sem=f"d_a5r{b}")
            rtoks = self.row_math_outer(tt * 512, 512, [self.tok_stats_ready] + prev_rep)
            Aps, Bps, rep = self.rep_rows(self.rA[0:1, 0:512], self.rB[0:1, 0:512],
                                          512, rtoks, banks=(4, 5))
            prev_rep = rep
            htoks = self.normalize_tile(
                lambda ic, rb=rb: rb[:, ic * 512:(ic + 1) * 512],
                lambda ic, hb=hb: hb[:, ic * 512:(ic + 1) * 512],
                512, Aps, Bps, rep, (4, 5),
                waits=[tok_r_ld, ln2_tok] + hfree5[b], gcol=ln2g, bcol=ln2b,
                scratch=nsc5)
            rfree5[b] = htoks
            hfree5[b] = []
            for oc in range(32):
                bank = oc % 2
                pb, bw = self.ps.get(bank)
                mm = None
                for ic in range(8):
                    mm = self.t(lambda h, ic=ic, oc=oc, hb=hb, pb=pb,
                                       st=(ic == 0), sp=(ic == 7):
                                h.matmul(pb[:, 0:512],
                                         wcfT[:, ic * FI + oc * 128: ic * FI + oc * 128 + 128],
                                         hb[:, ic * 512:(ic + 1) * 512],
                                         start=st, stop=sp),
                                waits=htoks + bw + [wcf_done])
                hfree5[b].append(mm)
                e0 = self.s(lambda h, pb=pb, oc=oc: h.activation(
                    gsc1[:], pb[:, 0:512], AF.Identity, bias=bcf_cols[:, oc:oc + 1]),
                    waits=[mm] + tok_bcf + getattr(self, "_g_prev", []))
                self.ps.consumed(bank, e0)
                e1 = self.s(lambda h: h.activation(gsc2[:], gsc1[:], AF.Square))
                e2 = self.v(lambda h: h.tensor_mul(gsc2[:], gsc2[:], gsc1[:]),
                            waits=[e1])
                e3 = self.v(lambda h: h.scalar_tensor_tensor(
                    gsc2[:], gsc2[:], GELU_A, gsc1[:], ALU.mult, ALU.add))
                e4 = self.s(lambda h: h.activation(gsc2[:], gsc2[:], AF.Tanh,
                                                   scale=GELU_C), waits=[e3])
                e5 = self.v(lambda h: h.tensor_scalar(gsc2[:], gsc2[:], 0.5, 0.5,
                                                      ALU.mult, ALU.add), waits=[e4])
                gi = oc % 4
                e6 = self.v(lambda h, gi=gi: h.tensor_mul(gst[gi][:], gsc1[:],
                                                          gsc2[:]),
                            waits=gfree5[gi])
                self._g_prev = [e6]
                td = em.dma("sync", self.g_hbm[oc * 128:(oc + 1) * 128,
                                               tt * 512:(tt + 1) * 512],
                            gst[gi][:], waits=[e6], sem=f"d_g{gi}")
                gfree5[gi] = [td]
        tok_gh = [em.stream_tok(f"d_g{i}") for i in range(4)]

        self.new_phase()
        wcmT = self.sbp("wcmT", 128, 32 * D, BF16)
        bcm_cols = self.sbp("bcm_cols", 128, 8)
        tok_wcm = self.gen_weight_fam("cm", D, FI, wcmT, waits=self.fence)
        tok_bcm = self.gen_bias_cols("cm", D, bcm_cols, self.fence)
        wcm_done = tok_wcm[-1]
        self.stb = self.sbp("b5stb", 128, 8 * 512, BF16)
        self.stq = self.sbp("b5stq", 128, 8 * 512, BF16)
        self.stf = self.sbp("b5stf", 128, 512)
        g5 = [self.sbp("b5g0", 128, 32 * 512, BF16)]
        x25 = [self.sbp("b5x20", 128, 8 * 512)]
        msc = self.sbp("b5msc", 128, 512)
        gfree = {0: []}
        x2free = {0: []}
        for tt in range(TT):
            b = 0
            gb = g5[b]
            x2b = x25[b]
            xnb = x2b
            tok_g_ld = em.dma("sync", gb.rearrange("p (c t) -> p c t", c=32),
                              self.g_hbm.rearrange("(c p) t -> p c t", p=128)[
                                  :, :, tt * 512:(tt + 1) * 512],
                              waits=gfree[b] + tok_gh + self.fence, sem="d_b5g")
            tok_x2_ld = em.dma("sync", x2b.rearrange("p (c t) -> p c t", c=8),
                               self.x2_hbm.rearrange("(c p) t -> p c t", p=128)[
                                   :, :, tt * 512:(tt + 1) * 512],
                               waits=x2free[b] + tok_x2 + self.fence, sem="d_b5x")
            xn_toks = []
            for oc in range(8):
                bank = oc % 2
                pb, bw = self.ps.get(bank)
                mm = None
                for ic in range(32):
                    mm = self.t(lambda h, ic=ic, oc=oc, gb=gb, pb=pb,
                                       st=(ic == 0), sp=(ic == 31):
                                h.matmul(pb[:, 0:512],
                                         wcmT[:, ic * D + oc * 128: ic * D + oc * 128 + 128],
                                         gb[:, ic * 512:(ic + 1) * 512],
                                         start=st, stop=sp),
                                waits=[tok_g_ld, wcm_done] + bw)
                gfree[b].append(mm)
                e0 = self.s(lambda h, pb=pb, oc=oc: h.activation(
                    msc[:], pb[:, 0:512], AF.Identity, bias=bcm_cols[:, oc:oc + 1]),
                    waits=[mm] + tok_bcm + getattr(self, "_m_prev", []))
                self.ps.consumed(bank, e0)
                e1 = self.v(lambda h, oc=oc, x2b=x2b, xnb=xnb: h.tensor_add(
                    xnb[:, oc * 512:(oc + 1) * 512], msc[:],
                    x2b[:, oc * 512:(oc + 1) * 512]),
                    waits=[e0, tok_x2_ld] + getattr(self, "_xn_prev", []))
                self._m_prev = [e1]
                xn_toks.append(e1)
            td = em.dma("sync", self.x_hbm.rearrange("(c p) t -> p c t", p=128)[
                :, :, tt * 512:(tt + 1) * 512],
                xnb.rearrange("p (c t) -> p c t", c=8), waits=xn_toks, sem="d_xn")
            rows, casts = self.stats_tile(
                lambda ic, xnb=xnb: xnb[:, ic * 512:(ic + 1) * 512], 512, tt * 512,
                xn_toks)
            x2free[b] = xn_toks + casts + [td]
            self._xn_prev = casts + [td]
        self.tok_x_ready = [em.stream_tok("d_xn")]
        self.tok_stats_ready = em.stream_tok("c_v")

    def p_attn_b(self):
        em = self.em
        T = self.T
        TT = self.TT
        SLC = T // 16

        self.new_phase()
        wqkv = self.sbp("bq_w", 128, 8 * I3, BF16)
        wout = self.sbp("bq_wo", 128, 8 * D, BF16)
        bq_cols = self.sbp("bq_bc", 128, 24)
        bo_cols = self.sbp("bq_bo", 128, 8)
        t_wq = em.dma("gpsimd", wqkv.rearrange("p (c o) -> p c o", c=8),
                      self.inp["L0_attn_b_W_qkv"].rearrange("(c p) o -> p c o", p=128),
                      waits=self.fence, sem="d_bqw")
        t_wo = em.dma("gpsimd", wout.rearrange("p (c o) -> p c o", c=8),
                      self.inp["L0_attn_b_W_out"].rearrange("(c p) o -> p c o", p=128),
                      waits=self.fence, sem="d_bqw")
        tok_bq = self.load_cols(self.inp["L0_attn_b_b_qkv"], 24, bq_cols, self.fence)
        tok_bo = self.load_cols(self.inp["L0_attn_b_b_out"], 8, bo_cols, self.fence)
        tok_bias = em.stream_tok("d_small")

        G = 2
        self.at_sbufs = 1
        self.attn_alloc(G, "b")
        self.stb = self.sbp("bqstb", 128, 8 * 256, BF16)
        self.stq = self.sbp("bqstq", 128, 8 * 256, BF16)
        self.stf = self.sbp("bqstf", 128, 256)
        qp = self.sbp("bq_qp", 64, T, BF16)
        kp = self.sbp("bq_kp", 64, T, BF16)
        vf = self.sbp("bq_vf", 64, T, BF16)
        vtk = self.sbp("bq_vtk", 128, (T // 128) * 64, BF16)
        ohb = vf
        xs = self.sbp("bq_xs", 128, 8 * SLC)
        xns = self.sbp("bq_xns", 128, 8 * SLC, BF16)
        nscb = self.sbp("bq_nsc", 128, SLC)
        qkvs = [self.sbp(f"bq_qkvs{i}", 128, SLC, BF16) for i in range(2)]
        sasl = self.sbp("bq_sa", 128, 8 * SLC, BF16)
        xnew = xs
        mscb = nscb

        xsfree = list(self.fence)
        xnsfree = list(self.fence)
        qkvsfree = list(self.fence)
        qpfree = list(self.fence)
        kpfree = list(self.fence)
        vffree = list(self.fence)
        vtkfree = list(self.fence)
        ohfree = {g: list(self.fence) for g in range(2)}
        safree = list(self.fence)
        xnewfree = list(self.fence)
        prev_rep = []

        def prep_slice(hd):
            nonlocal xsfree, xnsfree, qkvsfree, qpfree, kpfree, vffree, \
                vtkfree, prev_rep
            col0 = hd * SLC
            tok_x = em.dma("sync", xs.rearrange("p (c t) -> p c t", c=8),
                           self.x_hbm.rearrange("(c p) t -> p c t", p=128)[
                               :, :, col0:col0 + SLC],
                           waits=xsfree + self.tok_x_ready, sem="d_bqx")
            rtoks = self.row_math_outer(col0, SLC, [self.tok_stats_ready] + prev_rep)
            Aps, Bps, rep = self.rep_rows(self.rA[0:1, 0:SLC], self.rB[0:1, 0:SLC],
                                          SLC, rtoks, banks=(4, 5))
            prev_rep = rep
            xn_toks = []
            for ic in range(8):
                u1 = self.v(lambda h, ic=ic: h.scalar_tensor_tensor(
                    nscb[:, 0:SLC], xs[:, ic * SLC:(ic + 1) * SLC], 1.0,
                    Aps[:, 0:SLC], ALU.mult, ALU.mult),
                    waits=[tok_x] + rep)
                u2 = self.v(lambda h, ic=ic: h.scalar_tensor_tensor(
                    xns[:, ic * SLC:(ic + 1) * SLC], nscb[:, 0:SLC], 1.0,
                    Bps[:, 0:SLC], ALU.mult, ALU.add), waits=xnsfree)
                xn_toks.append(u2)
            xnsfree = []
            for bk in (4, 5):
                self.ps.consumed(bk, xn_toks[-1])
            cp_toks = []
            qfree_in = {0: list(qkvsfree), 1: list(qkvsfree)}
            for oc in range(24):
                bank = oc % 2
                pb, bw = self.ps.get(bank)
                mm = None
                for ic in range(8):
                    mm = self.t(lambda h, ic=ic, oc=oc, pb=pb,
                                       st=(ic == 0), sp=(ic == 7):
                                h.matmul(pb[:, 0:SLC],
                                         wqkv[:, ic * I3 + oc * 128: ic * I3 + oc * 128 + 128],
                                         xns[:, ic * SLC:(ic + 1) * SLC],
                                         start=st, stop=sp),
                                waits=xn_toks + bw + [t_wq])
                qb = qkvs[oc % 2]
                ev = self.s(lambda h, pb=pb, oc=oc, qb=qb: h.activation(
                    qb[:], pb[:, 0:SLC], AF.Identity,
                    bias=bq_cols[:, oc:oc + 1]),
                    waits=[mm] + tok_bq + qfree_in[oc % 2])
                self.ps.consumed(bank, ev)
                qfree_in[oc % 2] = []
                for h64 in range(2):
                    c = 128 * oc + 64 * h64
                    j, r = divmod(c, 192)
                    part = r // 64
                    dstbuf = (qp, kp, vf)[part]
                    sfree = (qpfree if part == 0 else kpfree if part == 1
                             else vffree)
                    dap = dstbuf[0:64, j: j + 16 * (SLC - 1) + 1:16]
                    tcp = self.v(lambda h, h64=h64, dap=dap, qb=qb:
                                 h.tensor_copy(dap, qb[64 * h64:64 * h64 + 64, :]),
                                 waits=[ev] + sfree)
                    sfree.clear()
                    cp_toks.append(tcp)
                    qfree_in[oc % 2].append(tcp)
            qpfree = []
            kpfree = []
            vffree = []
            vtr_toks = []
            for kt in range(T // 128):
                seq = len(self.at_bfseq)
                pbk = self.ps.t["bf" if seq % 2 == 0 else "bf2"]
                stag = (self.at_bfw0 if seq < 2 else [self.at_bfseq[seq - 2]])
                ttr = self.t(lambda h, kt=kt, pbk=pbk: h.transpose(
                    pbk[:, 0:64],
                    vf[0:64, kt * 128:(kt + 1) * 128],
                    self.ident_bf[0:64, 0:64]),
                    waits=cp_toks + stag + [self.tok_const])
                tev = self.v(lambda h, kt=kt, pbk=pbk:
                             h.tensor_copy(vtk[:, kt * 64:(kt + 1) * 64],
                                           pbk[:, 0:64]),
                             waits=[ttr] + vtkfree)
                vtkfree = []
                self.at_bfseq.append(tev)
                vtr_toks.append(tev)
            xsfree = xn_toks
            self._bq_x_tok = tok_x
            return cp_toks, vtr_toks, tok_x

        def post_slice(hd, oh, x_tok):
            nonlocal safree, xnewfree, xsfree
            col0 = hd * SLC
            sa_toks = []
            for j in range(16):
                cfull = 64 * j
                oc, off = divmod(cfull, 128)
                tcp = self.v(lambda h, oc=oc, off=off, j=j, oh=oh:
                             h.tensor_copy(sasl[off:off + 64,
                                                oc * SLC:(oc + 1) * SLC],
                                           oh[0:64, j: j + 16 * (SLC - 1) + 1:16]),
                             waits=safree)
                sa_toks.append(tcp)
            safree = []
            out_toks = []
            for oc in range(8):
                bank = oc % 2
                pb, bw = self.ps.get(bank)
                mm = None
                for ic in range(8):
                    mm = self.t(lambda h, ic=ic, oc=oc, pb=pb,
                                       st=(ic == 0), sp=(ic == 7):
                                h.matmul(pb[:, 0:SLC],
                                         wout[:, ic * D + oc * 128: ic * D + oc * 128 + 128],
                                         sasl[:, ic * SLC:(ic + 1) * SLC],
                                         start=st, stop=sp),
                                waits=sa_toks + bw + [t_wo])
                e0 = self.s(lambda h, pb=pb, oc=oc: h.activation(
                    mscb[:], pb[:, 0:SLC], AF.Identity, bias=bo_cols[:, oc:oc + 1]),
                    waits=[mm] + tok_bo + getattr(self, "_bq_m_prev", []))
                self.ps.consumed(bank, e0)
                e1 = self.v(lambda h, oc=oc: h.tensor_add(
                    xnew[:, oc * SLC:(oc + 1) * SLC], mscb[:],
                    xs[:, oc * SLC:(oc + 1) * SLC]),
                    waits=[e0, x_tok] + xnewfree)
                self._bq_m_prev = [e1]
                out_toks.append(e1)
            xnewfree = []
            td = em.dma("sync", self.x_hbm.rearrange("(c p) t -> p c t", p=128)[
                :, :, col0:col0 + SLC],
                xnew.rearrange("p (c t) -> p c t", c=8), waits=out_toks,
                sem="d_bqo")
            if not getattr(self, "SKIP_BQ_STATS", False):
                rows, casts = self.stats_tile(
                    lambda ic: xnew[:, ic * SLC:(ic + 1) * SLC], SLC, col0, out_toks)
                xnewfree = casts + [td]
            else:
                xnewfree = out_toks + [td]
            xsfree = xsfree + out_toks + list(xnewfree)
            return sa_toks

        for hd in range(H):
            cp_toks, vtr_toks, x_tok = prep_slice(hd)
            tok_qk = ("c_v", em.cnt["c_v"])
            tok_v2 = ("c_v", em.cnt["c_v"])

            def dst_fn(g):
                w = ohfree[g % 2] + safree
                ohfree[g % 2] = []
                return ohb[0:64, g * G * 128:(g + 1) * G * 128], w

            def done_fn(g, oe):
                ohfree[g % 2].append(oe)

            qk_cons, v_cons = self.attn_core2(
                qp, kp, vtk, SCALE_B, dst_fn, done_fn, tok_qk, tok_qk, tok_v2)
            qpfree += qk_cons
            kpfree += qk_cons
            vtkfree = vtkfree + v_cons
            tx = em.dma("sync", xs.rearrange("p (c t) -> p c t", c=8),
                        self.x_hbm.rearrange("(c p) t -> p c t", p=128)[
                            :, :, hd * SLC:(hd + 1) * SLC],
                        waits=xsfree + self.tok_x_ready, sem="d_bqx")
            xsfree = []
            sa = post_slice(hd, ohb, tx)
            safree = list(sa)
        self.tok_x_ready = [em.stream_tok("d_bqo")]
        self.tok_stats_ready = em.stream_tok("c_v")

    def p_ffn(self, idx):
        em = self.em
        TT = self.TT
        T = self.T
        sfx = f"f{idx}"

        self.new_phase()
        w_em = self.sbp(f"{sfx}_wem", NE, T, BF16)


        self.push_phase()
        xf = [self.sbp(f"{sfx}_x{i}", 128, 8 * 512) for i in range(2)]
        xnb = [self.sbp(f"{sfx}_xn{i}", 128, 8 * 512, BF16) for i in range(2)]
        xsc = self.sbp(f"{sfx}_xsc", 128, 512)
        xsc2 = self.sbp(f"{sfx}_xsc2", 128, 512)
        wg = self.sbp(f"{sfx}_wg", 128, 64)
        gt1 = self.sbp(f"{sfx}_gt1", 128, 8)
        gt2 = self.sbp(f"{sfx}_gt2", 128, 8)
        gm1 = self.sbp(f"{sfx}_gm1", 128, 1)
        gm2 = self.sbp(f"{sfx}_gm2", 128, 1)
        gsum = self.sbp(f"{sfx}_gs", 128, 1)
        weq = self.sbp(f"{sfx}_weq", 128, 8)
        tok_wg = em.dma("scalar", wg.rearrange("p (c e) -> p c e", e=NE),
                        self.inp["L0_ffn_Wg"].rearrange("(c p) e -> p c e", p=128),
                        waits=self.fence, sem="d_small")
        xfree = {0: [], 1: []}
        xnfree = {0: list(self.fence), 1: list(self.fence)}
        prev_rep = []
        for tt in range(TT):
            b = tt % 2
            x = xf[b]
            xn = xnb[b]
            tok_x = em.dma("sync", x.rearrange("p (c t) -> p c t", c=8),
                           self.x_hbm.rearrange("(c p) t -> p c t", p=128)[
                               :, :, tt * 512:(tt + 1) * 512],
                           waits=xfree[b] + self.tok_x_ready + self.fence,
                           sem=f"d_{sfx}x{b}")
            rtoks = self.row_math_outer(tt * 512, 512,
                                        [self.tok_stats_ready] + prev_rep)
            Aps, Bps, rep = self.rep_rows(self.rA[0:1, 0:512], self.rB[0:1, 0:512],
                                          512, rtoks, banks=(4, 5))
            prev_rep = rep
            gp = {}
            gpw = {}
            for sub in range(4):
                gp[sub], gpw[sub] = self.ps.get(sub)
            g_mms = {sub: None for sub in range(4)}
            xn_toks = []
            for ic in range(8):
                u1 = self.v(lambda h, ic=ic, x=x: h.scalar_tensor_tensor(
                    xsc[:, 0:512], x[:, ic * 512:(ic + 1) * 512], 1.0,
                    Aps[:, 0:512], ALU.mult, ALU.mult),
                    waits=[tok_x] + rep)
                u2 = self.v(lambda h: h.scalar_tensor_tensor(
                    xsc2[:, 0:512], xsc[:, 0:512], 1.0, Bps[:, 0:512],
                    ALU.mult, ALU.add))
                cv = self.s(lambda h, ic=ic, xn=xn: h.activation(
                    xn[:, ic * 512:(ic + 1) * 512], xsc2[:, 0:512], AF.Copy),
                    waits=[u2] + xnfree[b])
                xn_toks.append(cv)
                for sub in range(4):
                    g_mms[sub] = self.t(lambda h, ic=ic, sub=sub, gp=gp:
                                        h.matmul(gp[sub][:, 0:8],
                                                 xsc2[:, sub * 128:(sub + 1) * 128],
                                                 wg[:, ic * 8:(ic + 1) * 8],
                                                 start=(ic == 0), stop=(ic == 7)),
                                        waits=[u2, tok_wg] + (gpw[sub] if ic == 0 else []))
                self.em.last["vector"] = u2
                if ic < 7:
                    for s in range(4):
                        self.em.wait("vector", g_mms[s])
                    self.em.wait("vector", cv)
            td_xn = em.dma("sync", self.xn_hbm.rearrange("(c p) t -> p c t", p=128)[
                :, :, tt * 512:(tt + 1) * 512],
                xn.rearrange("p (c t) -> p c t", c=8), waits=xn_toks,
                sem=f"d_{sfx}xn{b}")
            xnfree[b] = [td_xn]
            xfree[b] = xn_toks
            for sub in range(4):
                pbk = gp[sub]
                e0 = self.s(lambda h, pbk=pbk: h.activation(
                    gt1[:], pbk[:, 0:8], AF.Exp, scale=-1.0),
                    waits=[g_mms[sub]] + getattr(self, "_gate_prev", []))
                e1 = self.v(lambda h: h.tensor_scalar(gt1[:], gt1[:], 1.0, None,
                                                      ALU.add), waits=[e0])
                e2 = self.v(lambda h: h.reciprocal(gt1[:], gt1[:]))
                e3 = self.v(lambda h: h.reduce_max(gm1[:], gt1[:], axis=AX.X))
                e4 = self.v(lambda h: h.tensor_single_scalar(weq[:], gt1[:], gm1[:],
                                                             ALU.is_equal))
                e5 = self.v(lambda h: h.scalar_tensor_tensor(
                    gt2[:], weq[:], -10000.0, gt1[:], ALU.mult, ALU.add))
                e6 = self.v(lambda h: h.reduce_max(gm2[:], gt2[:], axis=AX.X))
                e7 = self.v(lambda h: h.tensor_add(gsum[:], gm1[:], gm2[:]))
                e8 = self.v(lambda h: h.reciprocal(gsum[:], gsum[:]))
                e11 = self.v(lambda h: h.tensor_single_scalar(gt2[:], gt2[:], gm2[:],
                                                              ALU.is_equal))
                e9 = self.v(lambda h: h.tensor_mul(gm1[:], gm1[:], gsum[:]))
                e10 = self.v(lambda h: h.tensor_mul(gm2[:], gm2[:], gsum[:]))
                e12 = self.v(lambda h: h.tensor_scalar(weq[:], weq[:], gm1[:], None,
                                                       ALU.mult))
                e13 = self.v(lambda h: h.scalar_tensor_tensor(
                    weq[:], gt2[:], gm2[:], weq[:], ALU.mult, ALU.add))
                self.ps.consumed(sub, e0)
                tpb2, tbw2 = self.ps.get(sub)
                ttr = self.t(lambda h, tpb2=tpb2: h.transpose(
                    tpb2[0:8, 0:128], weq[:, 0:8], self.ident_f[:]),
                    waits=[e13, self.tok_const] + tbw2)
                dst = w_em[0:8, tt * 512 + sub * 128: tt * 512 + (sub + 1) * 128]
                tev = self.s(lambda h, tpb2=tpb2, dst=dst: h.activation(
                    dst, tpb2[0:8, 0:128], AF.Copy),
                    waits=[ttr] + getattr(self, "_wem_prev", self.fence))
                self.ps.consumed(sub, tev)
                self._gate_prev = [e13, tev]
                self._wem_prev = []
        tok_xn = [em.stream_tok(f"d_{sfx}xn{i}") for i in range(2)]
        tok_wem = em.stream_tok("c_s")
        self.tap_hbm(f"tap_xn{idx}", self.xn_hbm, tok_xn)

        self.pop_phase()
        self.push_phase()
        ohu = self.sbp(f"{sfx}_ohu", NE, 128, BF16)
        ohu_free = list(self.fence)
        w1 = self.sbp(f"{sfx}_w1", 128, 8 * FF, BF16)
        w3 = self.sbp(f"{sfx}_w3", 128, 8 * FF, BF16)
        w2 = self.sbp(f"{sfx}_w2", 128, 16 * D, BF16)
        xnu = self.sbp(f"{sfx}_xnu", 128, 8 * 512, BF16)
        xnw = self.sbp(f"{sfx}_xnw", 128, 8 * 512, BF16)
        hst = self.sbp(f"{sfx}_h", 128, 16 * 512, BF16)
        usg = self.sbp(f"{sfx}_u", 128, 512, BF16)
        vsg = self.sbp(f"{sfx}_v", 128, 512, BF16)
        acs = self.sbp(f"{sfx}_acs", 128, 4 * 512)
        wrp = self.sbp(f"{sfx}_wrp", 128, 512, BF16)
        w1free = list(self.fence)
        w3free = list(self.fence)
        w2free = list(self.fence)
        xnufree = list(self.fence)
        xnwfree = list(self.fence)
        hfree = list(self.fence)
        acsfree = list(self.fence)
        wrpfree = list(self.fence)
        self._ffn_uprev = list(self.fence)
        self._ffn_vprev = list(self.fence)
        acc_tok = {tt: None for tt in range(TT)}
        units = [("s", None)] + [("e", e) for e in range(NE)]
        for ui, (kind, e) in enumerate(units):
            if kind == "s":
                src1 = self.inp["L0_ffn_Ws1"]
                src3 = self.inp["L0_ffn_Ws3"]
                src2 = self.inp["L0_ffn_Ws2"]
            else:
                src1 = self.inp["L0_ffn_We1"][e]
                src3 = self.inp["L0_ffn_We3"][e]
                src2 = self.inp["L0_ffn_We2"][e]
            if kind == "e":
                t_oh = em.dma("gpsimd", ohu[:], self.inp["c_onehot"][:, e * 128:(e + 1) * 128],
                              waits=ohu_free, sem=f"d_{sfx}oh")
            t1 = em.dma("gpsimd", w1.rearrange("p (c o) -> p c o", c=8),
                        src1.rearrange("(c p) o -> p c o", p=128),
                        waits=w1free, sem=f"d_{sfx}w1")
            t3 = em.dma("gpsimd", w3.rearrange("p (c o) -> p c o", c=8),
                        src3.rearrange("(c p) o -> p c o", p=128),
                        waits=w3free, sem=f"d_{sfx}w3")
            t2 = em.dma("gpsimd", w2.rearrange("p (c o) -> p c o", c=16),
                        src2.rearrange("(c p) o -> p c o", p=128),
                        waits=w2free, sem=f"d_{sfx}w2")
            w1free, w3free, w2free = [], [], []
            for tt in range(TT):
                tok_xn_ld = em.dma("sync", xnu.rearrange("p (c t) -> p c t", c=8),
                                   self.xn_hbm.rearrange("(c p) t -> p c t", p=128)[
                                       :, :, tt * 512:(tt + 1) * 512],
                                   waits=xnufree + tok_xn, sem=f"d_{sfx}xnu")
                xin = xnu
                xin_tok = [tok_xn_ld]
                if kind == "e":
                    rpb, rbw = self.ps.get(5)
                    rm = self.t(lambda h, e=e, tt=tt, rpb=rpb: h.matmul(
                        rpb[:, 0:512], ohu[:, 0:128],
                        w_em[0:8, tt * 512:(tt + 1) * 512], start=True, stop=True),
                        waits=[t_oh, tok_wem] + rbw)
                    re = self.s(lambda h, rpb=rpb: h.activation(
                        wrp[:], rpb[:, 0:512], AF.Copy), waits=[rm] + wrpfree)
                    self.ps.consumed(5, re)
                    xw_toks = []
                    for ic in range(8):
                        xw = self.v(lambda h, ic=ic: h.tensor_mul(
                            xnw[:, ic * 512:(ic + 1) * 512],
                            xnu[:, ic * 512:(ic + 1) * 512], wrp[:]),
                            waits=[re, tok_xn_ld] + xnwfree)
                        xw_toks.append(xw)
                    xnwfree = []
                    wrpfree = xw_toks
                    xin = xnw
                    xin_tok = xw_toks
                h_toks = []
                for oc in range(16):
                    bu = oc % 2
                    bv = 2 + oc % 2
                    pu, puw = self.ps.get(bu)
                    pv, pvw = self.ps.get(bv)
                    mmu = None
                    mmv = None
                    for ic in range(8):
                        mmu = self.t(lambda h, ic=ic, oc=oc, pu=pu,
                                            st=(ic == 0), sp=(ic == 7):
                                     h.matmul(pu[:, 0:512],
                                              w1[:, ic * FF + oc * 128: ic * FF + oc * 128 + 128],
                                              xnu[:, ic * 512:(ic + 1) * 512],
                                              start=st, stop=sp),
                                     waits=[t1, tok_xn_ld] + (puw if ic == 0 else []))
                        mmv = self.t(lambda h, ic=ic, oc=oc, pv=pv, xin=xin,
                                            st=(ic == 0), sp=(ic == 7):
                                     h.matmul(pv[:, 0:512],
                                              w3[:, ic * FF + oc * 128: ic * FF + oc * 128 + 128],
                                              xin[:, ic * 512:(ic + 1) * 512],
                                              start=st, stop=sp),
                                     waits=[t3] + xin_tok + (pvw if ic == 0 else []))
                    eu = self.s(lambda h, pu=pu: h.activation(usg[:], pu[:, 0:512],
                                                              AF.Silu),
                                waits=[mmu] + getattr(self, "_ffn_uprev", []))
                    self.ps.consumed(bu, eu)
                    ev = self.v(lambda h, pv=pv: h.tensor_copy(vsg[:], pv[:, 0:512]),
                                waits=[mmv] + getattr(self, "_ffn_vprev", []))
                    self.ps.consumed(bv, ev)
                    hh = self.v(lambda h, oc=oc: h.tensor_mul(
                        hst[:, oc * 512:(oc + 1) * 512], usg[:], vsg[:]),
                        waits=[eu] + hfree)
                    self._ffn_uprev = [hh]
                    self._ffn_vprev = [hh]
                    h_toks.append(hh)
                hfree = []
                xnufree = h_toks
                if kind == "e":
                    xnwfree = h_toks
                accw = [acc_tok[tt]] if acc_tok[tt] is not None else list(self.fence)
                for half in range(2):
                    ac_toks = []
                    for oc4 in range(4):
                        oc = half * 4 + oc4
                        bk = oc % 2
                        po, pow_ = self.ps.get(bk)
                        mmo = None
                        for ic in range(16):
                            mmo = self.t(lambda h, ic=ic, oc=oc, po=po,
                                                st=(ic == 0), sp=(ic == 15):
                                         h.matmul(po[:, 0:512],
                                                  w2[:, ic * D + oc * 128: ic * D + oc * 128 + 128],
                                                  hst[:, ic * 512:(ic + 1) * 512],
                                                  start=st, stop=sp),
                                         waits=[t2] + h_toks + (pow_ if ic == 0 else []))
                        hfree.append(mmo)
                        ea = self.s(lambda h, po=po, oc4=oc4: h.activation(
                            acs[:, oc4 * 512:(oc4 + 1) * 512], po[:, 0:512], AF.Copy),
                            waits=[mmo] + acsfree)
                        self.ps.consumed(bk, ea)
                        ac_toks.append(ea)
                    acsfree = []
                    td = em.dma("gpsimd",
                                self.acc_hbm.rearrange("(c p) t -> p c t", p=128)[
                                    :, half * 4:(half + 1) * 4,
                                    tt * 512:(tt + 1) * 512],
                                acs.rearrange("p (c t) -> p c t", c=4),
                                waits=ac_toks + accw, sem=f"d_{sfx}acc",
                                **({} if ui == 0 else {"accum_op": ALU.add}))
                    accw = [td]
                    acsfree = [td]
                acc_tok[tt] = td
            w1free = [td]
            w3free = [td]
            w2free = [td]
            if kind == "e":
                ohu_free = [rm]
        self.tap_hbm(f"tap_acc{idx}", self.acc_hbm,
                     [em.stream_tok(f"d_{sfx}acc")])
        if f"tap_wem{idx}" in self.taps:
            em.dma("sync", self.taps[f"tap_wem{idx}"], w_em[0:NE, 0:T],
                   waits=[tok_wem], sem="d_tap")
        self.pop_phase()
        self.push_phase()
        self.stb = self.sbp(f"{sfx}_stb", 128, 8 * 512, BF16)
        self.stq = self.sbp(f"{sfx}_stq", 128, 8 * 512, BF16)
        self.stf = self.sbp(f"{sfx}_stf", 128, 512)
        af = [self.sbp(f"{sfx}_af{i}", 128, 8 * 512) for i in range(2)]
        xpf = [self.sbp(f"{sfx}_xpf{i}", 128, 8 * 512) for i in range(2)]
        affree = {0: [], 1: []}
        xpffree = {0: [], 1: []}
        is_last = (idx == 2)
        for tt in range(TT):
            b = tt % 2
            ab = af[b]
            xb = xpf[b]
            tok_a = em.dma("sync", ab.rearrange("p (c t) -> p c t", c=8),
                           self.acc_hbm.rearrange("(c p) t -> p c t", p=128)[
                               :, :, tt * 512:(tt + 1) * 512],
                           waits=affree[b] + [acc_tok[tt],
                                              em.stream_tok(f"d_{sfx}acc")] + self.fence,
                           sem=f"d_{sfx}a{b}")
            tok_x = em.dma("sync", xb.rearrange("p (c t) -> p c t", c=8),
                           self.x_hbm.rearrange("(c p) t -> p c t", p=128)[
                               :, :, tt * 512:(tt + 1) * 512],
                           waits=xpffree[b] + self.tok_x_ready + self.fence,
                           sem=f"d_{sfx}px{b}")
            out_toks = []
            for ic in range(8):
                u1 = self.v(lambda h, ic=ic, ab=ab, xb=xb: h.scalar_tensor_tensor(
                    ab[:, ic * 512:(ic + 1) * 512], ab[:, ic * 512:(ic + 1) * 512],
                    1.0 / 3.0, xb[:, ic * 512:(ic + 1) * 512], ALU.mult, ALU.add),
                    waits=[tok_a, tok_x])
                out_toks.append(u1)
            td = em.dma("sync", self.x_hbm.rearrange("(c p) t -> p c t", p=128)[
                :, :, tt * 512:(tt + 1) * 512],
                ab.rearrange("p (c t) -> p c t", c=8), waits=out_toks,
                sem=f"d_{sfx}o{b}")
            affree[b] = [td]
            xpffree[b] = out_toks
            if not is_last:
                rows, casts = self.stats_tile(
                    lambda ic, ab=ab: ab[:, ic * 512:(ic + 1) * 512], 512, tt * 512,
                    out_toks)
                affree[b] = affree[b] + casts
        self.tok_x_ready = [em.stream_tok(f"d_{sfx}o0"), em.stream_tok(f"d_{sfx}o1")]
        self.tok_stats_ready = em.stream_tok("c_v")
        self.pop_phase()
        self.pop_phase()
        self.push_phase()

    def p0_ingest(self):
        em = self.em
        self.new_phase()
        self.p0_tm = [self.sbp(f"p0_tm{i}", 128, 4 * 1024) for i in range(2)]
        self.p0_x = [self.sbp(f"p0_x{i}", 128, 8 * 512) for i in range(2)]
        self.stb = self.sbp("p0stb", 128, 8 * 512, BF16)
        self.stq = self.sbp("p0stq", 128, 8 * 512, BF16)
        self.stf = self.sbp("p0stf", 128, 512)
        tmfree = {0: [], 1: []}
        xfree = {0: [], 1: []}
        for tt in range(self.TT):
            b = tt % 2
            tm = self.p0_tm[b]
            xf = self.p0_x[b]
            tok_ld = em.dma("sync",
                            tm.rearrange("p (s f) -> p s f", s=4),
                            self.hs[tt * 512:(tt + 1) * 512, :].rearrange(
                                "(s p) f -> p s f", p=128),
                            waits=tmfree[b], sem=f"d_p0_{b}")
            ev_toks = []
            tr_all = []
            for idx in range(32):
                s4, ic = idx // 8, idx % 8
                bank = 3 + (idx % 2)
                pb, bw = self.ps.get(bank)
                src = tm[:, s4 * 1024 + ic * 128: s4 * 1024 + ic * 128 + 128]
                tr = self.t(
                    lambda h, s=src, p=pb[:, 0:128]:
                    h.transpose(p, s, self.ident_f[:]),
                    waits=[tok_ld, self.tok_const] + bw)
                tr_all.append(tr)
                dst = xf[:, ic * 512 + s4 * 128: ic * 512 + s4 * 128 + 128]
                srcp = pb[:, 0:128]
                if idx % 2 == 0:
                    tev = self.s(lambda h, p=srcp, d=dst: h.activation(d, p, AF.Copy),
                                 waits=[tr] + xfree[b])
                else:
                    tev = self.v(lambda h, p=srcp, d=dst: h.tensor_copy(d, p),
                                 waits=[tr] + xfree[b])
                ev_toks.append(tev)
                self.ps.consumed(bank, tev)
            tmfree[b] = tr_all
            tok_st = em.dma("sync",
                            self.x_hbm.rearrange("(c p) t -> p c t", p=128)[
                                :, :, tt * 512:(tt + 1) * 512],
                            xf.rearrange("p (c t) -> p c t", c=8),
                            waits=ev_toks, sem=f"d_xst{b}")
            rows, casts = self.stats_tile(
                lambda ic, xf=xf: xf[:, ic * 512:(ic + 1) * 512], 512, tt * 512,
                ev_toks)
            xfree[b] = casts + [tok_st]
        self.tok_x_ready = [em.stream_tok("d_xst0"), em.stream_tok("d_xst1")]
        self.tok_stats_ready = em.stream_tok("c_v")

    def finish(self, ready_tok=None):
        em = self.em
        self.new_phase()
        fstage = [self.sbp(f"fin_x{i}", 128, 8 * 512) for i in range(2)]
        ostage = [self.sbp(f"fin_o{i}", 128, 4 * 1024) for i in range(2)]
        ffree = {0: [], 1: []}
        ofree = {0: [], 1: []}
        waits0 = list(ready_tok) if ready_tok is not None else list(self.tok_x_ready)
        for tt in range(self.TT):
            b = tt % 2
            xf = fstage[b]
            om = ostage[b]
            tok_ld = em.dma("sync",
                            xf.rearrange("p (c t) -> p c t", c=8),
                            self.x_hbm.rearrange("(c p) t -> p c t", p=128)[
                                :, :, tt * 512:(tt + 1) * 512],
                            waits=ffree[b] + waits0 + self.fence, sem=f"d_fin{b}")
            ev_toks = []
            tr_all = []
            for idx in range(32):
                ic, s4 = idx // 4, idx % 4
                bank = 3 + (idx % 2)
                pb, bw = self.ps.get(bank)
                src = xf[:, ic * 512 + s4 * 128: ic * 512 + s4 * 128 + 128]
                tr = self.t(
                    lambda h, s=src, p=pb[:, 0:128]:
                    h.transpose(p, s, self.ident_f[:]),
                    waits=[tok_ld, self.tok_const] + bw)
                tr_all.append(tr)
                dst = om[:, s4 * 1024 + ic * 128: s4 * 1024 + ic * 128 + 128]
                srcp = pb[:, 0:128]
                if idx % 2 == 0:
                    tev = self.s(lambda h, p=srcp, d=dst: h.activation(d, p, AF.Copy),
                                 waits=[tr] + ofree[b])
                else:
                    tev = self.v(lambda h, p=srcp, d=dst: h.tensor_copy(d, p),
                                 waits=[tr] + ofree[b])
                ev_toks.append(tev)
                self.ps.consumed(bank, tev)
            ffree[b] = tr_all
            tok_st = em.dma("sync",
                            self.out[tt * 512:(tt + 1) * 512, :].rearrange(
                                "(s p) f -> p s f", p=128),
                            om.rearrange("p (s f) -> p s f", s=4),
                            waits=ev_toks, sem=f"d_out{b}")
            ofree[b] = [tok_st]
        em.wait("sync", em.stream_tok("d_out0"))
        em.wait("sync", em.stream_tok("d_out1"))

    def build_phases(self):
        self.phase_stack = []
        self.fence = []
        self.p0_ingest()
        if self.n_sub >= 1:
            self.p_attn_a()
        if self.n_sub >= 2:
            self.p_ffn(1)
        if self.n_sub >= 3:
            self.p_attn_b()
        if self.n_sub >= 4:
            self.p_ffn(2)
        self.finish()

    def build(self, detect_races=True):
        self.nc = bass.Bass("TRN2", target_bir_lowering=False,
                            detect_race_conditions=detect_races)
        self.es = ExitStack()
        self.inp = {}
        self.declare()
        with self.es:
            self.em = Em(self.nc, self.es)
            self.ps = Psum(self.nc, self.es)
            self.alloc_persistent()
            self.build_phases()
            while self.phase_stack:
                self.pop_phase()
            self.flush_block()
        return self.nc

    def _replayer(self, eng):
        q = list(self.em.q[eng])

        def run(h):
            for fn in q:
                fn(h)
        return run


def make_in_maps(inputs, T):
    consts = host_constants()
    full = inputs["L0_attn_a_hs"]
    B = full.shape[0]
    in_maps = []
    for b in range(B):
        m = {k: np.ascontiguousarray(v) for k, v in inputs.items()
             if k != "L0_attn_a_hs"}
        m["L0_attn_a_hs"] = np.ascontiguousarray(full[b, :T])
        m.update(consts)
        in_maps.append(m)
    return in_maps


def kernel(**inputs):
    T = 4096
    bld = Builder(T=T)
    nc = bld.build()
    in_maps = make_in_maps(inputs, T)
    res = run_bass_kernel_spmd(nc, in_maps, core_ids=list(range(len(in_maps))))
    return np.stack([np.asarray(r["out"]) for r in res.results], axis=0)


# version 06
# speedup vs baseline: 1.0303x; 1.0303x over previous
from contextlib import ExitStack

import numpy as np

import concourse.bass as bass
import concourse.mybir as mybir
from concourse.bass_utils import run_bass_kernel_spmd

F32 = mybir.dt.float32
BF16 = mybir.dt.bfloat16
AF = mybir.ActivationFunctionType
ALU = mybir.AluOpType
AX = mybir.AxisListType

ENGINES = ("sync", "gpsimd", "tensor", "vector", "scalar")
CHAINED = ("vector", "scalar", "gpsimd")

D = 1024
H = 16
HD = 64
I3 = 3072
FI = 4096
FF = 2048
NE = 8
EPS = 1e-5
GELU_C = 0.7978845608028654
GELU_A = 0.044715
SCALE_A = 0.125
SCALE_B = (2 * 1024) ** -0.5
MASK_NEG = -1.0e9
CONST_VALS = (0.0, 1.0, EPS, -0.5, -1.0, 0.5, 2.0, SCALE_A, SCALE_B, GELU_C,
              GELU_A, 1.0 / D, 1.0 / 3.0)


class Em:

    def __init__(self, nc, es):
        self.nc = nc
        self.es = es
        self.q = {e: [] for e in ENGINES}
        self.sems = {}
        self.cnt = {}
        self.mark = {e: {} for e in ENGINES}
        self.last = {e: None for e in ENGINES}

    def sem(self, name):
        if name not in self.sems:
            self.sems[name] = self.es.enter_context(self.nc.semaphore(name))
            self.cnt[name] = 0
        return self.sems[name]

    def op(self, eng, fn, waits=(), sem=None, inc=0):
        if eng in CHAINED:
            self.wait(eng, self.last[eng])
            if not inc:
                sem, inc = "c_" + eng[0], 1
        for t in waits:
            self.wait(eng, t)
        if inc:
            s = self.sem(sem)
            self.cnt[sem] += inc
            val = self.cnt[sem]

            def emit(h, fn=fn, s=s, inc=inc):
                fn(h).then_inc(s, inc)

            self.q[eng].append(emit)
            tok = (sem, val)
            if eng in CHAINED and inc == 1:
                self.last[eng] = tok
            return tok
        self.q[eng].append(fn)
        return None

    def wait(self, eng, tok):
        if tok is None:
            return
        name, val = tok
        if name.startswith("d_") and val != self.cnt[name]:
            val = self.cnt[name]
        if self.mark[eng].get(name, 0) >= val:
            return
        self.mark[eng][name] = val
        s = self.sems[name]
        self.q[eng].append(lambda h, s=s, val=val: h.wait_ge(s, val))

    def dma(self, queue, out, in_, waits=(), sem=None, **kw):
        assert sem is not None

        def fn(h, out=out, in_=in_, kw=kw):
            return h.dma_start(out=out, in_=in_, **kw)

        if queue == "gpsimd":
            for t in waits:
                self.wait(queue, t)
            s = self.sem(sem)
            self.cnt[sem] += 16
            val = self.cnt[sem]

            def emit(h, fn=fn, s=s):
                fn(h).then_inc(s, 16)

            self.q[queue].append(emit)
            return (sem, val)
        return self.op(queue, fn, waits=waits, sem=sem, inc=16)

    def stream_tok(self, sem):
        self.sem(sem)
        return (sem, self.cnt[sem])


class Psum:

    def __init__(self, nc, es):
        self.t = {i: es.enter_context(nc.psum_tensor(f"pb{i}", [128, 512], F32))
                  for i in range(6)}
        self.t["bf"] = es.enter_context(nc.psum_tensor("pbf", [128, 1024], BF16))
        self.t["bf2"] = es.enter_context(nc.psum_tensor("pbf2", [128, 1024], BF16))
        self.free = {k: [] for k in self.t}

    def get(self, i):
        w = self.free[i]
        self.free[i] = []
        return self.t[i], w

    def consumed(self, i, tok):
        if tok is not None:
            self.free[i].append(tok)


def host_constants():
    import ml_dtypes
    bf = ml_dtypes.bfloat16
    c = {}
    c["c_ident_bf"] = np.eye(128, dtype=bf)
    c["c_ident_f32"] = np.eye(128, dtype=np.float32)
    c["c_ones_row_f32"] = np.ones((1, 128), dtype=np.float32)
    c["c_ones_col_bf"] = np.ones((128, 1), dtype=bf)
    pool = np.tile(np.asarray(CONST_VALS, dtype=np.float32)[None, :], (128, 1))
    c["c_pool"] = pool
    c["c_pool_bf"] = pool.astype(bf)
    dm = np.zeros((128, 640), dtype=np.float32)
    qq = np.arange(128)[:, None]
    kk = np.arange(128)[None, :]
    dm[:, 512:640] = np.where(kk > qq, MASK_NEG, 0.0).astype(np.float32)
    c["c_dmask"] = dm
    oh = np.zeros((NE, NE, 128), dtype=np.float32)
    for e in range(NE):
        oh[e, e, :] = 1.0
    c["c_onehot"] = np.ascontiguousarray(oh.transpose(1, 0, 2).reshape(NE, NE * 128))
    return c


class Builder:
    def __init__(self, T=4096, taps=(), n_sub=4):
        self.T = T
        self.TT = T // 512
        self.taps = {}
        self.tap_req = list(taps)
        self.n_sub = n_sub

    def di(self, name, shape, dt=F32):
        ap = self.nc.dram_tensor(name, list(shape), dt, kind="ExternalInput").ap()
        self.inp[name] = ap
        return ap

    def scratch(self, name, shape, dt=F32):
        return self.nc.dram_tensor(name, list(shape), dt, kind="Internal").ap()

    def sb(self, name, p, f, dt=F32):
        return self.es.enter_context(self.nc.sbuf_tensor(name, [p, f], dt))

    def declare(self):
        T = self.T
        self.hs = self.di("L0_attn_a_hs", (T, D))
        for nm in ("ln1_g", "ln1_b", "ln2_g", "ln2_b"):
            self.di(f"L0_attn_a_{nm}", (D,))
        for fam, (o, i) in (("ca", (I3, D)), ("cp", (D, D)), ("cf", (FI, D)),
                            ("cm", (D, FI))):
            self.di(f"L0_attn_a_{fam}_sub_w", (2, o, i))
            self.di(f"L0_attn_a_{fam}_Wl_w", (1, 2))
            self.di(f"L0_attn_a_{fam}_W_w", (o, i))
            self.di(f"L0_attn_a_{fam}_B_w", (o, i))
            self.di(f"L0_attn_a_{fam}_sub_b", (2, o))
            self.di(f"L0_attn_a_{fam}_Wl_b", (1, 2))
            self.di(f"L0_attn_a_{fam}_W_b", (o,))
            self.di(f"L0_attn_a_{fam}_B_b", (o,))
        self.di("L0_attn_b_W_qkv", (D, I3))
        self.di("L0_attn_b_b_qkv", (I3,))
        self.di("L0_attn_b_W_out", (D, D))
        self.di("L0_attn_b_b_out", (D,))
        self.di("L0_ffn_Wg", (D, NE))
        self.di("L0_ffn_Ws1", (D, FF))
        self.di("L0_ffn_Ws3", (D, FF))
        self.di("L0_ffn_Ws2", (FF, D))
        self.di("L0_ffn_We1", (NE, D, FF))
        self.di("L0_ffn_We3", (NE, D, FF))
        self.di("L0_ffn_We2", (NE, FF, D))
        self.di("c_ident_bf", (128, 128), BF16)
        self.di("c_ident_f32", (128, 128))
        self.di("c_ones_row_f32", (1, 128))
        self.di("c_ones_col_bf", (128, 1), BF16)
        self.di("c_pool", (128, len(CONST_VALS)))
        self.di("c_pool_bf", (128, len(CONST_VALS)), BF16)
        self.di("c_dmask", (128, 640))
        self.di("c_onehot", (NE, NE * 128))

        self.out = self.nc.dram_tensor("out", [T, D], F32, kind="ExternalOutput").ap()
        for (name, shape, dt) in self.tap_req:
            self.taps[name] = self.nc.dram_tensor(
                name, list(shape), dt, kind="ExternalOutput").ap()

        self.x_hbm = self.scratch("x_hbm", (D, T))
        self.q_hbm = self.scratch("q_hbm", (D, T), BF16)
        self.k_hbm = self.scratch("k_hbm", (D, T), BF16)
        self.v_hbm = self.scratch("v_hbm", (T, D), BF16)
        self.o_hbm = self.scratch("o_hbm", (D, T), BF16)
        self.xn_hbm = self.scratch("xn_hbm", (D, T), BF16)
        self.r_hbm = self.scratch("r_hbm", (D, T))
        self.x2_hbm = self.scratch("x2_hbm", (D, T))
        self.acc_hbm = self.scratch("acc_hbm", (D, T))
        self.g_hbm = self.scratch("g_hbm", (FI, T), BF16)

    def alloc_persistent(self):
        em = self.em
        self.ident_bf = self.sb("ident_bf", 128, 128, BF16)
        self.ident_f = self.sb("ident_f", 128, 128)
        self.ones_row = self.sb("ones_row", 1, 128)
        self.ones_row_bf = self.sb("ones_row_bf", 1, 128, BF16)
        self.ones_col_bf = self.sb("ones_col_bf", 128, 1, BF16)
        self.sum_row = self.sb("sum_row", 1, self.T)
        self.ssq_row = self.sb("ssq_row", 1, self.T)
        self.rA = self.sb("rA", 1, 512)
        self.rB = self.sb("rB", 1, 512)
        self.rt1 = self.sb("rt1", 1, 512)
        self.rt2 = self.sb("rt2", 1, 512)
        self.rt3 = self.sb("rt3", 1, 512)

        em.dma("sync", self.ident_bf[:], self.inp["c_ident_bf"], sem="d_const")
        em.dma("sync", self.ident_f[:], self.inp["c_ident_f32"], sem="d_const")
        em.dma("sync", self.ones_row[:], self.inp["c_ones_row_f32"], sem="d_const")
        em.dma("sync", self.ones_col_bf[:], self.inp["c_ones_col_bf"], sem="d_const")
        self.cpool = self.sb("cpool", 128, len(CONST_VALS))
        self.cpool_bf = self.sb("cpool_bf", 128, len(CONST_VALS), BF16)
        em.dma("sync", self.cpool[:], self.inp["c_pool"], sem="d_const")
        em.dma("sync", self.cpool_bf[:], self.inp["c_pool_bf"], sem="d_const")
        for j, val in enumerate(CONST_VALS):
            self.nc.const_aps.register(F32, val, self.cpool[:, j:j + 1])
            self.nc.const_aps.register(BF16, val, self.cpool_bf[:, j:j + 1])
        self.tok_const = em.stream_tok("d_const")
        for e in CHAINED:
            em.last[e] = self.tok_const
        self.tok_onesbf = self.v(
            lambda h: h.tensor_copy(self.ones_row_bf[:], self.ones_row[:]),
            waits=[self.tok_const])

    def v(self, fn, waits=()):
        return self.em.op("vector", fn, waits=waits)

    def s(self, fn, waits=()):
        return self.em.op("scalar", fn, waits=waits)

    def t(self, fn, waits=()):
        return self.em.op("tensor", fn, waits=waits, sem="c_t", inc=1)

    def g(self, fn, waits=()):
        return self.em.op("gpsimd", fn, waits=waits)

    def stats_tile(self, src_fn, W, col0, waits, banks=(2, 0)):
        prev = getattr(self, "_stats_prev_mm", [])
        rows = []
        all_src_toks = []
        for j in range(2):
            cast_toks = []
            for ic in range(8):
                hi = self.stb[:, ic * W:(ic + 1) * W]
                lo = self.stq[:, ic * W:(ic + 1) * W]
                if j == 0:
                    t1 = self.s(lambda h, s=src_fn(ic), hi=hi:
                                h.activation(hi, s, AF.Copy),
                                waits=list(waits) + prev)
                    t2 = self.v(lambda h, s=src_fn(ic), hi=hi, lo=lo:
                                h.scalar_tensor_tensor(lo, hi, -1.0, s,
                                                       ALU.mult, ALU.add),
                                waits=[t1] + prev)
                else:
                    t0 = self.s(lambda h, s=src_fn(ic):
                                h.activation(self.stf[:, 0:W], s, AF.Square),
                                waits=list(waits) + prev +
                                getattr(self, "_stf_prev", []))
                    t1 = self.s(lambda h, hi=hi:
                                h.activation(hi, self.stf[:, 0:W], AF.Copy))
                    t2 = self.v(lambda h, hi=hi, lo=lo:
                                h.scalar_tensor_tensor(lo, hi, -1.0,
                                                       self.stf[:, 0:W],
                                                       ALU.mult, ALU.add),
                                waits=[t1])
                    self._stf_prev = [t2]
                cast_toks += [t1, t2]
            all_src_toks += cast_toks
            pb, bw = self.ps.get(banks[j])
            mm = None
            for k in range(2):
                buf = self.stb if k == 0 else self.stq
                for ic in range(8):
                    mm = self.t(
                        lambda h, r=buf[:, ic * W:(ic + 1) * W], o=pb[0:1, 0:W],
                               st=(k == 0 and ic == 0), sp=(k == 1 and ic == 7):
                        h.matmul(o, self.ones_col_bf[:, 0:1], r, start=st, stop=sp),
                        waits=cast_toks + bw + [self.tok_onesbf])
            dst = (self.sum_row if j == 0 else self.ssq_row)[0:1, col0:col0 + W]
            ev = self.v(lambda h, s=pb[0:1, 0:W], d=dst: h.tensor_copy(d, s),
                        waits=[mm])
            self.ps.consumed(banks[j], ev)
            rows.append(ev)
            prev = [mm]
            self._stats_prev_mm = [mm]
        return rows, all_src_toks

    def row_math_outer(self, col0, W, waits):
        sr = self.sum_row[0:1, col0:col0 + W]
        qr = self.ssq_row[0:1, col0:col0 + W]
        m = self.rt1[0:1, 0:W]
        vv = self.rt2[0:1, 0:W]
        t1 = self.v(lambda h: h.tensor_scalar(m, sr, 1.0 / D, None, ALU.mult),
                    waits=waits)
        t2 = self.v(lambda h: h.tensor_mul(self.rt3[0:1, 0:W], m, m))
        t3 = self.v(lambda h: h.scalar_tensor_tensor(
            vv, qr, 1.0 / D, self.rt3[0:1, 0:W], ALU.mult, ALU.subtract), waits=waits)
        t4 = self.s(lambda h: h.activation(self.rt3[0:1, 0:W], vv, AF.Ln, bias=EPS),
                    waits=[t3])
        t5 = self.s(lambda h: h.activation(self.rA[0:1, 0:W], self.rt3[0:1, 0:W],
                                           AF.Exp, scale=-0.5))
        t6 = self.v(lambda h: h.scalar_tensor_tensor(
            self.rB[0:1, 0:W], m, -1.0, self.rA[0:1, 0:W], ALU.mult, ALU.mult),
            waits=[t5])
        return [t5, t6]

    def row_math_fused_inner(self, W, waits):
        vv = self.rt2[0:1, 0:W]
        m = self.rt1[0:1, 0:W]
        t1 = self.v(lambda h: h.tensor_mul(self.rt3[0:1, 0:W], self.rA[0:1, 0:W],
                                           self.rA[0:1, 0:W]), waits=waits)
        t2 = self.v(lambda h: h.tensor_mul(self.rt3[0:1, 0:W], self.rt3[0:1, 0:W], vv))
        t3 = self.s(lambda h: h.activation(self.rt3[0:1, 0:W], self.rt3[0:1, 0:W],
                                           AF.Ln, bias=EPS), waits=[t2])
        t4 = self.s(lambda h: h.activation(self.rt3[0:1, 0:W], self.rt3[0:1, 0:W],
                                           AF.Exp, scale=-0.5))
        t5 = self.v(lambda h: h.tensor_mul(self.rA2[0:1, 0:W], self.rA[0:1, 0:W],
                                           self.rt3[0:1, 0:W]), waits=[t4])
        t6 = self.v(lambda h: h.scalar_tensor_tensor(
            self.rB2[0:1, 0:W], m, -1.0, self.rA2[0:1, 0:W], ALU.mult, ALU.mult))
        return [t5, t6]

    def rep_rows(self, arow_ap, brow_ap, W, waits, banks=(4, 5)):
        pa, wa = self.ps.get(banks[0])
        pb, wb = self.ps.get(banks[1])
        ta = self.t(lambda h: h.matmul(pa[:, 0:W], self.ones_row[0:1, 0:128],
                                       arow_ap, start=True, stop=True),
                    waits=list(waits) + wa + [self.tok_const])
        tb = self.t(lambda h: h.matmul(pb[:, 0:W], self.ones_row[0:1, 0:128],
                                       brow_ap, start=True, stop=True),
                    waits=list(waits) + wb + [self.tok_const])
        return pa, pb, [ta, tb]

    def normalize_tile(self, x_fn, out_fn, W, Aps, Bps, rep_toks, banks,
                       waits=(), gcol=None, bcol=None, scratch=None):
        toks = []
        psum_cons = []
        for ic in range(8):
            if gcol is None:
                t1 = self.v(lambda h, x=x_fn(ic): h.scalar_tensor_tensor(
                    scratch[:, 0:W], x, 1.0, Aps[:, 0:W], ALU.mult, ALU.mult),
                    waits=list(waits) + rep_toks)
                t2 = self.v(lambda h, o=out_fn(ic): h.scalar_tensor_tensor(
                    o, scratch[:, 0:W], 1.0, Bps[:, 0:W], ALU.mult, ALU.add))
                psum_cons += [t1, t2]
            else:
                t0 = self.v(lambda h, ic=ic: h.tensor_scalar(
                    self.nrmA[:, 0:W], Aps[:, 0:W], gcol[:, ic:ic + 1], None,
                    ALU.mult), waits=list(waits) + rep_toks)
                t0b = self.v(lambda h, ic=ic: h.tensor_scalar(
                    self.nrmB[:, 0:W], Bps[:, 0:W], gcol[:, ic:ic + 1],
                    bcol[:, ic:ic + 1], ALU.mult, ALU.add))
                t1 = self.v(lambda h, x=x_fn(ic): h.scalar_tensor_tensor(
                    scratch[:, 0:W], x, 1.0, self.nrmA[:, 0:W], ALU.mult, ALU.mult))
                t2 = self.v(lambda h, o=out_fn(ic): h.scalar_tensor_tensor(
                    o, scratch[:, 0:W], 1.0, self.nrmB[:, 0:W], ALU.mult, ALU.add))
                psum_cons += [t0, t0b]
            toks.append(t2)
        for b in banks:
            for c in psum_cons:
                self.ps.consumed(b, c)
        return toks

    def _set_fence(self):
        em = self.em
        self.fence = [(s, em.cnt[s]) for s in list(em.cnt.keys())]
        for attr in ("_stats_prev_mm", "_stf_prev", "_zprev", "_ocp_prev",
                     "_g_prev", "_m_prev", "_xn_prev", "_bq_m_prev",
                     "_wem_prev", "_ffn_uprev", "_ffn_vprev", "_gate_prev"):
            setattr(self, attr, list(self.fence))
        self._pfree = None

    def push_phase(self):
        self.phase_stack.append(ExitStack())

    def flush_block(self):
        em = self.em
        if not any(em.q[e] for e in ENGINES):
            return
        with self.nc.Block() as block:
            for eng in ENGINES:
                if em.q[eng]:
                    getattr(block, eng)(self._replayer(eng))
        for e in ENGINES:
            em.q[e] = []

    def pop_phase(self):
        self.flush_block()
        self.phase_stack.pop().close()
        self._set_fence()

    def new_phase(self):
        if self.phase_stack:
            self.pop_phase()
        self.push_phase()

    @property
    def phase_es(self):
        return self.phase_stack[-1] if self.phase_stack else None

    def sbp(self, name, p, f, dt=F32):
        return self.phase_stack[-1].enter_context(
            self.nc.sbuf_tensor(name, [p, f], dt))

    def tap_hbm(self, name, src_ap, waits):
        if name in self.taps:
            self.em.dma("sync", self.taps[name], src_ap, waits=waits, sem="d_tap")

    def load_scalar_pair(self, wl_name, cols, waits):
        em = self.em
        self._lsp = getattr(self, "_lsp", 0) + 1
        row = self.sbp(f"wlr{self._lsp}", 1, 2)
        t0 = em.dma("scalar", row[:], self.inp[wl_name], waits=waits, sem="d_small")
        pb, bw = self.ps.get(2)
        mms = []
        for j in range(2):
            mms.append(self.t(lambda h, j=j: h.matmul(
                pb[:, j * 128:j * 128 + 1], self.ones_row[0:1, 0:128],
                row[0:1, j:j + 1], start=True, stop=True),
                waits=[t0, self.tok_const] + bw))
        e1 = self.s(lambda h: h.activation(cols[:, 0:1], pb[:, 0:1], AF.Copy),
                    waits=mms)
        e2 = self.s(lambda h: h.activation(cols[:, 1:2], pb[:, 128:129], AF.Copy))
        self.ps.consumed(2, e1)
        self.ps.consumed(2, e2)
        return [e1, e2]

    def gen_bias_cols(self, fam, o_dim, dst_cols, waits):
        em = self.em
        nm = f"L0_attn_a_{fam}"
        n_oc = o_dim // 128
        wl = self.sbp(f"bwl_{fam}", 128, 2)
        ctoks = self.load_scalar_pair(f"{nm}_Wl_b", wl, waits)
        bufs = [self.sbp(f"bg{k}_{fam}", 128, n_oc) for k in range(4)]
        ldt = []
        for k, src in enumerate((self.inp[f"{nm}_sub_b"][0, :],
                                 self.inp[f"{nm}_sub_b"][1, :],
                                 self.inp[f"{nm}_W_b"],
                                 self.inp[f"{nm}_B_b"])):
            for oc in range(n_oc):
                ldt.append(em.dma("scalar", bufs[k][:, oc:oc + 1],
                                  src[oc * 128:(oc + 1) * 128],
                                  waits=waits, sem="d_small"))
        ls = em.stream_tok("d_small")
        t1 = self.v(lambda h: h.tensor_scalar(dst_cols[:, 0:n_oc], bufs[0][:, 0:n_oc],
                                              wl[:, 0:1], None, ALU.mult),
                    waits=[ls] + ctoks)
        t2 = self.v(lambda h: h.scalar_tensor_tensor(
            dst_cols[:, 0:n_oc], bufs[1][:, 0:n_oc], wl[:, 1:2],
            dst_cols[:, 0:n_oc], ALU.mult, ALU.add))
        t3 = self.s(lambda h: h.activation(dst_cols[:, 0:n_oc], dst_cols[:, 0:n_oc],
                                           AF.Tanh), waits=[t2])
        t4 = self.v(lambda h: h.tensor_mul(dst_cols[:, 0:n_oc], dst_cols[:, 0:n_oc],
                                           bufs[2][:, 0:n_oc]), waits=[t3])
        t5 = self.v(lambda h: h.tensor_add(dst_cols[:, 0:n_oc], dst_cols[:, 0:n_oc],
                                           bufs[3][:, 0:n_oc]))
        return [t5]

    def gen_bias_row(self, fam, lo, hi, dst_row, waits):
        em = self.em
        nm = f"L0_attn_a_{fam}"
        W = hi - lo
        wl = self.sbp(f"rwl_{fam}", 128, 2)
        ctoks = self.load_scalar_pair(f"{nm}_Wl_b", wl, waits)
        bufs = [self.sbp(f"rg{k}_{fam}", 1, W) for k in range(4)]
        ldt = []
        for k, src in enumerate((self.inp[f"{nm}_sub_b"][0, lo:hi],
                                 self.inp[f"{nm}_sub_b"][1, lo:hi],
                                 self.inp[f"{nm}_W_b"][lo:hi],
                                 self.inp[f"{nm}_B_b"][lo:hi])):
            ldt.append(em.dma("scalar", bufs[k][0:1, :], src, waits=waits,
                              sem="d_small"))
        ls = em.stream_tok("d_small")
        t1 = self.v(lambda h: h.tensor_scalar(dst_row[0:1, 0:W], bufs[0][0:1, :],
                                              wl[0:1, 0:1], None, ALU.mult),
                    waits=[ls] + ctoks)
        t2 = self.v(lambda h: h.scalar_tensor_tensor(
            dst_row[0:1, 0:W], bufs[1][0:1, :], wl[0:1, 1:2], dst_row[0:1, 0:W],
            ALU.mult, ALU.add))
        t3 = self.s(lambda h: h.activation(dst_row[0:1, 0:W], dst_row[0:1, 0:W],
                                           AF.Tanh), waits=[t2])
        t4 = self.v(lambda h: h.tensor_mul(dst_row[0:1, 0:W], dst_row[0:1, 0:W],
                                           bufs[2][0:1, :]), waits=[t3])
        t5 = self.v(lambda h: h.tensor_add(dst_row[0:1, 0:W], dst_row[0:1, 0:W],
                                           bufs[3][0:1, :]))
        return [t5]

    def gen_weight_fam(self, fam, o_dim, i_dim, dst_sb, waits=()):
        em = self.em
        nm = f"L0_attn_a_{fam}"
        wl = self.sbp(f"gwl_{fam}", 128, 2)
        ctoks = self.load_scalar_pair(f"{nm}_Wl_w", wl, waits)
        GW = 512
        gw = [[self.sbp(f"gw{j}{k}_{fam}", 128, GW) for k in range(4)]
              for j in range(2)]
        gtmp = [self.sbp(f"gt{j}_{fam}", 128, GW) for j in range(2)]
        gout = [self.sbp(f"go{j}_{fam}", 128, GW, BF16) for j in range(2)]
        gfree = {0: list(waits), 1: list(waits)}
        sub = self.inp[f"{nm}_sub_w"]
        Wt = self.inp[f"{nm}_W_w"]
        Bt = self.inp[f"{nm}_B_w"]
        n_ot = o_dim // 128
        n_ig = i_dim // GW
        done = []
        pc = 0
        for ot in range(n_ot):
            for ig in range(n_ig):
                j = pc % 2
                pc += 1
                b0, b1, bW, bB = gw[j]
                osl = slice(ot * 128, (ot + 1) * 128)
                isl = slice(ig * GW, (ig + 1) * GW)
                em.dma("gpsimd", b0[:, :], sub[0, osl, isl], waits=gfree[j],
                       sem=f"d_gen{j}")
                em.dma("gpsimd", b1[:, :], sub[1, osl, isl], sem=f"d_gen{j}")
                em.dma("gpsimd", bW[:, :], Wt[osl, isl], sem=f"d_gen{j}")
                em.dma("gpsimd", bB[:, :], Bt[osl, isl], sem=f"d_gen{j}")
                ldt = em.stream_tok(f"d_gen{j}")
                t1 = self.v(lambda h, b0=b0, j=j: h.tensor_scalar(
                    gtmp[j][:, :], b0[:, :], wl[:, 0:1], None, ALU.mult),
                    waits=[ldt] + ctoks)
                t2 = self.v(lambda h, b1=b1, j=j: h.scalar_tensor_tensor(
                    gtmp[j][:, :], b1[:, :], wl[:, 1:2], gtmp[j][:, :],
                    ALU.mult, ALU.add))
                t3 = self.s(lambda h, j=j: h.activation(gtmp[j][:, :], gtmp[j][:, :],
                                                        AF.Tanh), waits=[t2])
                t4 = self.v(lambda h, bW=bW, j=j: h.tensor_mul(
                    gtmp[j][:, :], gtmp[j][:, :], bW[:, :]), waits=[t3])
                t5 = self.v(lambda h, bB=bB, j=j: h.tensor_add(
                    gout[j][:, :], gtmp[j][:, :], bB[:, :]), waits=gfree[j])
                gfree[j] = [t5]
                for icl in range(GW // 128):
                    ic = ig * (GW // 128) + icl
                    pbf, bw = self.ps.get("bf")
                    ttr = self.t(lambda h, s=gout[j][:, icl * 128:(icl + 1) * 128],
                                        p=pbf[:, 0:128]:
                                 h.transpose(p, s, self.ident_bf[:]),
                                 waits=[t5, self.tok_const] + bw)
                    dst = dst_sb[:, ic * o_dim + ot * 128: ic * o_dim + ot * 128 + 128]
                    tev = self.v(lambda h, p=pbf[:, 0:128], d=dst: h.tensor_copy(d, p),
                                 waits=[ttr])
                    self.ps.consumed("bf", tev)
                    gfree[j].append(tev)
                    done.append(tev)
        return done

    def load_cols(self, src_1d, n_oc, dst_cols, waits):
        toks = []
        for oc in range(n_oc):
            toks.append(self.em.dma("scalar", dst_cols[:, oc:oc + 1],
                                    src_1d[oc * 128:(oc + 1) * 128],
                                    waits=waits, sem="d_small"))
        return toks

    def attn_alloc(self, G, sfx):
        T = self.T
        em = self.em
        self.atG = G
        self.at_dmask = self.sbp(f"at_dmask{sfx}", 128, 640)
        em.dma("sync", self.at_dmask[:], self.inp["c_dmask"],
               waits=self.fence, sem="d_atc")
        self.at_ctok = em.stream_tok("d_atc")
        self.at_S = [self.sbp(f"at_S{i}{sfx}", 128, T)
                     for i in range(getattr(self, "at_sbufs", 2))]
        npbuf = 2 * G if getattr(self, "at_sbufs", 2) >= 4 else G
        self.at_P = [self.sbp(f"at_P{i}{sfx}", 128, T, BF16) for i in range(npbuf)]
        self.at_PT = [self.sbp(f"at_PT{i}{sfx}", 128, G * 128, BF16)
                      for i in range(2)]
        self.at_mrow = self.sbp(f"at_mrow{sfx}", 128, 2)
        self.at_negm = self.sbp(f"at_negm{sfx}", 128, G)
        self.at_zparts = self.sbp(f"at_zparts{sfx}", 128, G * 8)
        self.at_zscr = self.sbp(f"at_zscr{sfx}", 128, 8)
        self.at_zsum = self.sbp(f"at_zsum{sfx}", 128, 1)
        self.at_zc = self.sbp(f"at_zc{sfx}", 128, 2 * G)
        self.at_zrow = self.sbp(f"at_zrow{sfx}", 1, 2 * G * 128)
        self.at_zrep = self.sbp(f"at_zrep{sfx}", 64, G * 128)
        self.at_sfree = {(p, w): list(self.fence)
                         for p in range(len(self.at_S)) for w in range(T // 512)}
        _, bfw0 = self.ps.get("bf")
        _, bfw1 = self.ps.get("bf2")
        self.at_bfw0 = bfw0 + bfw1 + list(self.fence)
        self.at_bfseq = []
        self.at_pfree = {i: list(self.fence) for i in range(len(self.at_P))}
        self.at_ptfree = {i: list(self.fence) for i in range(2)}
        self.at_negmfree = {i: list(self.fence) for i in range(G)}
        self.at_zfree = list(self.fence)
        self.at_zcfree = {0: list(self.fence), 1: list(self.fence)}
        self.at_zrowfree = {0: list(self.fence), 1: list(self.fence)}
        self.at_zsumfree = list(self.fence)

    def attn_core2(self, q_all, k_all, v_tok, scale, dst_fn, done_fn,
                   tq, tk, tv):
        em = self.em
        T = self.T
        G = self.atG
        NT = T // 128
        ng_groups = NT // G
        qk_cons = []
        v_cons = []

        def emit_S(g, tiles=None, ttrs=None):
            ttrs = {} if ttrs is None else ttrs
            for tl in (range(G) if tiles is None else tiles):
                t = g * G + tl
                sp = ((g % 2) * 2 + tl % 2) % len(self.at_S)
                S = self.at_S[sp]
                kspan = (t + 1) * 128
                nwin = (kspan + 511) // 512
                for w in range(nwin):
                    N = min(512, kspan - w * 512)
                    bank = w % 2
                    spb, sbw = self.ps.get(bank)
                    mm = self.t(lambda h, t=t, w=w, N=N, spb=spb: h.matmul(
                        spb[:, 0:N], q_all[0:64, t * 128:(t + 1) * 128],
                        k_all[0:64, w * 512:w * 512 + N], start=True, stop=True),
                        waits=[tq, tk] + sbw)
                    qk_cons.append(mm)
                    last = (w == nwin - 1)
                    in1 = (self.at_dmask[:, 640 - N:640] if last
                           else self.at_dmask[:, 0:N])
                    ttr = self.v(lambda h, w=w, N=N, in1=in1, spb=spb, S=S:
                                 h.tensor_tensor(
                                     S[:, w * 512:w * 512 + N],
                                     spb[:, 0:N], in1, ALU.add),
                                 waits=[mm, self.at_ctok] + self.at_sfree[(sp, w)])
                    self.at_sfree[(sp, w)] = []
                    self.ps.consumed(bank, ttr)
                    if w == 0:
                        self.v(lambda h, w=w, N=N, S=S: h.reduce_max(
                            self.at_mrow[:, 0:1],
                            S[:, w * 512:w * 512 + N], axis=AX.X))
                    else:
                        self.v(lambda h, w=w, N=N, S=S: h.reduce_max(
                            self.at_mrow[:, 1:2],
                            S[:, w * 512:w * 512 + N], axis=AX.X))
                        self.v(lambda h: h.tensor_max(
                            self.at_mrow[:, 0:1], self.at_mrow[:, 0:1],
                            self.at_mrow[:, 1:2]))
                    ttrs[(tl, w)] = ttr
                ngt = self.v(lambda h, tl=tl: h.tensor_scalar(
                    self.at_negm[:, tl:tl + 1], self.at_mrow[:, 0:1],
                    -scale, None, ALU.mult), waits=self.at_negmfree[tl])
                self.at_negmfree[tl] = []
                ttrs[("ng", tl)] = ngt
            return ttrs

        def emit_E(g, ttrs, tiles=None, exp_toks=None):
            par = g % 2
            exp_toks = {} if exp_toks is None else exp_toks
            for tl in (range(G) if tiles is None else tiles):
                t = g * G + tl
                sp = ((g % 2) * 2 + tl % 2) % len(self.at_S)
                S = self.at_S[sp]
                pslot = ((g % 2) * G + tl) % len(self.at_P)
                kspan = (t + 1) * 128
                nwin = (kspan + 511) // 512
                for w in range(nwin):
                    N = min(512, kspan - w * 512)
                    pe = self.s(lambda h, tl=tl, w=w, N=N, S=S, pslot=pslot: h.activation(
                        self.at_P[pslot][:, w * 512:w * 512 + N],
                        S[:, w * 512:w * 512 + N],
                        AF.Exp, bias=self.at_negm[:, tl:tl + 1], scale=scale,
                        accum_out=self.at_zparts[:, tl * 8 + w:tl * 8 + w + 1]),
                        waits=[ttrs[(tl, w)], ttrs[("ng", tl)]]
                              + (self.at_pfree[pslot] if w == 0 else []))
                    if w == 0:
                        self.at_pfree[pslot] = []
                    self.at_sfree[(sp, w)].append(pe)
                    self.at_negmfree[tl].append(pe)
                    exp_toks[(tl, w)] = pe
                zs = self.s(lambda h, tl=tl, nw=nwin: h.activation(
                    self.at_zscr[:, 0:nw], self.at_zparts[:, tl * 8:tl * 8 + nw],
                    AF.Copy, accum_out=self.at_zsum[:, 0:1]),
                    waits=[exp_toks[(tl, w2)] for w2 in range(nwin)]
                          + self.at_zsumfree)
                rc = self.v(lambda h, tl=tl, par=par: h.reciprocal(
                    self.at_zc[:, par * G + tl:par * G + tl + 1],
                    self.at_zsum[:, 0:1]),
                    waits=[zs] + self.at_zcfree[par])
                self.at_zsumfree = [rc]
                self.at_zcfree[par] = []
                pz, pzw = self.ps.get(3)
                ztr = self.t(lambda h, tl=tl, par=par, pz=pz: h.transpose(
                    pz[0:1, 0:128],
                    self.at_zc[:, par * G + tl:par * G + tl + 1],
                    self.ident_f[:]),
                    waits=[rc, self.tok_const] + pzw)
                self.at_zcfree[par].append(ztr)
                ze = self.s(lambda h, tl=tl, par=par, pz=pz: h.activation(
                    self.at_zrow[0:1, (par * G + tl) * 128:(par * G + tl + 1) * 128],
                    pz[0:1, 0:128], AF.Copy),
                    waits=[ztr] + self.at_zrowfree[par])
                self.at_zrowfree[par] = []
                self.ps.consumed(3, ze)
                exp_toks[("z", tl)] = ze
            return exp_toks

        def emit_BC(g, exp_toks):
            par = g % 2
            nkc = g * G + G
            po, pow_ = self.ps.get(2)
            mmo = None
            for kc in range(nkc):
                j = kc % 2
                pt = self.at_PT[j]
                pt_toks = []
                for tl in range(G):
                    t = g * G + tl
                    pslot = ((g % 2) * G + tl) % len(self.at_P)
                    if kc <= t:
                        seq = len(self.at_bfseq)
                        bname = "bf" if seq % 2 == 0 else "bf2"
                        pbf = self.ps.t[bname]
                        stag = (self.at_bfw0 if seq < 2
                                else [self.at_bfseq[seq - 2]])
                        ttr2 = self.t(lambda h, kc=kc, pslot=pslot, pbf=pbf:
                                      h.transpose(
                                          pbf[:, 0:128],
                                          self.at_P[pslot][:, kc * 128:(kc + 1) * 128],
                                          self.ident_bf[:]),
                                      waits=[exp_toks[(tl, kc // 4)],
                                             self.tok_const] + stag)
                        self.at_pfree[pslot].append(ttr2)
                        if seq % 2 == 0:
                            ev = self.s(lambda h, tl=tl, pt=pt, pbf=pbf:
                                        h.activation(pt[:, tl * 128:(tl + 1) * 128],
                                                     pbf[:, 0:128], AF.Copy),
                                        waits=[ttr2] + self.at_ptfree[j])
                        else:
                            ev = self.v(lambda h, tl=tl, pt=pt, pbf=pbf:
                                        h.tensor_copy(pt[:, tl * 128:(tl + 1) * 128],
                                                      pbf[:, 0:128]),
                                        waits=[ttr2] + self.at_ptfree[j])
                        self.at_bfseq.append(ev)
                    else:
                        ev = self.v(lambda h, tl=tl, pt=pt: h.memset(
                            pt[:, tl * 128:(tl + 1) * 128], 0.0),
                            waits=self.at_ptfree[j])
                    self.at_ptfree[j] = []
                    pt_toks.append(ev)
                mmo = self.t(lambda h, kc=kc, pt=pt, po=po,
                                    st=(kc == 0), sp=(kc == nkc - 1):
                             h.matmul(po[0:64, 0:G * 128],
                                      v_tok[:, kc * 64:(kc + 1) * 64],
                                      pt[:, 0:G * 128], start=st, stop=sp),
                             waits=pt_toks + [tv] + (pow_ if kc == 0 else []))
                v_cons.append(mmo)
                self.at_ptfree[j] = [mmo]
            pzr, pzrw = self.ps.get(4)
            zm = self.t(lambda h, par=par, pzr=pzr: h.matmul(
                pzr[0:64, 0:G * 128], self.ones_row[0:1, 0:64],
                self.at_zrow[0:1, par * G * 128:(par + 1) * G * 128],
                start=True, stop=True),
                waits=[exp_toks[("z", tl)] for tl in range(G)]
                      + [self.tok_const] + pzrw)
            self.at_zrowfree[par].append(zm)
            ze2 = self.s(lambda h, pzr=pzr: h.activation(
                self.at_zrep[:, 0:G * 128], pzr[0:64, 0:G * 128], AF.Copy),
                waits=[zm] + self.at_zfree)
            self.ps.consumed(4, ze2)
            dap, dwaits = dst_fn(g)
            oe = self.v(lambda h, po=po, dap=dap: h.tensor_mul(
                dap, po[0:64, 0:G * 128], self.at_zrep[:, 0:G * 128]),
                waits=[ze2, mmo] + list(dwaits))
            self.ps.consumed(2, oe)
            self.at_zfree = [oe]
            done_fn(g, oe)

        if len(self.at_S) >= G:
            def emit_SE(g):
                ttrs = {}
                exps = {}
                for pair in (list(range(G))[:2], list(range(G))[2:]):
                    emit_S(g, tiles=pair, ttrs=ttrs)
                    emit_E(g, ttrs, tiles=pair, exp_toks=exps)
                return exps
            exps = {0: emit_SE(0)}
            for g in range(ng_groups):
                if g + 1 < ng_groups:
                    exps[g + 1] = emit_SE(g + 1)
                emit_BC(g, exps[g])
        else:
            for g in range(ng_groups):
                ttrs = {}
                exps = {}
                for tl in range(G):
                    emit_S(g, tiles=[tl], ttrs=ttrs)
                    emit_E(g, ttrs, tiles=[tl], exp_toks=exps)
                emit_BC(g, exps)
        return qk_cons, v_cons

    def p_attn_a(self):
        em = self.em
        TT = self.TT
        T = self.T

        self.new_phase()
        wcaT = self.sbp("wcaT", 128, 8 * I3, BF16)
        self.nrmA = self.sbp("nrmA_a", 128, 512)
        self.nrmB = self.sbp("nrmB_a", 128, 512)
        self.rA2 = self.sbp("rA2", 1, 512)
        self.rB2 = self.sbp("rB2", 1, 512)
        bca_cols = self.sbp("bca_cols", 128, 16)
        bvrow = self.sbp("bvrow", 1, 1024)
        bvrep = self.sbp("bvrep", 128, 1024)
        ln1g = self.sbp("ln1g", 128, 8)
        ln1b = self.sbp("ln1b", 128, 8)
        tok_w = self.gen_weight_fam("ca", I3, D, wcaT, waits=self.fence)
        tok_bc = self.gen_bias_cols("ca", 2048, bca_cols, self.fence)
        tok_bv = self.gen_bias_row("ca", 2048, 3072, bvrow, self.fence)
        rep_toks = []
        for half in range(2):
            pb, bw = self.ps.get(2)
            tm = self.t(lambda h, half=half: h.matmul(
                pb[:, 0:512], self.ones_row[0:1, 0:128],
                bvrow[0:1, half * 512:(half + 1) * 512], start=True, stop=True),
                waits=tok_bv + bw + [self.tok_const])
            te = self.s(lambda h, half=half: h.activation(
                bvrep[:, half * 512:(half + 1) * 512], pb[:, 0:512], AF.Copy),
                waits=[tm])
            self.ps.consumed(2, te)
            rep_toks.append(te)
        tok_g = self.load_cols(self.inp["L0_attn_a_ln1_g"], 8, ln1g, self.fence)
        tok_b = self.load_cols(self.inp["L0_attn_a_ln1_b"], 8, ln1b, self.fence)
        ln1_tok = em.stream_tok("d_small")
        w_done = tok_w[-1]

        xa = [self.sbp("a1x0", 128, 8 * 512)]
        h1 = [self.sbp(f"a1h{i}", 128, 8 * 512, BF16) for i in range(2)]
        nsc = self.sbp("a1nsc", 128, 512)
        qst = [self.sbp(f"a1q{i}", 128, 512, BF16) for i in range(4)]
        vst = [self.sbp(f"a1v{i}", 128, 1024, BF16) for i in range(2)]
        xfree = {0: [], 1: []}
        hfree = {0: [], 1: []}
        qfree = {i: [] for i in range(4)}
        vfree = {0: [], 1: []}
        prev_rep = []
        for tt in range(TT):
            b = tt % 2
            x = xa[0]
            hb = h1[b]
            tok_x = em.dma("sync", x.rearrange("p (c t) -> p c t", c=8),
                           self.x_hbm.rearrange("(c p) t -> p c t", p=128)[
                               :, :, tt * 512:(tt + 1) * 512],
                           waits=xfree[0] + xfree[1] + self.tok_x_ready,
                           sem="d_a1x0")
            rtoks = self.row_math_outer(tt * 512, 512, [self.tok_stats_ready] + prev_rep)
            rtoks2 = self.row_math_fused_inner(512, rtoks)
            Aps, Bps, rep, = self.rep_rows(self.rA2[0:1, 0:512], self.rB2[0:1, 0:512],
                                           512, rtoks2, banks=(4, 5))
            prev_rep = rep
            htoks = self.normalize_tile(
                lambda ic, x=x: x[:, ic * 512:(ic + 1) * 512],
                lambda ic, hb=hb: hb[:, ic * 512:(ic + 1) * 512],
                512, Aps, Bps, rep, (4, 5),
                waits=[tok_x, ln1_tok] + hfree[b], gcol=ln1g, bcol=ln1b, scratch=nsc)
            xfree[b] = []
            hfree[b] = []
            for oc in range(16):
                bank = oc % 2
                pb, bw = self.ps.get(bank)
                mm = None
                for ic in range(8):
                    mm = self.t(lambda h, ic=ic, oc=oc, hb=hb, pb=pb,
                                       st=(ic == 0), sp=(ic == 7):
                                h.matmul(pb[:, 0:512],
                                         wcaT[:, ic * I3 + oc * 128: ic * I3 + oc * 128 + 128],
                                         hb[:, ic * 512:(ic + 1) * 512],
                                         start=st, stop=sp),
                                waits=htoks + bw + [w_done])
                q = qst[oc % 4]
                ev = self.s(lambda h, pb=pb, q=q, oc=oc: h.activation(
                    q[:], pb[:, 0:512], AF.Identity, bias=bca_cols[:, oc:oc + 1]),
                    waits=[mm] + qfree[oc % 4] + tok_bc)
                self.ps.consumed(bank, ev)
                dst = (self.q_hbm if oc < 8 else self.k_hbm)
                ocr = oc if oc < 8 else oc - 8
                td = em.dma("sync", dst[ocr * 128:(ocr + 1) * 128,
                                        tt * 512:(tt + 1) * 512], q[:],
                            waits=[ev], sem=f"d_qk{oc % 4}")
                qfree[oc % 4] = [td]
                hfree[b].append(mm)
            for ts4 in range(4):
                vb = ts4 % 2
                vs = vst[vb]
                evs = []
                for half in range(2):
                    bank = 2 + half
                    pb, bw = self.ps.get(bank)
                    mm = None
                    for ic in range(8):
                        mm = self.t(lambda h, ic=ic, ts4=ts4, half=half, hb=hb, pb=pb,
                                           st=(ic == 0), sp=(ic == 7):
                                    h.matmul(pb[:, 0:512],
                                             hb[:, ic * 512 + ts4 * 128: ic * 512 + ts4 * 128 + 128],
                                             wcaT[:, ic * I3 + 2048 + half * 512:
                                                  ic * I3 + 2048 + (half + 1) * 512],
                                             start=st, stop=sp),
                                    waits=htoks + bw + [w_done])
                    ev = self.v(lambda h, pb=pb, vs=vs, half=half: h.scalar_tensor_tensor(
                        vs[:, half * 512:(half + 1) * 512], pb[:, 0:512], 1.0,
                        bvrep[:, half * 512:(half + 1) * 512], ALU.mult, ALU.add),
                        waits=[mm] + vfree[vb] + rep_toks)
                    self.ps.consumed(bank, ev)
                    evs.append(ev)
                    hfree[b].append(mm)
                td = em.dma("sync", self.v_hbm[tt * 512 + ts4 * 128:
                                               tt * 512 + (ts4 + 1) * 128, :], vs[:],
                            waits=evs, sem=f"d_v{vb}")
                vfree[vb] = [td]
            xfree[b] = htoks
        tok_qk = [em.stream_tok(f"d_qk{i}") for i in range(4)]
        tok_v = [em.stream_tok(f"d_v{i}") for i in range(2)]
        self.tap_hbm("tap_q", self.q_hbm, tok_qk)
        self.tap_hbm("tap_k", self.k_hbm, tok_qk)
        self.tap_hbm("tap_v", self.v_hbm, tok_v)

        self.new_phase()
        G = 4
        self.at_sbufs = 4
        self.attn_alloc(G, "a")
        qh = [self.sbp("qh0", 64, T, BF16)]
        kh = [self.sbp("kh0", 64, T, BF16)]
        vth = [self.sbp("vth0", 128, (T // 128) * 64, BF16)]
        ost = [self.sbp(f"ost{i}", 64, G * 128, BF16) for i in range(2)]
        qfree2 = {0: list(self.fence), 1: list(self.fence)}
        vfree2 = {0: list(self.fence), 1: list(self.fence)}
        ofree = {i: list(self.fence) for i in range(2)}
        for hd in range(H):
            b = 0
            tq = em.dma("sync", qh[b][:], self.q_hbm[hd * 64:(hd + 1) * 64, :],
                        waits=qfree2[b] + tok_qk, sem=f"d_at_q{b}")
            tk = em.dma("sync", kh[b][:], self.k_hbm[hd * 64:(hd + 1) * 64, :],
                        waits=qfree2[b] + tok_qk, sem=f"d_at_q{b}")
            tv = em.dma("gpsimd",
                        vth[b].rearrange("p (k c) -> p k c", c=64),
                        self.v_hbm.rearrange("(k p) f -> p k f", p=128)[
                            :, :, hd * 64:(hd + 1) * 64],
                        waits=vfree2[b] + tok_v, sem=f"d_at_v{b}")
            qfree2[b] = []
            vfree2[b] = []

            def dst_fn(g, hd=hd):
                oi = g % 2
                w = ofree[oi]
                ofree[oi] = []
                return ost[oi][:], w

            def done_fn(g, oe, hd=hd):
                oi = g % 2
                td = em.dma("sync", self.o_hbm[hd * 64:(hd + 1) * 64,
                                               g * G * 128:(g + 1) * G * 128],
                            ost[oi][:], waits=[oe], sem=f"d_o{oi}")
                ofree[oi] = [td]

            qk_cons, v_cons = self.attn_core2(
                qh[b], kh[b], vth[b], SCALE_A, dst_fn, done_fn, tq, tk, tv)
            qfree2[b] += qk_cons
            vfree2[b] += v_cons
        tok_o = [em.stream_tok(f"d_o{i}") for i in range(2)]
        self.tap_hbm("tap_o", self.o_hbm, tok_o)

        self.new_phase()
        wcpT = self.sbp("wcpT", 128, 8 * D, BF16)
        bcp_cols = self.sbp("bcp_cols", 128, 8)
        tok_wcp = self.gen_weight_fam("cp", D, D, wcpT, waits=self.fence)
        tok_bcp = self.gen_bias_cols("cp", D, bcp_cols, self.fence)
        wcp_done = tok_wcp[-1]
        self.stb = self.sbp("c1stb", 128, 8 * 512, BF16)
        self.stq = self.sbp("c1stq", 128, 8 * 512, BF16)
        self.stf = self.sbp("c1stf", 128, 512)
        ot_b = [self.sbp(f"c1o{i}", 128, 8 * 512, BF16) for i in range(2)]
        x_b = [self.sbp(f"c1x{i}", 128, 8 * 512) for i in range(2)]
        ocp_b = self.sbp("c1ocp", 128, 512)
        r_b = [self.sbp(f"c1r{i}", 128, 8 * 512) for i in range(2)]
        nsc1 = self.sbp("c1nsc", 128, 512)
        ofree1 = {0: [], 1: []}
        xfree1 = {0: [], 1: []}
        rfree1 = {0: [], 1: []}
        prev_rep = []
        for tt in range(TT):
            b = tt % 2
            ob = ot_b[b]
            xb = x_b[b]
            rb = r_b[b]
            tok_o_ld = em.dma("sync", ob.rearrange("p (c t) -> p c t", c=8),
                              self.o_hbm.rearrange("(c p) t -> p c t", p=128)[
                                  :, :, tt * 512:(tt + 1) * 512],
                              waits=ofree1[b] + tok_o + self.fence, sem=f"d_c1o{b}")
            tok_x_ld = em.dma("sync", xb.rearrange("p (c t) -> p c t", c=8),
                              self.x_hbm.rearrange("(c p) t -> p c t", p=128)[
                                  :, :, tt * 512:(tt + 1) * 512],
                              waits=xfree1[b] + self.tok_x_ready + self.fence,
                              sem=f"d_c1x{b}")
            rtoks = self.row_math_outer(tt * 512, 512, [self.tok_stats_ready] + prev_rep)
            Aps, Bps, rep = self.rep_rows(self.rA[0:1, 0:512], self.rB[0:1, 0:512],
                                          512, rtoks, banks=(4, 5))
            prev_rep = rep
            rstat_toks = []
            for oc in range(8):
                bank = oc % 2
                pb, bw = self.ps.get(bank)
                mm = None
                for ic in range(8):
                    mm = self.t(lambda h, ic=ic, oc=oc, ob=ob, pb=pb,
                                       st=(ic == 0), sp=(ic == 7):
                                h.matmul(pb[:, 0:512],
                                         wcpT[:, ic * D + oc * 128: ic * D + oc * 128 + 128],
                                         ob[:, ic * 512:(ic + 1) * 512],
                                         start=st, stop=sp),
                                waits=[tok_o_ld, wcp_done] + bw)
                ev = self.s(lambda h, pb=pb, oc=oc: h.activation(
                    ocp_b[:], pb[:, 0:512], AF.Identity, bias=bcp_cols[:, oc:oc + 1]),
                    waits=[mm] + tok_bcp + getattr(self, "_ocp_prev", []))
                self.ps.consumed(bank, ev)
                u1 = self.v(lambda h, oc=oc, xb=xb: h.scalar_tensor_tensor(
                    nsc1[:, 0:512], xb[:, oc * 512:(oc + 1) * 512], 1.0,
                    Aps[:, 0:512], ALU.mult, ALU.mult),
                    waits=[tok_x_ld] + rep + rfree1[b])
                u2 = self.v(lambda h: h.scalar_tensor_tensor(
                    nsc1[:, 0:512], nsc1[:, 0:512], 1.0, Bps[:, 0:512],
                    ALU.mult, ALU.add))
                u3 = self.v(lambda h, oc=oc, rb=rb: h.tensor_add(
                    rb[:, oc * 512:(oc + 1) * 512], nsc1[:, 0:512], ocp_b[:]),
                    waits=[ev])
                self._ocp_prev = [u3]
                u4 = self.v(lambda h, oc=oc, xb=xb, rb=rb: h.tensor_add(
                    xb[:, oc * 512:(oc + 1) * 512], xb[:, oc * 512:(oc + 1) * 512],
                    rb[:, oc * 512:(oc + 1) * 512]))
                self.ps.consumed(4, u2)
                self.ps.consumed(5, u2)
                rstat_toks.append(u4)
            td_r = em.dma("sync", self.r_hbm.rearrange("(c p) t -> p c t", p=128)[
                :, :, tt * 512:(tt + 1) * 512],
                rb.rearrange("p (c t) -> p c t", c=8), waits=rstat_toks, sem=f"d_r{b}")
            td_x2 = em.dma("sync", self.x2_hbm.rearrange("(c p) t -> p c t", p=128)[
                :, :, tt * 512:(tt + 1) * 512],
                xb.rearrange("p (c t) -> p c t", c=8), waits=rstat_toks,
                sem=f"d_x2{b}")
            rows, casts = self.stats_tile(
                lambda ic, rb=rb: rb[:, ic * 512:(ic + 1) * 512], 512, tt * 512,
                rstat_toks)
            ofree1[b] = rstat_toks
            xfree1[b] = rstat_toks + [td_x2]
            rfree1[b] = casts + [td_r]
        tok_r = [em.stream_tok("d_r0"), em.stream_tok("d_r1")]
        tok_x2 = [em.stream_tok("d_x20"), em.stream_tok("d_x21")]
        self.tap_hbm("tap_r", self.r_hbm, tok_r)

        self.new_phase()
        wcfT = self.sbp("wcfT", 128, 8 * FI, BF16)
        self.nrmA = self.sbp("nrmA_5", 128, 512)
        self.nrmB = self.sbp("nrmB_5", 128, 512)
        bcf_cols = self.sbp("bcf_cols", 128, 32)
        ln2g = self.sbp("ln2g", 128, 8)
        ln2b = self.sbp("ln2b", 128, 8)
        tok_wcf = self.gen_weight_fam("cf", FI, D, wcfT, waits=self.fence)
        tok_bcf = self.gen_bias_cols("cf", FI, bcf_cols, self.fence)
        tok_g2 = self.load_cols(self.inp["L0_attn_a_ln2_g"], 8, ln2g, self.fence)
        tok_b2 = self.load_cols(self.inp["L0_attn_a_ln2_b"], 8, ln2b, self.fence)
        ln2_tok = em.stream_tok("d_small")
        wcf_done = tok_wcf[-1]
        r5 = [self.sbp(f"a5r{i}", 128, 8 * 512) for i in range(2)]
        h2 = [self.sbp(f"a5h{i}", 128, 8 * 512, BF16) for i in range(2)]
        nsc5 = self.sbp("a5nsc", 128, 512)
        gsc1 = self.sbp("a5g1", 128, 512)
        gsc2 = self.sbp("a5g2", 128, 512)
        gst = [self.sbp(f"a5gs{i}", 128, 512, BF16) for i in range(4)]
        rfree5 = {0: [], 1: []}
        hfree5 = {0: [], 1: []}
        gfree5 = {i: [] for i in range(4)}
        prev_rep = []
        for tt in range(TT):
            b = tt % 2
            rb = r5[b]
            hb = h2[b]
            tok_r_ld = em.dma("sync", rb.rearrange("p (c t) -> p c t", c=8),
                              self.r_hbm.rearrange("(c p) t -> p c t", p=128)[
                                  :, :, tt * 512:(tt + 1) * 512],
                              waits=rfree5[b] + tok_r + self.fence, sem=f"d_a5r{b}")
            rtoks = self.row_math_outer(tt * 512, 512, [self.tok_stats_ready] + prev_rep)
            Aps, Bps, rep = self.rep_rows(self.rA[0:1, 0:512], self.rB[0:1, 0:512],
                                          512, rtoks, banks=(4, 5))
            prev_rep = rep
            htoks = self.normalize_tile(
                lambda ic, rb=rb: rb[:, ic * 512:(ic + 1) * 512],
                lambda ic, hb=hb: hb[:, ic * 512:(ic + 1) * 512],
                512, Aps, Bps, rep, (4, 5),
                waits=[tok_r_ld, ln2_tok] + hfree5[b], gcol=ln2g, bcol=ln2b,
                scratch=nsc5)
            rfree5[b] = htoks
            hfree5[b] = []
            for oc in range(32):
                bank = oc % 2
                pb, bw = self.ps.get(bank)
                mm = None
                for ic in range(8):
                    mm = self.t(lambda h, ic=ic, oc=oc, hb=hb, pb=pb,
                                       st=(ic == 0), sp=(ic == 7):
                                h.matmul(pb[:, 0:512],
                                         wcfT[:, ic * FI + oc * 128: ic * FI + oc * 128 + 128],
                                         hb[:, ic * 512:(ic + 1) * 512],
                                         start=st, stop=sp),
                                waits=htoks + bw + [wcf_done])
                hfree5[b].append(mm)
                e0 = self.s(lambda h, pb=pb, oc=oc: h.activation(
                    gsc1[:], pb[:, 0:512], AF.Identity, bias=bcf_cols[:, oc:oc + 1]),
                    waits=[mm] + tok_bcf + getattr(self, "_g_prev", []))
                self.ps.consumed(bank, e0)
                e1 = self.s(lambda h: h.activation(gsc2[:], gsc1[:], AF.Square))
                e2 = self.v(lambda h: h.tensor_mul(gsc2[:], gsc2[:], gsc1[:]),
                            waits=[e1])
                e3 = self.v(lambda h: h.scalar_tensor_tensor(
                    gsc2[:], gsc2[:], GELU_A, gsc1[:], ALU.mult, ALU.add))
                e4 = self.s(lambda h: h.activation(gsc2[:], gsc2[:], AF.Tanh,
                                                   scale=GELU_C), waits=[e3])
                e5 = self.v(lambda h: h.tensor_scalar(gsc2[:], gsc2[:], 0.5, 0.5,
                                                      ALU.mult, ALU.add), waits=[e4])
                gi = oc % 4
                e6 = self.v(lambda h, gi=gi: h.tensor_mul(gst[gi][:], gsc1[:],
                                                          gsc2[:]),
                            waits=gfree5[gi])
                self._g_prev = [e6]
                td = em.dma("sync", self.g_hbm[oc * 128:(oc + 1) * 128,
                                               tt * 512:(tt + 1) * 512],
                            gst[gi][:], waits=[e6], sem=f"d_g{gi}")
                gfree5[gi] = [td]
        tok_gh = [em.stream_tok(f"d_g{i}") for i in range(4)]

        self.new_phase()
        wcmT = self.sbp("wcmT", 128, 32 * D, BF16)
        bcm_cols = self.sbp("bcm_cols", 128, 8)
        tok_wcm = self.gen_weight_fam("cm", D, FI, wcmT, waits=self.fence)
        tok_bcm = self.gen_bias_cols("cm", D, bcm_cols, self.fence)
        wcm_done = tok_wcm[-1]
        self.stb = self.sbp("b5stb", 128, 8 * 512, BF16)
        self.stq = self.sbp("b5stq", 128, 8 * 512, BF16)
        self.stf = self.sbp("b5stf", 128, 512)
        g5 = [self.sbp("b5g0", 128, 32 * 512, BF16)]
        x25 = [self.sbp("b5x20", 128, 8 * 512)]
        msc = self.sbp("b5msc", 128, 512)
        gfree = {0: []}
        x2free = {0: []}
        for tt in range(TT):
            b = 0
            gb = g5[b]
            x2b = x25[b]
            xnb = x2b
            tok_g_ld = em.dma("sync", gb.rearrange("p (c t) -> p c t", c=32),
                              self.g_hbm.rearrange("(c p) t -> p c t", p=128)[
                                  :, :, tt * 512:(tt + 1) * 512],
                              waits=gfree[b] + tok_gh + self.fence, sem="d_b5g")
            tok_x2_ld = em.dma("sync", x2b.rearrange("p (c t) -> p c t", c=8),
                               self.x2_hbm.rearrange("(c p) t -> p c t", p=128)[
                                   :, :, tt * 512:(tt + 1) * 512],
                               waits=x2free[b] + tok_x2 + self.fence, sem="d_b5x")
            xn_toks = []
            for oc in range(8):
                bank = oc % 2
                pb, bw = self.ps.get(bank)
                mm = None
                for ic in range(32):
                    mm = self.t(lambda h, ic=ic, oc=oc, gb=gb, pb=pb,
                                       st=(ic == 0), sp=(ic == 31):
                                h.matmul(pb[:, 0:512],
                                         wcmT[:, ic * D + oc * 128: ic * D + oc * 128 + 128],
                                         gb[:, ic * 512:(ic + 1) * 512],
                                         start=st, stop=sp),
                                waits=[tok_g_ld, wcm_done] + bw)
                gfree[b].append(mm)
                e0 = self.s(lambda h, pb=pb, oc=oc: h.activation(
                    msc[:], pb[:, 0:512], AF.Identity, bias=bcm_cols[:, oc:oc + 1]),
                    waits=[mm] + tok_bcm + getattr(self, "_m_prev", []))
                self.ps.consumed(bank, e0)
                e1 = self.v(lambda h, oc=oc, x2b=x2b, xnb=xnb: h.tensor_add(
                    xnb[:, oc * 512:(oc + 1) * 512], msc[:],
                    x2b[:, oc * 512:(oc + 1) * 512]),
                    waits=[e0, tok_x2_ld] + getattr(self, "_xn_prev", []))
                self._m_prev = [e1]
                xn_toks.append(e1)
            td = em.dma("sync", self.x_hbm.rearrange("(c p) t -> p c t", p=128)[
                :, :, tt * 512:(tt + 1) * 512],
                xnb.rearrange("p (c t) -> p c t", c=8), waits=xn_toks, sem="d_xn")
            rows, casts = self.stats_tile(
                lambda ic, xnb=xnb: xnb[:, ic * 512:(ic + 1) * 512], 512, tt * 512,
                xn_toks)
            x2free[b] = xn_toks + casts + [td]
            self._xn_prev = casts + [td]
        self.tok_x_ready = [em.stream_tok("d_xn")]
        self.tok_stats_ready = em.stream_tok("c_v")

    def p_attn_b(self):
        em = self.em
        T = self.T
        TT = self.TT
        SLC = T // 16

        self.new_phase()
        wqkv = self.sbp("bq_w", 128, 8 * I3, BF16)
        wout = self.sbp("bq_wo", 128, 8 * D, BF16)
        bq_cols = self.sbp("bq_bc", 128, 24)
        bo_cols = self.sbp("bq_bo", 128, 8)
        t_wq = em.dma("gpsimd", wqkv.rearrange("p (c o) -> p c o", c=8),
                      self.inp["L0_attn_b_W_qkv"].rearrange("(c p) o -> p c o", p=128),
                      waits=self.fence, sem="d_bqw")
        t_wo = em.dma("gpsimd", wout.rearrange("p (c o) -> p c o", c=8),
                      self.inp["L0_attn_b_W_out"].rearrange("(c p) o -> p c o", p=128),
                      waits=self.fence, sem="d_bqw")
        tok_bq = self.load_cols(self.inp["L0_attn_b_b_qkv"], 24, bq_cols, self.fence)
        tok_bo = self.load_cols(self.inp["L0_attn_b_b_out"], 8, bo_cols, self.fence)
        tok_bias = em.stream_tok("d_small")

        G = 2
        self.at_sbufs = 1
        self.attn_alloc(G, "b")
        self.stb = self.sbp("bqstb", 128, 8 * 256, BF16)
        self.stq = self.sbp("bqstq", 128, 8 * 256, BF16)
        self.stf = self.sbp("bqstf", 128, 256)
        qp = self.sbp("bq_qp", 64, T, BF16)
        kp = self.sbp("bq_kp", 64, T, BF16)
        vf = self.sbp("bq_vf", 64, T, BF16)
        vtk = self.sbp("bq_vtk", 128, (T // 128) * 64, BF16)
        ohb = vf
        xs = self.sbp("bq_xs", 128, 8 * SLC)
        xns = self.sbp("bq_xns", 128, 8 * SLC, BF16)
        nscb = self.sbp("bq_nsc", 128, SLC)
        qkvs = [self.sbp(f"bq_qkvs{i}", 128, SLC, BF16) for i in range(2)]
        sasl = self.sbp("bq_sa", 128, 8 * SLC, BF16)
        xnew = xs
        mscb = nscb

        xsfree = list(self.fence)
        xnsfree = list(self.fence)
        qkvsfree = list(self.fence)
        qpfree = list(self.fence)
        kpfree = list(self.fence)
        vffree = list(self.fence)
        vtkfree = list(self.fence)
        ohfree = {g: list(self.fence) for g in range(2)}
        safree = list(self.fence)
        xnewfree = list(self.fence)
        prev_rep = []

        def prep_slice(hd):
            nonlocal xsfree, xnsfree, qkvsfree, qpfree, kpfree, vffree, \
                vtkfree, prev_rep
            col0 = hd * SLC
            tok_x = em.dma("sync", xs.rearrange("p (c t) -> p c t", c=8),
                           self.x_hbm.rearrange("(c p) t -> p c t", p=128)[
                               :, :, col0:col0 + SLC],
                           waits=xsfree + self.tok_x_ready, sem="d_bqx")
            rtoks = self.row_math_outer(col0, SLC, [self.tok_stats_ready] + prev_rep)
            Aps, Bps, rep = self.rep_rows(self.rA[0:1, 0:SLC], self.rB[0:1, 0:SLC],
                                          SLC, rtoks, banks=(4, 5))
            prev_rep = rep
            xn_toks = []
            for ic in range(8):
                u1 = self.v(lambda h, ic=ic: h.scalar_tensor_tensor(
                    nscb[:, 0:SLC], xs[:, ic * SLC:(ic + 1) * SLC], 1.0,
                    Aps[:, 0:SLC], ALU.mult, ALU.mult),
                    waits=[tok_x] + rep)
                u2 = self.v(lambda h, ic=ic: h.scalar_tensor_tensor(
                    xns[:, ic * SLC:(ic + 1) * SLC], nscb[:, 0:SLC], 1.0,
                    Bps[:, 0:SLC], ALU.mult, ALU.add), waits=xnsfree)
                xn_toks.append(u2)
            xnsfree = []
            for bk in (4, 5):
                self.ps.consumed(bk, xn_toks[-1])
            cp_toks = []
            qfree_in = {0: list(qkvsfree), 1: list(qkvsfree)}
            for oc in range(24):
                bank = oc % 2
                pb, bw = self.ps.get(bank)
                mm = None
                for ic in range(8):
                    mm = self.t(lambda h, ic=ic, oc=oc, pb=pb,
                                       st=(ic == 0), sp=(ic == 7):
                                h.matmul(pb[:, 0:SLC],
                                         wqkv[:, ic * I3 + oc * 128: ic * I3 + oc * 128 + 128],
                                         xns[:, ic * SLC:(ic + 1) * SLC],
                                         start=st, stop=sp),
                                waits=xn_toks + bw + [t_wq])
                qb = qkvs[oc % 2]
                ev = self.s(lambda h, pb=pb, oc=oc, qb=qb: h.activation(
                    qb[:], pb[:, 0:SLC], AF.Identity,
                    bias=bq_cols[:, oc:oc + 1]),
                    waits=[mm] + tok_bq + qfree_in[oc % 2])
                self.ps.consumed(bank, ev)
                qfree_in[oc % 2] = []
                for h64 in range(2):
                    c = 128 * oc + 64 * h64
                    j, r = divmod(c, 192)
                    part = r // 64
                    dstbuf = (qp, kp, vf)[part]
                    sfree = (qpfree if part == 0 else kpfree if part == 1
                             else vffree)
                    dap = dstbuf[0:64, j: j + 16 * (SLC - 1) + 1:16]
                    tcp = self.v(lambda h, h64=h64, dap=dap, qb=qb:
                                 h.tensor_copy(dap, qb[64 * h64:64 * h64 + 64, :]),
                                 waits=[ev] + sfree)
                    sfree.clear()
                    cp_toks.append(tcp)
                    qfree_in[oc % 2].append(tcp)
            qpfree = []
            kpfree = []
            vffree = []
            vtr_toks = []
            for kt in range(T // 128):
                seq = len(self.at_bfseq)
                pbk = self.ps.t["bf" if seq % 2 == 0 else "bf2"]
                stag = (self.at_bfw0 if seq < 2 else [self.at_bfseq[seq - 2]])
                ttr = self.t(lambda h, kt=kt, pbk=pbk: h.transpose(
                    pbk[:, 0:64],
                    vf[0:64, kt * 128:(kt + 1) * 128],
                    self.ident_bf[0:64, 0:64]),
                    waits=cp_toks + stag + [self.tok_const])
                tev = self.v(lambda h, kt=kt, pbk=pbk:
                             h.tensor_copy(vtk[:, kt * 64:(kt + 1) * 64],
                                           pbk[:, 0:64]),
                             waits=[ttr] + vtkfree)
                vtkfree = []
                self.at_bfseq.append(tev)
                vtr_toks.append(tev)
            xsfree = xn_toks
            self._bq_x_tok = tok_x
            return cp_toks, vtr_toks, tok_x

        def post_slice(hd, oh, x_tok):
            nonlocal safree, xnewfree, xsfree
            col0 = hd * SLC
            sa_toks = []
            for j in range(16):
                cfull = 64 * j
                oc, off = divmod(cfull, 128)
                tcp = self.v(lambda h, oc=oc, off=off, j=j, oh=oh:
                             h.tensor_copy(sasl[off:off + 64,
                                                oc * SLC:(oc + 1) * SLC],
                                           oh[0:64, j: j + 16 * (SLC - 1) + 1:16]),
                             waits=safree)
                sa_toks.append(tcp)
            safree = []
            out_toks = []
            for oc in range(8):
                bank = oc % 2
                pb, bw = self.ps.get(bank)
                mm = None
                for ic in range(8):
                    mm = self.t(lambda h, ic=ic, oc=oc, pb=pb,
                                       st=(ic == 0), sp=(ic == 7):
                                h.matmul(pb[:, 0:SLC],
                                         wout[:, ic * D + oc * 128: ic * D + oc * 128 + 128],
                                         sasl[:, ic * SLC:(ic + 1) * SLC],
                                         start=st, stop=sp),
                                waits=sa_toks + bw + [t_wo])
                e0 = self.s(lambda h, pb=pb, oc=oc: h.activation(
                    mscb[:], pb[:, 0:SLC], AF.Identity, bias=bo_cols[:, oc:oc + 1]),
                    waits=[mm] + tok_bo + getattr(self, "_bq_m_prev", []))
                self.ps.consumed(bank, e0)
                e1 = self.v(lambda h, oc=oc: h.tensor_add(
                    xnew[:, oc * SLC:(oc + 1) * SLC], mscb[:],
                    xs[:, oc * SLC:(oc + 1) * SLC]),
                    waits=[e0, x_tok] + xnewfree)
                self._bq_m_prev = [e1]
                out_toks.append(e1)
            xnewfree = []
            td = em.dma("sync", self.x_hbm.rearrange("(c p) t -> p c t", p=128)[
                :, :, col0:col0 + SLC],
                xnew.rearrange("p (c t) -> p c t", c=8), waits=out_toks,
                sem="d_bqo")
            if not getattr(self, "SKIP_BQ_STATS", False):
                rows, casts = self.stats_tile(
                    lambda ic: xnew[:, ic * SLC:(ic + 1) * SLC], SLC, col0, out_toks)
                xnewfree = casts + [td]
            else:
                xnewfree = out_toks + [td]
            xsfree = xsfree + out_toks + list(xnewfree)
            return sa_toks

        for hd in range(H):
            cp_toks, vtr_toks, x_tok = prep_slice(hd)
            tok_qk = ("c_v", em.cnt["c_v"])
            tok_v2 = ("c_v", em.cnt["c_v"])

            def dst_fn(g):
                w = ohfree[g % 2] + safree
                ohfree[g % 2] = []
                return ohb[0:64, g * G * 128:(g + 1) * G * 128], w

            def done_fn(g, oe):
                ohfree[g % 2].append(oe)

            qk_cons, v_cons = self.attn_core2(
                qp, kp, vtk, SCALE_B, dst_fn, done_fn, tok_qk, tok_qk, tok_v2)
            qpfree += qk_cons
            kpfree += qk_cons
            vtkfree = vtkfree + v_cons
            tx = em.dma("sync", xs.rearrange("p (c t) -> p c t", c=8),
                        self.x_hbm.rearrange("(c p) t -> p c t", p=128)[
                            :, :, hd * SLC:(hd + 1) * SLC],
                        waits=xsfree + self.tok_x_ready, sem="d_bqx")
            xsfree = []
            sa = post_slice(hd, ohb, tx)
            safree = list(sa)
        self.tok_x_ready = [em.stream_tok("d_bqo")]
        self.tok_stats_ready = em.stream_tok("c_v")

    def p_ffn(self, idx):
        em = self.em
        TT = self.TT
        T = self.T
        sfx = f"f{idx}"

        self.new_phase()
        w_em = self.sbp(f"{sfx}_wem", NE, T, BF16)


        self.push_phase()
        xf = [self.sbp(f"{sfx}_x{i}", 128, 8 * 512) for i in range(2)]
        xnb = [self.sbp(f"{sfx}_xn{i}", 128, 8 * 512, BF16) for i in range(2)]
        xsc = self.sbp(f"{sfx}_xsc", 128, 512)
        xsc2 = self.sbp(f"{sfx}_xsc2", 128, 512)
        wg = self.sbp(f"{sfx}_wg", 128, 64)
        gt1 = self.sbp(f"{sfx}_gt1", 128, 8)
        gt2 = self.sbp(f"{sfx}_gt2", 128, 8)
        gm1 = self.sbp(f"{sfx}_gm1", 128, 1)
        gm2 = self.sbp(f"{sfx}_gm2", 128, 1)
        gsum = self.sbp(f"{sfx}_gs", 128, 1)
        weq = self.sbp(f"{sfx}_weq", 128, 8)
        tok_wg = em.dma("scalar", wg.rearrange("p (c e) -> p c e", e=NE),
                        self.inp["L0_ffn_Wg"].rearrange("(c p) e -> p c e", p=128),
                        waits=self.fence, sem="d_small")
        xfree = {0: [], 1: []}
        xnfree = {0: list(self.fence), 1: list(self.fence)}
        prev_rep = []
        for tt in range(TT):
            b = tt % 2
            x = xf[b]
            xn = xnb[b]
            tok_x = em.dma("sync", x.rearrange("p (c t) -> p c t", c=8),
                           self.x_hbm.rearrange("(c p) t -> p c t", p=128)[
                               :, :, tt * 512:(tt + 1) * 512],
                           waits=xfree[b] + self.tok_x_ready + self.fence,
                           sem=f"d_{sfx}x{b}")
            rtoks = self.row_math_outer(tt * 512, 512,
                                        [self.tok_stats_ready] + prev_rep)
            Aps, Bps, rep = self.rep_rows(self.rA[0:1, 0:512], self.rB[0:1, 0:512],
                                          512, rtoks, banks=(4, 5))
            prev_rep = rep
            gp = {}
            gpw = {}
            for sub in range(4):
                gp[sub], gpw[sub] = self.ps.get(sub)
            g_mms = {sub: None for sub in range(4)}
            xn_toks = []
            for ic in range(8):
                u1 = self.v(lambda h, ic=ic, x=x: h.scalar_tensor_tensor(
                    xsc[:, 0:512], x[:, ic * 512:(ic + 1) * 512], 1.0,
                    Aps[:, 0:512], ALU.mult, ALU.mult),
                    waits=[tok_x] + rep)
                u2 = self.v(lambda h: h.scalar_tensor_tensor(
                    xsc2[:, 0:512], xsc[:, 0:512], 1.0, Bps[:, 0:512],
                    ALU.mult, ALU.add))
                cv = self.s(lambda h, ic=ic, xn=xn: h.activation(
                    xn[:, ic * 512:(ic + 1) * 512], xsc2[:, 0:512], AF.Copy),
                    waits=[u2] + xnfree[b])
                xn_toks.append(cv)
                for sub in range(4):
                    g_mms[sub] = self.t(lambda h, ic=ic, sub=sub, gp=gp:
                                        h.matmul(gp[sub][:, 0:8],
                                                 xsc2[:, sub * 128:(sub + 1) * 128],
                                                 wg[:, ic * 8:(ic + 1) * 8],
                                                 start=(ic == 0), stop=(ic == 7)),
                                        waits=[u2, tok_wg] + (gpw[sub] if ic == 0 else []))
                self.em.last["vector"] = u2
                if ic < 7:
                    for s in range(4):
                        self.em.wait("vector", g_mms[s])
                    self.em.wait("vector", cv)
            td_xn = em.dma("sync", self.xn_hbm.rearrange("(c p) t -> p c t", p=128)[
                :, :, tt * 512:(tt + 1) * 512],
                xn.rearrange("p (c t) -> p c t", c=8), waits=xn_toks,
                sem=f"d_{sfx}xn{b}")
            xnfree[b] = [td_xn]
            xfree[b] = xn_toks
            for sub in range(4):
                pbk = gp[sub]
                e0 = self.s(lambda h, pbk=pbk: h.activation(
                    gt1[:], pbk[:, 0:8], AF.Exp, scale=-1.0),
                    waits=[g_mms[sub]] + getattr(self, "_gate_prev", []))
                e1 = self.v(lambda h: h.tensor_scalar(gt1[:], gt1[:], 1.0, None,
                                                      ALU.add), waits=[e0])
                e2 = self.v(lambda h: h.reciprocal(gt1[:], gt1[:]))
                e3 = self.v(lambda h: h.reduce_max(gm1[:], gt1[:], axis=AX.X))
                e4 = self.v(lambda h: h.tensor_single_scalar(weq[:], gt1[:], gm1[:],
                                                             ALU.is_equal))
                e5 = self.v(lambda h: h.scalar_tensor_tensor(
                    gt2[:], weq[:], -10000.0, gt1[:], ALU.mult, ALU.add))
                e6 = self.v(lambda h: h.reduce_max(gm2[:], gt2[:], axis=AX.X))
                e7 = self.v(lambda h: h.tensor_add(gsum[:], gm1[:], gm2[:]))
                e8 = self.v(lambda h: h.reciprocal(gsum[:], gsum[:]))
                e11 = self.v(lambda h: h.tensor_single_scalar(gt2[:], gt2[:], gm2[:],
                                                              ALU.is_equal))
                e9 = self.v(lambda h: h.tensor_mul(gm1[:], gm1[:], gsum[:]))
                e10 = self.v(lambda h: h.tensor_mul(gm2[:], gm2[:], gsum[:]))
                e12 = self.v(lambda h: h.tensor_scalar(weq[:], weq[:], gm1[:], None,
                                                       ALU.mult))
                e13 = self.v(lambda h: h.scalar_tensor_tensor(
                    weq[:], gt2[:], gm2[:], weq[:], ALU.mult, ALU.add))
                self.ps.consumed(sub, e0)
                tpb2, tbw2 = self.ps.get(sub)
                ttr = self.t(lambda h, tpb2=tpb2: h.transpose(
                    tpb2[0:8, 0:128], weq[:, 0:8], self.ident_f[:]),
                    waits=[e13, self.tok_const] + tbw2)
                dst = w_em[0:8, tt * 512 + sub * 128: tt * 512 + (sub + 1) * 128]
                tev = self.s(lambda h, tpb2=tpb2, dst=dst: h.activation(
                    dst, tpb2[0:8, 0:128], AF.Copy),
                    waits=[ttr] + getattr(self, "_wem_prev", self.fence))
                self.ps.consumed(sub, tev)
                self._gate_prev = [e13, tev]
                self._wem_prev = []
        tok_xn = [em.stream_tok(f"d_{sfx}xn{i}") for i in range(2)]
        tok_wem = em.stream_tok("c_s")
        self.tap_hbm(f"tap_xn{idx}", self.xn_hbm, tok_xn)

        self.pop_phase()
        self.push_phase()
        ohu = self.sbp(f"{sfx}_ohu", NE, 128, BF16)
        ohu_free = list(self.fence)
        w1 = self.sbp(f"{sfx}_w1", 128, 8 * FF, BF16)
        w3 = self.sbp(f"{sfx}_w3", 128, 8 * FF, BF16)
        w2 = self.sbp(f"{sfx}_w2", 128, 16 * D, BF16)
        xnu = self.sbp(f"{sfx}_xnu", 128, 8 * 512, BF16)
        xnw = self.sbp(f"{sfx}_xnw", 128, 8 * 512, BF16)
        hst = self.sbp(f"{sfx}_h", 128, 16 * 512, BF16)
        usg = self.sbp(f"{sfx}_u", 128, 512, BF16)
        vsg = self.sbp(f"{sfx}_v", 128, 512, BF16)
        acs = self.sbp(f"{sfx}_acs", 128, 4 * 512)
        wrp = self.sbp(f"{sfx}_wrp", 128, 512, BF16)
        w1free = list(self.fence)
        w3free = list(self.fence)
        w2free = list(self.fence)
        xnufree = list(self.fence)
        xnwfree = list(self.fence)
        hfree = list(self.fence)
        acsfree = list(self.fence)
        wrpfree = list(self.fence)
        self._ffn_uprev = list(self.fence)
        self._ffn_vprev = list(self.fence)
        acc_tok = {tt: None for tt in range(TT)}
        units = [("s", None)] + [("e", e) for e in range(NE)]
        for ui, (kind, e) in enumerate(units):
            if kind == "s":
                src1 = self.inp["L0_ffn_Ws1"]
                src3 = self.inp["L0_ffn_Ws3"]
                src2 = self.inp["L0_ffn_Ws2"]
            else:
                src1 = self.inp["L0_ffn_We1"][e]
                src3 = self.inp["L0_ffn_We3"][e]
                src2 = self.inp["L0_ffn_We2"][e]
            if kind == "e":
                t_oh = em.dma("gpsimd", ohu[:], self.inp["c_onehot"][:, e * 128:(e + 1) * 128],
                              waits=ohu_free, sem=f"d_{sfx}oh")
            t1 = em.dma("gpsimd", w1.rearrange("p (c o) -> p c o", c=8),
                        src1.rearrange("(c p) o -> p c o", p=128),
                        waits=w1free, sem=f"d_{sfx}w1")
            t3 = em.dma("gpsimd", w3.rearrange("p (c o) -> p c o", c=8),
                        src3.rearrange("(c p) o -> p c o", p=128),
                        waits=w3free, sem=f"d_{sfx}w3")
            t2 = em.dma("gpsimd", w2.rearrange("p (c o) -> p c o", c=16),
                        src2.rearrange("(c p) o -> p c o", p=128),
                        waits=w2free, sem=f"d_{sfx}w2")
            w1free, w3free, w2free = [], [], []
            for tt in range(TT):
                tok_xn_ld = em.dma("sync", xnu.rearrange("p (c t) -> p c t", c=8),
                                   self.xn_hbm.rearrange("(c p) t -> p c t", p=128)[
                                       :, :, tt * 512:(tt + 1) * 512],
                                   waits=xnufree + tok_xn, sem=f"d_{sfx}xnu")
                xin = xnu
                xin_tok = [tok_xn_ld]
                if kind == "e":
                    rpb, rbw = self.ps.get(5)
                    rm = self.t(lambda h, e=e, tt=tt, rpb=rpb: h.matmul(
                        rpb[:, 0:512], ohu[:, 0:128],
                        w_em[0:8, tt * 512:(tt + 1) * 512], start=True, stop=True),
                        waits=[t_oh, tok_wem] + rbw)
                    re = self.s(lambda h, rpb=rpb: h.activation(
                        wrp[:], rpb[:, 0:512], AF.Copy), waits=[rm] + wrpfree)
                    self.ps.consumed(5, re)
                    xw_toks = []
                    for ic in range(8):
                        xw = self.v(lambda h, ic=ic: h.tensor_mul(
                            xnw[:, ic * 512:(ic + 1) * 512],
                            xnu[:, ic * 512:(ic + 1) * 512], wrp[:]),
                            waits=[re, tok_xn_ld] + xnwfree)
                        xw_toks.append(xw)
                    xnwfree = []
                    wrpfree = xw_toks
                    xin = xnw
                    xin_tok = xw_toks
                h_toks = []
                for oc in range(16):
                    bu = oc % 2
                    bv = 2 + oc % 2
                    pu, puw = self.ps.get(bu)
                    pv, pvw = self.ps.get(bv)
                    mmu = None
                    mmv = None
                    for ic in range(8):
                        mmu = self.t(lambda h, ic=ic, oc=oc, pu=pu,
                                            st=(ic == 0), sp=(ic == 7):
                                     h.matmul(pu[:, 0:512],
                                              w1[:, ic * FF + oc * 128: ic * FF + oc * 128 + 128],
                                              xnu[:, ic * 512:(ic + 1) * 512],
                                              start=st, stop=sp),
                                     waits=[t1, tok_xn_ld] + (puw if ic == 0 else []))
                        mmv = self.t(lambda h, ic=ic, oc=oc, pv=pv, xin=xin,
                                            st=(ic == 0), sp=(ic == 7):
                                     h.matmul(pv[:, 0:512],
                                              w3[:, ic * FF + oc * 128: ic * FF + oc * 128 + 128],
                                              xin[:, ic * 512:(ic + 1) * 512],
                                              start=st, stop=sp),
                                     waits=[t3] + xin_tok + (pvw if ic == 0 else []))
                    eu = self.s(lambda h, pu=pu: h.activation(usg[:], pu[:, 0:512],
                                                              AF.Silu),
                                waits=[mmu] + getattr(self, "_ffn_uprev", []))
                    self.ps.consumed(bu, eu)
                    ev = self.v(lambda h, pv=pv: h.tensor_copy(vsg[:], pv[:, 0:512]),
                                waits=[mmv] + getattr(self, "_ffn_vprev", []))
                    self.ps.consumed(bv, ev)
                    hh = self.v(lambda h, oc=oc: h.tensor_mul(
                        hst[:, oc * 512:(oc + 1) * 512], usg[:], vsg[:]),
                        waits=[eu] + hfree)
                    self._ffn_uprev = [hh]
                    self._ffn_vprev = [hh]
                    h_toks.append(hh)
                hfree = []
                xnufree = h_toks
                if kind == "e":
                    xnwfree = h_toks
                accw = [acc_tok[tt]] if acc_tok[tt] is not None else list(self.fence)
                for half in range(2):
                    ac_toks = []
                    for oc4 in range(4):
                        oc = half * 4 + oc4
                        bk = oc % 2
                        po, pow_ = self.ps.get(bk)
                        mmo = None
                        for ic in range(16):
                            mmo = self.t(lambda h, ic=ic, oc=oc, po=po,
                                                st=(ic == 0), sp=(ic == 15):
                                         h.matmul(po[:, 0:512],
                                                  w2[:, ic * D + oc * 128: ic * D + oc * 128 + 128],
                                                  hst[:, ic * 512:(ic + 1) * 512],
                                                  start=st, stop=sp),
                                         waits=[t2] + h_toks + (pow_ if ic == 0 else []))
                        hfree.append(mmo)
                        ea = self.s(lambda h, po=po, oc4=oc4: h.activation(
                            acs[:, oc4 * 512:(oc4 + 1) * 512], po[:, 0:512], AF.Copy),
                            waits=[mmo] + acsfree)
                        self.ps.consumed(bk, ea)
                        ac_toks.append(ea)
                    acsfree = []
                    td = em.dma("gpsimd",
                                self.acc_hbm.rearrange("(c p) t -> p c t", p=128)[
                                    :, half * 4:(half + 1) * 4,
                                    tt * 512:(tt + 1) * 512],
                                acs.rearrange("p (c t) -> p c t", c=4),
                                waits=ac_toks + accw, sem=f"d_{sfx}acc",
                                **({} if ui == 0 else {"accum_op": ALU.add}))
                    accw = [td]
                    acsfree = [td]
                acc_tok[tt] = td
            w1free = [td]
            w3free = [td]
            w2free = [td]
            if kind == "e":
                ohu_free = [rm]
        self.tap_hbm(f"tap_acc{idx}", self.acc_hbm,
                     [em.stream_tok(f"d_{sfx}acc")])
        if f"tap_wem{idx}" in self.taps:
            em.dma("sync", self.taps[f"tap_wem{idx}"], w_em[0:NE, 0:T],
                   waits=[tok_wem], sem="d_tap")
        self.pop_phase()
        self.push_phase()
        self.stb = self.sbp(f"{sfx}_stb", 128, 8 * 512, BF16)
        self.stq = self.sbp(f"{sfx}_stq", 128, 8 * 512, BF16)
        self.stf = self.sbp(f"{sfx}_stf", 128, 512)
        af = [self.sbp(f"{sfx}_af{i}", 128, 8 * 512) for i in range(2)]
        xpf = [self.sbp(f"{sfx}_xpf{i}", 128, 8 * 512) for i in range(2)]
        affree = {0: [], 1: []}
        xpffree = {0: [], 1: []}
        is_last = (idx == 2)
        for tt in range(TT):
            b = tt % 2
            ab = af[b]
            xb = xpf[b]
            tok_a = em.dma("sync", ab.rearrange("p (c t) -> p c t", c=8),
                           self.acc_hbm.rearrange("(c p) t -> p c t", p=128)[
                               :, :, tt * 512:(tt + 1) * 512],
                           waits=affree[b] + [acc_tok[tt],
                                              em.stream_tok(f"d_{sfx}acc")] + self.fence,
                           sem=f"d_{sfx}a{b}")
            tok_x = em.dma("sync", xb.rearrange("p (c t) -> p c t", c=8),
                           self.x_hbm.rearrange("(c p) t -> p c t", p=128)[
                               :, :, tt * 512:(tt + 1) * 512],
                           waits=xpffree[b] + self.tok_x_ready + self.fence,
                           sem=f"d_{sfx}px{b}")
            out_toks = []
            for ic in range(8):
                u1 = self.v(lambda h, ic=ic, ab=ab, xb=xb: h.scalar_tensor_tensor(
                    ab[:, ic * 512:(ic + 1) * 512], ab[:, ic * 512:(ic + 1) * 512],
                    1.0 / 3.0, xb[:, ic * 512:(ic + 1) * 512], ALU.mult, ALU.add),
                    waits=[tok_a, tok_x])
                out_toks.append(u1)
            td = em.dma("sync", self.x_hbm.rearrange("(c p) t -> p c t", p=128)[
                :, :, tt * 512:(tt + 1) * 512],
                ab.rearrange("p (c t) -> p c t", c=8), waits=out_toks,
                sem=f"d_{sfx}o{b}")
            affree[b] = [td]
            xpffree[b] = out_toks
            if not is_last:
                rows, casts = self.stats_tile(
                    lambda ic, ab=ab: ab[:, ic * 512:(ic + 1) * 512], 512, tt * 512,
                    out_toks)
                affree[b] = affree[b] + casts
        self.tok_x_ready = [em.stream_tok(f"d_{sfx}o0"), em.stream_tok(f"d_{sfx}o1")]
        self.tok_stats_ready = em.stream_tok("c_v")
        self.pop_phase()
        self.pop_phase()
        self.push_phase()

    def p0_ingest(self):
        em = self.em
        self.new_phase()
        self.p0_tm = [self.sbp(f"p0_tm{i}", 128, 4 * 1024) for i in range(2)]
        self.p0_x = [self.sbp(f"p0_x{i}", 128, 8 * 512) for i in range(2)]
        self.stb = self.sbp("p0stb", 128, 8 * 512, BF16)
        self.stq = self.sbp("p0stq", 128, 8 * 512, BF16)
        self.stf = self.sbp("p0stf", 128, 512)
        tmfree = {0: [], 1: []}
        xfree = {0: [], 1: []}
        for tt in range(self.TT):
            b = tt % 2
            tm = self.p0_tm[b]
            xf = self.p0_x[b]
            tok_ld = em.dma("sync",
                            tm.rearrange("p (s f) -> p s f", s=4),
                            self.hs[tt * 512:(tt + 1) * 512, :].rearrange(
                                "(s p) f -> p s f", p=128),
                            waits=tmfree[b], sem=f"d_p0_{b}")
            ev_toks = []
            tr_all = []
            for idx in range(32):
                s4, ic = idx // 8, idx % 8
                bank = 3 + (idx % 2)
                pb, bw = self.ps.get(bank)
                src = tm[:, s4 * 1024 + ic * 128: s4 * 1024 + ic * 128 + 128]
                tr = self.t(
                    lambda h, s=src, p=pb[:, 0:128]:
                    h.transpose(p, s, self.ident_f[:]),
                    waits=[tok_ld, self.tok_const] + bw)
                tr_all.append(tr)
                dst = xf[:, ic * 512 + s4 * 128: ic * 512 + s4 * 128 + 128]
                srcp = pb[:, 0:128]
                if idx % 2 == 0:
                    tev = self.s(lambda h, p=srcp, d=dst: h.activation(d, p, AF.Copy),
                                 waits=[tr] + xfree[b])
                else:
                    tev = self.v(lambda h, p=srcp, d=dst: h.tensor_copy(d, p),
                                 waits=[tr] + xfree[b])
                ev_toks.append(tev)
                self.ps.consumed(bank, tev)
            tmfree[b] = tr_all
            tok_st = em.dma("sync",
                            self.x_hbm.rearrange("(c p) t -> p c t", p=128)[
                                :, :, tt * 512:(tt + 1) * 512],
                            xf.rearrange("p (c t) -> p c t", c=8),
                            waits=ev_toks, sem=f"d_xst{b}")
            rows, casts = self.stats_tile(
                lambda ic, xf=xf: xf[:, ic * 512:(ic + 1) * 512], 512, tt * 512,
                ev_toks)
            xfree[b] = casts + [tok_st]
        self.tok_x_ready = [em.stream_tok("d_xst0"), em.stream_tok("d_xst1")]
        self.tok_stats_ready = em.stream_tok("c_v")

    def finish(self, ready_tok=None):
        em = self.em
        self.new_phase()
        fstage = [self.sbp(f"fin_x{i}", 128, 8 * 512) for i in range(2)]
        ostage = [self.sbp(f"fin_o{i}", 128, 4 * 1024) for i in range(2)]
        ffree = {0: [], 1: []}
        ofree = {0: [], 1: []}
        waits0 = list(ready_tok) if ready_tok is not None else list(self.tok_x_ready)
        for tt in range(self.TT):
            b = tt % 2
            xf = fstage[b]
            om = ostage[b]
            tok_ld = em.dma("sync",
                            xf.rearrange("p (c t) -> p c t", c=8),
                            self.x_hbm.rearrange("(c p) t -> p c t", p=128)[
                                :, :, tt * 512:(tt + 1) * 512],
                            waits=ffree[b] + waits0 + self.fence, sem=f"d_fin{b}")
            ev_toks = []
            tr_all = []
            for idx in range(32):
                ic, s4 = idx // 4, idx % 4
                bank = 3 + (idx % 2)
                pb, bw = self.ps.get(bank)
                src = xf[:, ic * 512 + s4 * 128: ic * 512 + s4 * 128 + 128]
                tr = self.t(
                    lambda h, s=src, p=pb[:, 0:128]:
                    h.transpose(p, s, self.ident_f[:]),
                    waits=[tok_ld, self.tok_const] + bw)
                tr_all.append(tr)
                dst = om[:, s4 * 1024 + ic * 128: s4 * 1024 + ic * 128 + 128]
                srcp = pb[:, 0:128]
                if idx % 2 == 0:
                    tev = self.s(lambda h, p=srcp, d=dst: h.activation(d, p, AF.Copy),
                                 waits=[tr] + ofree[b])
                else:
                    tev = self.v(lambda h, p=srcp, d=dst: h.tensor_copy(d, p),
                                 waits=[tr] + ofree[b])
                ev_toks.append(tev)
                self.ps.consumed(bank, tev)
            ffree[b] = tr_all
            tok_st = em.dma("sync",
                            self.out[tt * 512:(tt + 1) * 512, :].rearrange(
                                "(s p) f -> p s f", p=128),
                            om.rearrange("p (s f) -> p s f", s=4),
                            waits=ev_toks, sem=f"d_out{b}")
            ofree[b] = [tok_st]
        em.wait("sync", em.stream_tok("d_out0"))
        em.wait("sync", em.stream_tok("d_out1"))

    def build_phases(self):
        self.phase_stack = []
        self.fence = []
        self.p0_ingest()
        if self.n_sub >= 1:
            self.p_attn_a()
        if self.n_sub >= 2:
            self.p_ffn(1)
        if self.n_sub >= 3:
            self.p_attn_b()
        if self.n_sub >= 4:
            self.p_ffn(2)
        self.finish()

    def build(self, detect_races=True):
        self.nc = bass.Bass("TRN2", target_bir_lowering=False,
                            detect_race_conditions=detect_races)
        self.es = ExitStack()
        self.inp = {}
        self.declare()
        with self.es:
            self.em = Em(self.nc, self.es)
            self.ps = Psum(self.nc, self.es)
            self.alloc_persistent()
            self.build_phases()
            while self.phase_stack:
                self.pop_phase()
            self.flush_block()
        return self.nc

    def _replayer(self, eng):
        q = list(self.em.q[eng])

        def run(h):
            for fn in q:
                fn(h)
        return run


def make_in_maps(inputs, T):
    consts = host_constants()
    full = inputs["L0_attn_a_hs"]
    B = full.shape[0]
    in_maps = []
    for b in range(B):
        m = {k: np.ascontiguousarray(v) for k, v in inputs.items()
             if k != "L0_attn_a_hs"}
        m["L0_attn_a_hs"] = np.ascontiguousarray(full[b, :T])
        m.update(consts)
        in_maps.append(m)
    return in_maps


def kernel(**inputs):
    T = 4096
    bld = Builder(T=T)
    nc = bld.build()
    in_maps = make_in_maps(inputs, T)
    res = run_bass_kernel_spmd(nc, in_maps, core_ids=list(range(len(in_maps))))
    return np.stack([np.asarray(r["out"]) for r in res.results], axis=0)


# version 08
# speedup vs baseline: 1.0589x; 1.0278x over previous
from contextlib import ExitStack

import numpy as np

import concourse.bass as bass
import concourse.mybir as mybir
from concourse.bass_utils import run_bass_kernel_spmd

F32 = mybir.dt.float32
BF16 = mybir.dt.bfloat16
AF = mybir.ActivationFunctionType
ALU = mybir.AluOpType
AX = mybir.AxisListType

ENGINES = ("sync", "gpsimd", "tensor", "vector", "scalar")
CHAINED = ("vector", "scalar", "gpsimd")

D = 1024
H = 16
HD = 64
I3 = 3072
FI = 4096
FF = 2048
NE = 8
EPS = 1e-5
GELU_C = 0.7978845608028654
GELU_A = 0.044715
SCALE_A = 0.125
SCALE_B = (2 * 1024) ** -0.5
MASK_NEG = -1.0e9
CONST_VALS = (0.0, 1.0, EPS, -0.5, -1.0, 0.5, 2.0, SCALE_A, SCALE_B, GELU_C,
              GELU_A, 1.0 / D, 1.0 / 3.0)


class Em:

    def __init__(self, nc, es):
        self.nc = nc
        self.es = es
        self.q = {e: [] for e in ENGINES}
        self.sems = {}
        self.cnt = {}
        self.mark = {e: {} for e in ENGINES}
        self.last = {e: None for e in ENGINES}

    def sem(self, name):
        if name not in self.sems:
            self.sems[name] = self.es.enter_context(self.nc.semaphore(name))
            self.cnt[name] = 0
        return self.sems[name]

    def op(self, eng, fn, waits=(), sem=None, inc=0):
        if eng in CHAINED:
            self.wait(eng, self.last[eng])
            if not inc:
                sem, inc = "c_" + eng[0], 1
        for t in waits:
            self.wait(eng, t)
        if inc:
            s = self.sem(sem)
            self.cnt[sem] += inc
            val = self.cnt[sem]

            def emit(h, fn=fn, s=s, inc=inc):
                fn(h).then_inc(s, inc)

            self.q[eng].append(emit)
            tok = (sem, val)
            if eng in CHAINED and inc == 1:
                self.last[eng] = tok
            return tok
        self.q[eng].append(fn)
        return None

    def wait(self, eng, tok):
        if tok is None:
            return
        name, val = tok
        if name.startswith("d_") and val != self.cnt[name]:
            val = self.cnt[name]
        if self.mark[eng].get(name, 0) >= val:
            return
        self.mark[eng][name] = val
        s = self.sems[name]
        self.q[eng].append(lambda h, s=s, val=val: h.wait_ge(s, val))

    def dma(self, queue, out, in_, waits=(), sem=None, **kw):
        assert sem is not None

        def fn(h, out=out, in_=in_, kw=kw):
            return h.dma_start(out=out, in_=in_, **kw)

        if queue == "gpsimd":
            for t in waits:
                self.wait(queue, t)
            s = self.sem(sem)
            self.cnt[sem] += 16
            val = self.cnt[sem]

            def emit(h, fn=fn, s=s):
                fn(h).then_inc(s, 16)

            self.q[queue].append(emit)
            return (sem, val)
        return self.op(queue, fn, waits=waits, sem=sem, inc=16)

    def stream_tok(self, sem):
        self.sem(sem)
        return (sem, self.cnt[sem])


class Psum:

    def __init__(self, nc, es):
        self.t = {i: es.enter_context(nc.psum_tensor(f"pb{i}", [128, 512], F32))
                  for i in range(6)}
        self.t["bf"] = es.enter_context(nc.psum_tensor("pbf", [128, 1024], BF16))
        self.t["bf2"] = es.enter_context(nc.psum_tensor("pbf2", [128, 1024], BF16))
        self.free = {k: [] for k in self.t}

    def get(self, i):
        w = self.free[i]
        self.free[i] = []
        return self.t[i], w

    def consumed(self, i, tok):
        if tok is not None:
            self.free[i].append(tok)


def host_constants():
    import ml_dtypes
    bf = ml_dtypes.bfloat16
    c = {}
    c["c_ident_bf"] = np.eye(128, dtype=bf)
    c["c_ident_f32"] = np.eye(128, dtype=np.float32)
    c["c_ones_row_f32"] = np.ones((1, 128), dtype=np.float32)
    c["c_ones_col_bf"] = np.ones((128, 1), dtype=bf)
    pool = np.tile(np.asarray(CONST_VALS, dtype=np.float32)[None, :], (128, 1))
    c["c_pool"] = pool
    c["c_pool_bf"] = pool.astype(bf)
    dm = np.zeros((128, 640), dtype=np.float32)
    qq = np.arange(128)[:, None]
    kk = np.arange(128)[None, :]
    dm[:, 512:640] = np.where(kk > qq, MASK_NEG, 0.0).astype(np.float32)
    c["c_dmask"] = dm
    oh = np.zeros((NE, NE, 128), dtype=np.float32)
    for e in range(NE):
        oh[e, e, :] = 1.0
    c["c_onehot"] = np.ascontiguousarray(oh.transpose(1, 0, 2).reshape(NE, NE * 128))
    return c


class Builder:
    def __init__(self, T=4096, taps=(), n_sub=4):
        self.T = T
        self.TT = T // 512
        self.taps = {}
        self.tap_req = list(taps)
        self.n_sub = n_sub

    def di(self, name, shape, dt=F32):
        ap = self.nc.dram_tensor(name, list(shape), dt, kind="ExternalInput").ap()
        self.inp[name] = ap
        return ap

    def scratch(self, name, shape, dt=F32):
        return self.nc.dram_tensor(name, list(shape), dt, kind="Internal").ap()

    def sb(self, name, p, f, dt=F32):
        return self.es.enter_context(self.nc.sbuf_tensor(name, [p, f], dt))

    def declare(self):
        T = self.T
        self.hs = self.di("L0_attn_a_hs", (T, D))
        for nm in ("ln1_g", "ln1_b", "ln2_g", "ln2_b"):
            self.di(f"L0_attn_a_{nm}", (D,))
        for fam, (o, i) in (("ca", (I3, D)), ("cp", (D, D)), ("cf", (FI, D)),
                            ("cm", (D, FI))):
            self.di(f"L0_attn_a_{fam}_sub_w", (2, o, i))
            self.di(f"L0_attn_a_{fam}_Wl_w", (1, 2))
            self.di(f"L0_attn_a_{fam}_W_w", (o, i))
            self.di(f"L0_attn_a_{fam}_B_w", (o, i))
            self.di(f"L0_attn_a_{fam}_sub_b", (2, o))
            self.di(f"L0_attn_a_{fam}_Wl_b", (1, 2))
            self.di(f"L0_attn_a_{fam}_W_b", (o,))
            self.di(f"L0_attn_a_{fam}_B_b", (o,))
        self.di("L0_attn_b_W_qkv", (D, I3))
        self.di("L0_attn_b_b_qkv", (I3,))
        self.di("L0_attn_b_W_out", (D, D))
        self.di("L0_attn_b_b_out", (D,))
        self.di("L0_ffn_Wg", (D, NE))
        self.di("L0_ffn_Ws1", (D, FF))
        self.di("L0_ffn_Ws3", (D, FF))
        self.di("L0_ffn_Ws2", (FF, D))
        self.di("L0_ffn_We1", (NE, D, FF))
        self.di("L0_ffn_We3", (NE, D, FF))
        self.di("L0_ffn_We2", (NE, FF, D))
        self.di("c_ident_bf", (128, 128), BF16)
        self.di("c_ident_f32", (128, 128))
        self.di("c_ones_row_f32", (1, 128))
        self.di("c_ones_col_bf", (128, 1), BF16)
        self.di("c_pool", (128, len(CONST_VALS)))
        self.di("c_pool_bf", (128, len(CONST_VALS)), BF16)
        self.di("c_dmask", (128, 640))
        self.di("c_onehot", (NE, NE * 128))

        self.out = self.nc.dram_tensor("out", [T, D], F32, kind="ExternalOutput").ap()
        for (name, shape, dt) in self.tap_req:
            self.taps[name] = self.nc.dram_tensor(
                name, list(shape), dt, kind="ExternalOutput").ap()

        self.x_hbm = self.scratch("x_hbm", (D, T))
        self.q_hbm = self.scratch("q_hbm", (D, T), BF16)
        self.k_hbm = self.scratch("k_hbm", (D, T), BF16)
        self.v_hbm = self.scratch("v_hbm", (T, D), BF16)
        self.o_hbm = self.scratch("o_hbm", (D, T), BF16)
        self.xn_hbm = self.scratch("xn_hbm", (D, T), BF16)
        self.r_hbm = self.scratch("r_hbm", (D, T))
        self.x2_hbm = self.scratch("x2_hbm", (D, T))
        self.acc_hbm = self.scratch("acc_hbm", (D, T))
        self.g_hbm = self.scratch("g_hbm", (FI, T), BF16)

    def alloc_persistent(self):
        em = self.em
        self.ident_bf = self.sb("ident_bf", 128, 128, BF16)
        self.ident_f = self.sb("ident_f", 128, 128)
        self.ones_row = self.sb("ones_row", 1, 128)
        self.ones_row_bf = self.sb("ones_row_bf", 1, 128, BF16)
        self.ones_col_bf = self.sb("ones_col_bf", 128, 1, BF16)
        self.sum_row = self.sb("sum_row", 1, self.T)
        self.ssq_row = self.sb("ssq_row", 1, self.T)
        self.rA = self.sb("rA", 1, 512)
        self.rB = self.sb("rB", 1, 512)
        self.rt1 = self.sb("rt1", 1, 512)
        self.rt2 = self.sb("rt2", 1, 512)
        self.rt3 = self.sb("rt3", 1, 512)

        em.dma("sync", self.ident_bf[:], self.inp["c_ident_bf"], sem="d_const")
        em.dma("sync", self.ident_f[:], self.inp["c_ident_f32"], sem="d_const")
        em.dma("sync", self.ones_row[:], self.inp["c_ones_row_f32"], sem="d_const")
        em.dma("sync", self.ones_col_bf[:], self.inp["c_ones_col_bf"], sem="d_const")
        self.cpool = self.sb("cpool", 128, len(CONST_VALS))
        self.cpool_bf = self.sb("cpool_bf", 128, len(CONST_VALS), BF16)
        em.dma("sync", self.cpool[:], self.inp["c_pool"], sem="d_const")
        em.dma("sync", self.cpool_bf[:], self.inp["c_pool_bf"], sem="d_const")
        for j, val in enumerate(CONST_VALS):
            self.nc.const_aps.register(F32, val, self.cpool[:, j:j + 1])
            self.nc.const_aps.register(BF16, val, self.cpool_bf[:, j:j + 1])
        self.tok_const = em.stream_tok("d_const")
        for e in CHAINED:
            em.last[e] = self.tok_const
        self.tok_onesbf = self.v(
            lambda h: h.tensor_copy(self.ones_row_bf[:], self.ones_row[:]),
            waits=[self.tok_const])

    def v(self, fn, waits=()):
        return self.em.op("vector", fn, waits=waits)

    def s(self, fn, waits=()):
        return self.em.op("scalar", fn, waits=waits)

    def t(self, fn, waits=()):
        return self.em.op("tensor", fn, waits=waits, sem="c_t", inc=1)

    def g(self, fn, waits=()):
        return self.em.op("gpsimd", fn, waits=waits)

    def stats_tile(self, src_fn, W, col0, waits, banks=(2, 0)):
        prev = getattr(self, "_stats_prev_mm", [])
        rows = []
        all_src_toks = []
        for j in range(2):
            cast_toks = []
            for ic in range(8):
                hi = self.stb[:, ic * W:(ic + 1) * W]
                lo = self.stq[:, ic * W:(ic + 1) * W]
                if j == 0:
                    t1 = self.s(lambda h, s=src_fn(ic), hi=hi:
                                h.activation(hi, s, AF.Copy),
                                waits=list(waits) + prev)
                    t2 = self.v(lambda h, s=src_fn(ic), hi=hi, lo=lo:
                                h.scalar_tensor_tensor(lo, hi, -1.0, s,
                                                       ALU.mult, ALU.add),
                                waits=[t1] + prev)
                else:
                    t0 = self.s(lambda h, s=src_fn(ic):
                                h.activation(self.stf[:, 0:W], s, AF.Square),
                                waits=list(waits) + prev +
                                getattr(self, "_stf_prev", []))
                    t1 = self.s(lambda h, hi=hi:
                                h.activation(hi, self.stf[:, 0:W], AF.Copy))
                    t2 = self.v(lambda h, hi=hi, lo=lo:
                                h.scalar_tensor_tensor(lo, hi, -1.0,
                                                       self.stf[:, 0:W],
                                                       ALU.mult, ALU.add),
                                waits=[t1])
                    self._stf_prev = [t2]
                cast_toks += [t1, t2]
            all_src_toks += cast_toks
            pb, bw = self.ps.get(banks[j])
            mm = None
            for k in range(2):
                buf = self.stb if k == 0 else self.stq
                for ic in range(8):
                    mm = self.t(
                        lambda h, r=buf[:, ic * W:(ic + 1) * W], o=pb[0:1, 0:W],
                               st=(k == 0 and ic == 0), sp=(k == 1 and ic == 7):
                        h.matmul(o, self.ones_col_bf[:, 0:1], r, start=st, stop=sp),
                        waits=cast_toks + bw + [self.tok_onesbf])
            dst = (self.sum_row if j == 0 else self.ssq_row)[0:1, col0:col0 + W]
            ev = self.v(lambda h, s=pb[0:1, 0:W], d=dst: h.tensor_copy(d, s),
                        waits=[mm])
            self.ps.consumed(banks[j], ev)
            rows.append(ev)
            prev = [mm]
            self._stats_prev_mm = [mm]
        return rows, all_src_toks

    def row_math_outer(self, col0, W, waits):
        sr = self.sum_row[0:1, col0:col0 + W]
        qr = self.ssq_row[0:1, col0:col0 + W]
        m = self.rt1[0:1, 0:W]
        vv = self.rt2[0:1, 0:W]
        t1 = self.v(lambda h: h.tensor_scalar(m, sr, 1.0 / D, None, ALU.mult),
                    waits=waits)
        t2 = self.v(lambda h: h.tensor_mul(self.rt3[0:1, 0:W], m, m))
        t3 = self.v(lambda h: h.scalar_tensor_tensor(
            vv, qr, 1.0 / D, self.rt3[0:1, 0:W], ALU.mult, ALU.subtract), waits=waits)
        t4 = self.s(lambda h: h.activation(self.rt3[0:1, 0:W], vv, AF.Ln, bias=EPS),
                    waits=[t3])
        t5 = self.s(lambda h: h.activation(self.rA[0:1, 0:W], self.rt3[0:1, 0:W],
                                           AF.Exp, scale=-0.5))
        t6 = self.v(lambda h: h.scalar_tensor_tensor(
            self.rB[0:1, 0:W], m, -1.0, self.rA[0:1, 0:W], ALU.mult, ALU.mult),
            waits=[t5])
        return [t5, t6]

    def row_math_fused_inner(self, W, waits):
        vv = self.rt2[0:1, 0:W]
        m = self.rt1[0:1, 0:W]
        t1 = self.v(lambda h: h.tensor_mul(self.rt3[0:1, 0:W], self.rA[0:1, 0:W],
                                           self.rA[0:1, 0:W]), waits=waits)
        t2 = self.v(lambda h: h.tensor_mul(self.rt3[0:1, 0:W], self.rt3[0:1, 0:W], vv))
        t3 = self.s(lambda h: h.activation(self.rt3[0:1, 0:W], self.rt3[0:1, 0:W],
                                           AF.Ln, bias=EPS), waits=[t2])
        t4 = self.s(lambda h: h.activation(self.rt3[0:1, 0:W], self.rt3[0:1, 0:W],
                                           AF.Exp, scale=-0.5))
        t5 = self.v(lambda h: h.tensor_mul(self.rA2[0:1, 0:W], self.rA[0:1, 0:W],
                                           self.rt3[0:1, 0:W]), waits=[t4])
        t6 = self.v(lambda h: h.scalar_tensor_tensor(
            self.rB2[0:1, 0:W], m, -1.0, self.rA2[0:1, 0:W], ALU.mult, ALU.mult))
        return [t5, t6]

    def rep_rows(self, arow_ap, brow_ap, W, waits, banks=(4, 5)):
        pa, wa = self.ps.get(banks[0])
        pb, wb = self.ps.get(banks[1])
        ta = self.t(lambda h: h.matmul(pa[:, 0:W], self.ones_row[0:1, 0:128],
                                       arow_ap, start=True, stop=True),
                    waits=list(waits) + wa + [self.tok_const])
        tb = self.t(lambda h: h.matmul(pb[:, 0:W], self.ones_row[0:1, 0:128],
                                       brow_ap, start=True, stop=True),
                    waits=list(waits) + wb + [self.tok_const])
        return pa, pb, [ta, tb]

    def normalize_tile(self, x_fn, out_fn, W, Aps, Bps, rep_toks, banks,
                       waits=(), gcol=None, bcol=None, scratch=None):
        toks = []
        psum_cons = []
        for ic in range(8):
            if gcol is None:
                t1 = self.v(lambda h, x=x_fn(ic): h.scalar_tensor_tensor(
                    scratch[:, 0:W], x, 1.0, Aps[:, 0:W], ALU.mult, ALU.mult),
                    waits=list(waits) + rep_toks)
                t2 = self.v(lambda h, o=out_fn(ic): h.scalar_tensor_tensor(
                    o, scratch[:, 0:W], 1.0, Bps[:, 0:W], ALU.mult, ALU.add))
                psum_cons += [t1, t2]
            else:
                t0 = self.v(lambda h, ic=ic: h.tensor_scalar(
                    self.nrmA[:, 0:W], Aps[:, 0:W], gcol[:, ic:ic + 1], None,
                    ALU.mult), waits=list(waits) + rep_toks)
                t0b = self.v(lambda h, ic=ic: h.tensor_scalar(
                    self.nrmB[:, 0:W], Bps[:, 0:W], gcol[:, ic:ic + 1],
                    bcol[:, ic:ic + 1], ALU.mult, ALU.add))
                t1 = self.v(lambda h, x=x_fn(ic): h.scalar_tensor_tensor(
                    scratch[:, 0:W], x, 1.0, self.nrmA[:, 0:W], ALU.mult, ALU.mult))
                t2 = self.v(lambda h, o=out_fn(ic): h.scalar_tensor_tensor(
                    o, scratch[:, 0:W], 1.0, self.nrmB[:, 0:W], ALU.mult, ALU.add))
                psum_cons += [t0, t0b]
            toks.append(t2)
        for b in banks:
            for c in psum_cons:
                self.ps.consumed(b, c)
        return toks

    def _set_fence(self):
        em = self.em
        self.fence = [(s, em.cnt[s]) for s in list(em.cnt.keys())]
        for attr in ("_stats_prev_mm", "_stf_prev", "_zprev", "_ocp_prev",
                     "_g_prev", "_m_prev", "_xn_prev", "_bq_m_prev",
                     "_wem_prev", "_ffn_uprev", "_ffn_vprev", "_gate_prev"):
            setattr(self, attr, list(self.fence))
        self._pfree = None

    def push_phase(self):
        self.phase_stack.append(ExitStack())

    def flush_block(self):
        em = self.em
        if not any(em.q[e] for e in ENGINES):
            return
        with self.nc.Block() as block:
            for eng in ENGINES:
                if em.q[eng]:
                    getattr(block, eng)(self._replayer(eng))
        for e in ENGINES:
            em.q[e] = []

    def pop_phase(self):
        self.flush_block()
        self.phase_stack.pop().close()
        self._set_fence()

    def new_phase(self):
        if self.phase_stack:
            self.pop_phase()
        self.push_phase()

    @property
    def phase_es(self):
        return self.phase_stack[-1] if self.phase_stack else None

    def sbp(self, name, p, f, dt=F32):
        return self.phase_stack[-1].enter_context(
            self.nc.sbuf_tensor(name, [p, f], dt))

    def tap_hbm(self, name, src_ap, waits):
        if name in self.taps:
            self.em.dma("sync", self.taps[name], src_ap, waits=waits, sem="d_tap")

    def load_scalar_pair(self, wl_name, cols, waits):
        em = self.em
        self._lsp = getattr(self, "_lsp", 0) + 1
        row = self.sbp(f"wlr{self._lsp}", 1, 2)
        t0 = em.dma("scalar", row[:], self.inp[wl_name], waits=waits, sem="d_small")
        pb, bw = self.ps.get(2)
        mms = []
        for j in range(2):
            mms.append(self.t(lambda h, j=j: h.matmul(
                pb[:, j * 128:j * 128 + 1], self.ones_row[0:1, 0:128],
                row[0:1, j:j + 1], start=True, stop=True),
                waits=[t0, self.tok_const] + bw))
        e1 = self.s(lambda h: h.activation(cols[:, 0:1], pb[:, 0:1], AF.Copy),
                    waits=mms)
        e2 = self.s(lambda h: h.activation(cols[:, 1:2], pb[:, 128:129], AF.Copy))
        self.ps.consumed(2, e1)
        self.ps.consumed(2, e2)
        return [e1, e2]

    def gen_bias_cols(self, fam, o_dim, dst_cols, waits):
        em = self.em
        nm = f"L0_attn_a_{fam}"
        n_oc = o_dim // 128
        wl = self.sbp(f"bwl_{fam}", 128, 2)
        ctoks = self.load_scalar_pair(f"{nm}_Wl_b", wl, waits)
        bufs = [self.sbp(f"bg{k}_{fam}", 128, n_oc) for k in range(4)]
        ldt = []
        for k, src in enumerate((self.inp[f"{nm}_sub_b"][0, :],
                                 self.inp[f"{nm}_sub_b"][1, :],
                                 self.inp[f"{nm}_W_b"],
                                 self.inp[f"{nm}_B_b"])):
            for oc in range(n_oc):
                ldt.append(em.dma("scalar", bufs[k][:, oc:oc + 1],
                                  src[oc * 128:(oc + 1) * 128],
                                  waits=waits, sem="d_small"))
        ls = em.stream_tok("d_small")
        t1 = self.v(lambda h: h.tensor_scalar(dst_cols[:, 0:n_oc], bufs[0][:, 0:n_oc],
                                              wl[:, 0:1], None, ALU.mult),
                    waits=[ls] + ctoks)
        t2 = self.v(lambda h: h.scalar_tensor_tensor(
            dst_cols[:, 0:n_oc], bufs[1][:, 0:n_oc], wl[:, 1:2],
            dst_cols[:, 0:n_oc], ALU.mult, ALU.add))
        t3 = self.s(lambda h: h.activation(dst_cols[:, 0:n_oc], dst_cols[:, 0:n_oc],
                                           AF.Tanh), waits=[t2])
        t4 = self.v(lambda h: h.tensor_mul(dst_cols[:, 0:n_oc], dst_cols[:, 0:n_oc],
                                           bufs[2][:, 0:n_oc]), waits=[t3])
        t5 = self.v(lambda h: h.tensor_add(dst_cols[:, 0:n_oc], dst_cols[:, 0:n_oc],
                                           bufs[3][:, 0:n_oc]))
        return [t5]

    def gen_bias_row(self, fam, lo, hi, dst_row, waits):
        em = self.em
        nm = f"L0_attn_a_{fam}"
        W = hi - lo
        wl = self.sbp(f"rwl_{fam}", 128, 2)
        ctoks = self.load_scalar_pair(f"{nm}_Wl_b", wl, waits)
        bufs = [self.sbp(f"rg{k}_{fam}", 1, W) for k in range(4)]
        ldt = []
        for k, src in enumerate((self.inp[f"{nm}_sub_b"][0, lo:hi],
                                 self.inp[f"{nm}_sub_b"][1, lo:hi],
                                 self.inp[f"{nm}_W_b"][lo:hi],
                                 self.inp[f"{nm}_B_b"][lo:hi])):
            ldt.append(em.dma("scalar", bufs[k][0:1, :], src, waits=waits,
                              sem="d_small"))
        ls = em.stream_tok("d_small")
        t1 = self.v(lambda h: h.tensor_scalar(dst_row[0:1, 0:W], bufs[0][0:1, :],
                                              wl[0:1, 0:1], None, ALU.mult),
                    waits=[ls] + ctoks)
        t2 = self.v(lambda h: h.scalar_tensor_tensor(
            dst_row[0:1, 0:W], bufs[1][0:1, :], wl[0:1, 1:2], dst_row[0:1, 0:W],
            ALU.mult, ALU.add))
        t3 = self.s(lambda h: h.activation(dst_row[0:1, 0:W], dst_row[0:1, 0:W],
                                           AF.Tanh), waits=[t2])
        t4 = self.v(lambda h: h.tensor_mul(dst_row[0:1, 0:W], dst_row[0:1, 0:W],
                                           bufs[2][0:1, :]), waits=[t3])
        t5 = self.v(lambda h: h.tensor_add(dst_row[0:1, 0:W], dst_row[0:1, 0:W],
                                           bufs[3][0:1, :]))
        return [t5]

    def gen_weight_fam(self, fam, o_dim, i_dim, dst_sb, waits=()):
        em = self.em
        nm = f"L0_attn_a_{fam}"
        wl = self.sbp(f"gwl_{fam}", 128, 2)
        ctoks = self.load_scalar_pair(f"{nm}_Wl_w", wl, waits)
        GW = 512
        gw = [[self.sbp(f"gw{j}{k}_{fam}", 128, GW) for k in range(4)]
              for j in range(2)]
        gtmp = [self.sbp(f"gt{j}_{fam}", 128, GW) for j in range(2)]
        gout = [self.sbp(f"go{j}_{fam}", 128, GW, BF16) for j in range(2)]
        gfree = {0: list(waits), 1: list(waits)}
        sub = self.inp[f"{nm}_sub_w"]
        Wt = self.inp[f"{nm}_W_w"]
        Bt = self.inp[f"{nm}_B_w"]
        n_ot = o_dim // 128
        n_ig = i_dim // GW
        done = []
        pc = 0
        for ot in range(n_ot):
            for ig in range(n_ig):
                j = pc % 2
                pc += 1
                b0, b1, bW, bB = gw[j]
                osl = slice(ot * 128, (ot + 1) * 128)
                isl = slice(ig * GW, (ig + 1) * GW)
                em.dma("gpsimd", b0[:, :], sub[0, osl, isl], waits=gfree[j],
                       sem=f"d_gen{j}")
                em.dma("gpsimd", b1[:, :], sub[1, osl, isl], sem=f"d_gen{j}")
                em.dma("gpsimd", bW[:, :], Wt[osl, isl], sem=f"d_gen{j}")
                em.dma("gpsimd", bB[:, :], Bt[osl, isl], sem=f"d_gen{j}")
                ldt = em.stream_tok(f"d_gen{j}")
                t1 = self.v(lambda h, b0=b0, j=j: h.tensor_scalar(
                    gtmp[j][:, :], b0[:, :], wl[:, 0:1], None, ALU.mult),
                    waits=[ldt] + ctoks)
                t2 = self.v(lambda h, b1=b1, j=j: h.scalar_tensor_tensor(
                    gtmp[j][:, :], b1[:, :], wl[:, 1:2], gtmp[j][:, :],
                    ALU.mult, ALU.add))
                t3 = self.s(lambda h, j=j: h.activation(gtmp[j][:, :], gtmp[j][:, :],
                                                        AF.Tanh), waits=[t2])
                t4 = self.v(lambda h, bW=bW, j=j: h.tensor_mul(
                    gtmp[j][:, :], gtmp[j][:, :], bW[:, :]), waits=[t3])
                t5 = self.v(lambda h, bB=bB, j=j: h.tensor_add(
                    gout[j][:, :], gtmp[j][:, :], bB[:, :]), waits=gfree[j])
                gfree[j] = [t5]
                for icl in range(GW // 128):
                    ic = ig * (GW // 128) + icl
                    pbf, bw = self.ps.get("bf")
                    ttr = self.t(lambda h, s=gout[j][:, icl * 128:(icl + 1) * 128],
                                        p=pbf[:, 0:128]:
                                 h.transpose(p, s, self.ident_bf[:]),
                                 waits=[t5, self.tok_const] + bw)
                    dst = dst_sb[:, ic * o_dim + ot * 128: ic * o_dim + ot * 128 + 128]
                    tev = self.v(lambda h, p=pbf[:, 0:128], d=dst: h.tensor_copy(d, p),
                                 waits=[ttr])
                    self.ps.consumed("bf", tev)
                    gfree[j].append(tev)
                    done.append(tev)
        return done

    def load_cols(self, src_1d, n_oc, dst_cols, waits):
        toks = []
        for oc in range(n_oc):
            toks.append(self.em.dma("scalar", dst_cols[:, oc:oc + 1],
                                    src_1d[oc * 128:(oc + 1) * 128],
                                    waits=waits, sem="d_small"))
        return toks

    def attn_alloc(self, G, sfx):
        T = self.T
        em = self.em
        self.atG = G
        self.at_dmask = self.sbp(f"at_dmask{sfx}", 128, 640)
        em.dma("sync", self.at_dmask[:], self.inp["c_dmask"],
               waits=self.fence, sem="d_atc")
        self.at_ctok = em.stream_tok("d_atc")
        self.at_S = [self.sbp(f"at_S{i}{sfx}", 128, T)
                     for i in range(getattr(self, "at_sbufs", 2))]
        npbuf = 2 * G if getattr(self, "at_sbufs", 2) >= 4 else G
        self.at_P = [self.sbp(f"at_P{i}{sfx}", 128, T, BF16) for i in range(npbuf)]
        self.at_PT = [self.sbp(f"at_PT{i}{sfx}", 128, G * 128, BF16)
                      for i in range(2)]
        self.at_mrow = self.sbp(f"at_mrow{sfx}", 128, 2)
        self.at_negm = self.sbp(f"at_negm{sfx}", 128, G)
        self.at_zparts = self.sbp(f"at_zparts{sfx}", 128, G * 8)
        self.at_zscr = self.sbp(f"at_zscr{sfx}", 128, 8)
        self.at_zsum = self.sbp(f"at_zsum{sfx}", 128, 1)
        self.at_zc = self.sbp(f"at_zc{sfx}", 128, 2 * G)
        self.at_zrow = self.sbp(f"at_zrow{sfx}", 1, 2 * G * 128)
        self.at_zrep = self.sbp(f"at_zrep{sfx}", 64, G * 128)
        self.at_sfree = {(p, w): list(self.fence)
                         for p in range(len(self.at_S)) for w in range(T // 512)}
        _, bfw0 = self.ps.get("bf")
        _, bfw1 = self.ps.get("bf2")
        self.at_bfw0 = bfw0 + bfw1 + list(self.fence)
        self.at_bfseq = []
        self.at_pfree = {i: list(self.fence) for i in range(len(self.at_P))}
        self.at_ptfree = {i: list(self.fence) for i in range(2)}
        self.at_negmfree = {i: list(self.fence) for i in range(G)}
        self.at_zfree = list(self.fence)
        self.at_zcfree = {0: list(self.fence), 1: list(self.fence)}
        self.at_zrowfree = {0: list(self.fence), 1: list(self.fence)}
        self.at_zsumfree = list(self.fence)
        self.at_mrowfree = list(self.fence)

    def attn_core2(self, q_all, k_all, v_tok, scale, dst_fn, done_fn,
                   tq, tk, tv):
        em = self.em
        T = self.T
        G = self.atG
        NT = T // 128
        ng_groups = NT // G
        qk_cons = []
        v_cons = []

        def emit_S(g, tiles=None, ttrs=None):
            ttrs = {} if ttrs is None else ttrs
            for tl in (range(G) if tiles is None else tiles):
                t = g * G + tl
                sp = ((g % 2) * 2 + tl % 2) % len(self.at_S)
                S = self.at_S[sp]
                kspan = (t + 1) * 128
                nwin = (kspan + 511) // 512
                rvs = []
                for w in range(nwin):
                    N = min(512, kspan - w * 512)
                    bank = w % 2
                    spb, sbw = self.ps.get(bank)
                    mm = self.t(lambda h, t=t, w=w, N=N, spb=spb: h.matmul(
                        spb[:, 0:N], q_all[0:64, t * 128:(t + 1) * 128],
                        k_all[0:64, w * 512:w * 512 + N], start=True, stop=True),
                        waits=[tq, tk] + sbw)
                    qk_cons.append(mm)
                    last = (w == nwin - 1)
                    in1 = (self.at_dmask[:, 640 - N:640] if last
                           else self.at_dmask[:, 0:N])
                    ttr = self.v(lambda h, w=w, N=N, in1=in1, spb=spb, S=S:
                                 h.tensor_tensor(
                                     S[:, w * 512:w * 512 + N],
                                     spb[:, 0:N], in1, ALU.add),
                                 waits=[mm, self.at_ctok] + self.at_sfree[(sp, w)])
                    self.at_sfree[(sp, w)] = []
                    self.ps.consumed(bank, ttr)
                    if w == 0:
                        rv = self.v(lambda h, w=w, N=N, S=S: h.reduce_max(
                            self.at_mrow[:, 0:1],
                            S[:, w * 512:w * 512 + N:4], axis=AX.X),
                            waits=[ttr] + self.at_mrowfree)
                        self.at_mrowfree = []
                    else:
                        rv = self.v(lambda h, w=w, N=N, S=S: h.reduce_max(
                            self.at_mrow[:, 1:2],
                            S[:, w * 512:w * 512 + N:4], axis=AX.X),
                            waits=[ttr] + self.at_mrowfree)
                        self.at_mrowfree = []
                        cmb = self.v(lambda h: h.tensor_max(
                            self.at_mrow[:, 0:1], self.at_mrow[:, 0:1],
                            self.at_mrow[:, 1:2]), waits=[rv])
                        self.at_mrowfree.append(cmb)
                    rvs.append(rv)
                    ttrs[(tl, w)] = ttr
                ngt = self.v(lambda h, tl=tl: h.tensor_scalar(
                    self.at_negm[:, tl:tl + 1], self.at_mrow[:, 0:1],
                    -scale, None, ALU.mult),
                    waits=self.at_negmfree[tl] + rvs)
                self.at_mrowfree.append(ngt)
                self.at_negmfree[tl] = []
                ttrs[("ng", tl)] = ngt
            return ttrs

        def emit_E(g, ttrs, tiles=None, exp_toks=None):
            par = g % 2
            exp_toks = {} if exp_toks is None else exp_toks
            for tl in (range(G) if tiles is None else tiles):
                t = g * G + tl
                sp = ((g % 2) * 2 + tl % 2) % len(self.at_S)
                S = self.at_S[sp]
                pslot = ((g % 2) * G + tl) % len(self.at_P)
                kspan = (t + 1) * 128
                nwin = (kspan + 511) // 512
                for w in range(nwin):
                    N = min(512, kspan - w * 512)
                    pe = self.s(lambda h, tl=tl, w=w, N=N, S=S, pslot=pslot: h.activation(
                        self.at_P[pslot][:, w * 512:w * 512 + N],
                        S[:, w * 512:w * 512 + N],
                        AF.Exp, bias=self.at_negm[:, tl:tl + 1], scale=scale,
                        accum_out=self.at_zparts[:, tl * 8 + w:tl * 8 + w + 1]),
                        waits=[ttrs[(tl, w)], ttrs[("ng", tl)]]
                              + (self.at_pfree[pslot] if w == 0 else []))
                    if w == 0:
                        self.at_pfree[pslot] = []
                    self.at_sfree[(sp, w)].append(pe)
                    self.at_negmfree[tl].append(pe)
                    exp_toks[(tl, w)] = pe
                zs = self.s(lambda h, tl=tl, nw=nwin: h.activation(
                    self.at_zscr[:, 0:nw], self.at_zparts[:, tl * 8:tl * 8 + nw],
                    AF.Copy, accum_out=self.at_zsum[:, 0:1]),
                    waits=[exp_toks[(tl, w2)] for w2 in range(nwin)]
                          + self.at_zsumfree)
                rc = self.v(lambda h, tl=tl, par=par: h.reciprocal(
                    self.at_zc[:, par * G + tl:par * G + tl + 1],
                    self.at_zsum[:, 0:1]),
                    waits=[zs] + self.at_zcfree[par])
                self.at_zsumfree = [rc]
                self.at_zcfree[par] = []
                pz, pzw = self.ps.get(3)
                ztr = self.t(lambda h, tl=tl, par=par, pz=pz: h.transpose(
                    pz[0:1, 0:128],
                    self.at_zc[:, par * G + tl:par * G + tl + 1],
                    self.ident_f[:]),
                    waits=[rc, self.tok_const] + pzw)
                self.at_zcfree[par].append(ztr)
                ze = self.s(lambda h, tl=tl, par=par, pz=pz: h.activation(
                    self.at_zrow[0:1, (par * G + tl) * 128:(par * G + tl + 1) * 128],
                    pz[0:1, 0:128], AF.Copy),
                    waits=[ztr] + self.at_zrowfree[par])
                self.at_zrowfree[par] = []
                self.ps.consumed(3, ze)
                exp_toks[("z", tl)] = ze
            return exp_toks

        def emit_BC(g, exp_toks):
            par = g % 2
            nkc = g * G + G
            po, pow_ = self.ps.get(2)
            mmo = None
            for kc in range(nkc):
                j = kc % 2
                pt = self.at_PT[j]
                pt_toks = []
                for tl in range(G):
                    t = g * G + tl
                    pslot = ((g % 2) * G + tl) % len(self.at_P)
                    if kc <= t:
                        seq = len(self.at_bfseq)
                        bname = "bf" if seq % 2 == 0 else "bf2"
                        pbf = self.ps.t[bname]
                        stag = (self.at_bfw0 if seq < 2
                                else [self.at_bfseq[seq - 2]])
                        ttr2 = self.t(lambda h, kc=kc, pslot=pslot, pbf=pbf:
                                      h.transpose(
                                          pbf[:, 0:128],
                                          self.at_P[pslot][:, kc * 128:(kc + 1) * 128],
                                          self.ident_bf[:]),
                                      waits=[exp_toks[(tl, kc // 4)],
                                             self.tok_const] + stag)
                        self.at_pfree[pslot].append(ttr2)
                        if seq % 2 == 0:
                            ev = self.s(lambda h, tl=tl, pt=pt, pbf=pbf:
                                        h.activation(pt[:, tl * 128:(tl + 1) * 128],
                                                     pbf[:, 0:128], AF.Copy),
                                        waits=[ttr2] + self.at_ptfree[j])
                        else:
                            ev = self.v(lambda h, tl=tl, pt=pt, pbf=pbf:
                                        h.tensor_copy(pt[:, tl * 128:(tl + 1) * 128],
                                                      pbf[:, 0:128]),
                                        waits=[ttr2] + self.at_ptfree[j])
                        self.at_bfseq.append(ev)
                    else:
                        ev = self.v(lambda h, tl=tl, pt=pt: h.memset(
                            pt[:, tl * 128:(tl + 1) * 128], 0.0),
                            waits=self.at_ptfree[j])
                    self.at_ptfree[j] = []
                    pt_toks.append(ev)
                mmo = self.t(lambda h, kc=kc, pt=pt, po=po,
                                    st=(kc == 0), sp=(kc == nkc - 1):
                             h.matmul(po[0:64, 0:G * 128],
                                      v_tok[:, kc * 64:(kc + 1) * 64],
                                      pt[:, 0:G * 128], start=st, stop=sp),
                             waits=pt_toks + [tv] + (pow_ if kc == 0 else []))
                v_cons.append(mmo)
                self.at_ptfree[j] = [mmo]
            pzr, pzrw = self.ps.get(4)
            zm = self.t(lambda h, par=par, pzr=pzr: h.matmul(
                pzr[0:64, 0:G * 128], self.ones_row[0:1, 0:64],
                self.at_zrow[0:1, par * G * 128:(par + 1) * G * 128],
                start=True, stop=True),
                waits=[exp_toks[("z", tl)] for tl in range(G)]
                      + [self.tok_const] + pzrw)
            self.at_zrowfree[par].append(zm)
            ze2 = self.s(lambda h, pzr=pzr: h.activation(
                self.at_zrep[:, 0:G * 128], pzr[0:64, 0:G * 128], AF.Copy),
                waits=[zm] + self.at_zfree)
            self.ps.consumed(4, ze2)
            dap, dwaits = dst_fn(g)
            oe = self.v(lambda h, po=po, dap=dap: h.tensor_mul(
                dap, po[0:64, 0:G * 128], self.at_zrep[:, 0:G * 128]),
                waits=[ze2, mmo] + list(dwaits))
            self.ps.consumed(2, oe)
            self.at_zfree = [oe]
            done_fn(g, oe)

        if len(self.at_S) >= G:
            def emit_SE(g):
                ttrs = {}
                exps = {}
                for pair in (list(range(G))[:2], list(range(G))[2:]):
                    emit_S(g, tiles=pair, ttrs=ttrs)
                    emit_E(g, ttrs, tiles=pair, exp_toks=exps)
                return exps
            exps = {0: emit_SE(0)}
            for g in range(ng_groups):
                if g + 1 < ng_groups:
                    exps[g + 1] = emit_SE(g + 1)
                emit_BC(g, exps[g])
        else:
            for g in range(ng_groups):
                ttrs = {}
                exps = {}
                for tl in range(G):
                    emit_S(g, tiles=[tl], ttrs=ttrs)
                    emit_E(g, ttrs, tiles=[tl], exp_toks=exps)
                emit_BC(g, exps)
        return qk_cons, v_cons

    def p_attn_a(self):
        em = self.em
        TT = self.TT
        T = self.T

        self.new_phase()
        wcaT = self.sbp("wcaT", 128, 8 * I3, BF16)
        self.nrmA = self.sbp("nrmA_a", 128, 512)
        self.nrmB = self.sbp("nrmB_a", 128, 512)
        self.rA2 = self.sbp("rA2", 1, 512)
        self.rB2 = self.sbp("rB2", 1, 512)
        bca_cols = self.sbp("bca_cols", 128, 16)
        bvrow = self.sbp("bvrow", 1, 1024)
        bvrep = self.sbp("bvrep", 128, 1024)
        ln1g = self.sbp("ln1g", 128, 8)
        ln1b = self.sbp("ln1b", 128, 8)
        tok_w = self.gen_weight_fam("ca", I3, D, wcaT, waits=self.fence)
        tok_bc = self.gen_bias_cols("ca", 2048, bca_cols, self.fence)
        tok_bv = self.gen_bias_row("ca", 2048, 3072, bvrow, self.fence)
        rep_toks = []
        for half in range(2):
            pb, bw = self.ps.get(2)
            tm = self.t(lambda h, half=half: h.matmul(
                pb[:, 0:512], self.ones_row[0:1, 0:128],
                bvrow[0:1, half * 512:(half + 1) * 512], start=True, stop=True),
                waits=tok_bv + bw + [self.tok_const])
            te = self.s(lambda h, half=half: h.activation(
                bvrep[:, half * 512:(half + 1) * 512], pb[:, 0:512], AF.Copy),
                waits=[tm])
            self.ps.consumed(2, te)
            rep_toks.append(te)
        tok_g = self.load_cols(self.inp["L0_attn_a_ln1_g"], 8, ln1g, self.fence)
        tok_b = self.load_cols(self.inp["L0_attn_a_ln1_b"], 8, ln1b, self.fence)
        ln1_tok = em.stream_tok("d_small")
        w_done = tok_w[-1]

        xa = [self.sbp("a1x0", 128, 8 * 512)]
        h1 = [self.sbp(f"a1h{i}", 128, 8 * 512, BF16) for i in range(2)]
        nsc = self.sbp("a1nsc", 128, 512)
        qst = [self.sbp(f"a1q{i}", 128, 512, BF16) for i in range(4)]
        vst = [self.sbp(f"a1v{i}", 128, 1024, BF16) for i in range(2)]
        xfree = {0: [], 1: []}
        hfree = {0: [], 1: []}
        qfree = {i: [] for i in range(4)}
        vfree = {0: [], 1: []}
        prev_rep = []
        for tt in range(TT):
            b = tt % 2
            x = xa[0]
            hb = h1[b]
            tok_x = em.dma("sync", x.rearrange("p (c t) -> p c t", c=8),
                           self.x_hbm.rearrange("(c p) t -> p c t", p=128)[
                               :, :, tt * 512:(tt + 1) * 512],
                           waits=xfree[0] + xfree[1] + self.tok_x_ready,
                           sem="d_a1x0")
            rtoks = self.row_math_outer(tt * 512, 512, [self.tok_stats_ready] + prev_rep)
            rtoks2 = self.row_math_fused_inner(512, rtoks)
            Aps, Bps, rep, = self.rep_rows(self.rA2[0:1, 0:512], self.rB2[0:1, 0:512],
                                           512, rtoks2, banks=(4, 5))
            prev_rep = rep
            htoks = self.normalize_tile(
                lambda ic, x=x: x[:, ic * 512:(ic + 1) * 512],
                lambda ic, hb=hb: hb[:, ic * 512:(ic + 1) * 512],
                512, Aps, Bps, rep, (4, 5),
                waits=[tok_x, ln1_tok] + hfree[b], gcol=ln1g, bcol=ln1b, scratch=nsc)
            xfree[b] = []
            hfree[b] = []
            for oc in range(16):
                bank = oc % 2
                pb, bw = self.ps.get(bank)
                mm = None
                for ic in range(8):
                    mm = self.t(lambda h, ic=ic, oc=oc, hb=hb, pb=pb,
                                       st=(ic == 0), sp=(ic == 7):
                                h.matmul(pb[:, 0:512],
                                         wcaT[:, ic * I3 + oc * 128: ic * I3 + oc * 128 + 128],
                                         hb[:, ic * 512:(ic + 1) * 512],
                                         start=st, stop=sp),
                                waits=htoks + bw + [w_done])
                q = qst[oc % 4]
                ev = self.s(lambda h, pb=pb, q=q, oc=oc: h.activation(
                    q[:], pb[:, 0:512], AF.Identity, bias=bca_cols[:, oc:oc + 1]),
                    waits=[mm] + qfree[oc % 4] + tok_bc)
                self.ps.consumed(bank, ev)
                dst = (self.q_hbm if oc < 8 else self.k_hbm)
                ocr = oc if oc < 8 else oc - 8
                td = em.dma("sync", dst[ocr * 128:(ocr + 1) * 128,
                                        tt * 512:(tt + 1) * 512], q[:],
                            waits=[ev], sem=f"d_qk{oc % 4}")
                qfree[oc % 4] = [td]
                hfree[b].append(mm)
            for ts4 in range(4):
                vb = ts4 % 2
                vs = vst[vb]
                evs = []
                for half in range(2):
                    bank = 2 + half
                    pb, bw = self.ps.get(bank)
                    mm = None
                    for ic in range(8):
                        mm = self.t(lambda h, ic=ic, ts4=ts4, half=half, hb=hb, pb=pb,
                                           st=(ic == 0), sp=(ic == 7):
                                    h.matmul(pb[:, 0:512],
                                             hb[:, ic * 512 + ts4 * 128: ic * 512 + ts4 * 128 + 128],
                                             wcaT[:, ic * I3 + 2048 + half * 512:
                                                  ic * I3 + 2048 + (half + 1) * 512],
                                             start=st, stop=sp),
                                    waits=htoks + bw + [w_done])
                    ev = self.v(lambda h, pb=pb, vs=vs, half=half: h.scalar_tensor_tensor(
                        vs[:, half * 512:(half + 1) * 512], pb[:, 0:512], 1.0,
                        bvrep[:, half * 512:(half + 1) * 512], ALU.mult, ALU.add),
                        waits=[mm] + vfree[vb] + rep_toks)
                    self.ps.consumed(bank, ev)
                    evs.append(ev)
                    hfree[b].append(mm)
                td = em.dma("sync", self.v_hbm[tt * 512 + ts4 * 128:
                                               tt * 512 + (ts4 + 1) * 128, :], vs[:],
                            waits=evs, sem=f"d_v{vb}")
                vfree[vb] = [td]
            xfree[b] = htoks
        tok_qk = [em.stream_tok(f"d_qk{i}") for i in range(4)]
        tok_v = [em.stream_tok(f"d_v{i}") for i in range(2)]
        self.tap_hbm("tap_q", self.q_hbm, tok_qk)
        self.tap_hbm("tap_k", self.k_hbm, tok_qk)
        self.tap_hbm("tap_v", self.v_hbm, tok_v)

        self.new_phase()
        G = 4
        self.at_sbufs = 4
        self.attn_alloc(G, "a")
        qh = [self.sbp("qh0", 64, T, BF16)]
        kh = [self.sbp("kh0", 64, T, BF16)]
        vth = [self.sbp("vth0", 128, (T // 128) * 64, BF16)]
        ost = [self.sbp(f"ost{i}", 64, G * 128, BF16) for i in range(2)]
        qfree2 = {0: list(self.fence), 1: list(self.fence)}
        vfree2 = {0: list(self.fence), 1: list(self.fence)}
        ofree = {i: list(self.fence) for i in range(2)}
        for hd in range(H):
            b = 0
            tq = em.dma("sync", qh[b][:], self.q_hbm[hd * 64:(hd + 1) * 64, :],
                        waits=qfree2[b] + tok_qk, sem=f"d_at_q{b}")
            tk = em.dma("sync", kh[b][:], self.k_hbm[hd * 64:(hd + 1) * 64, :],
                        waits=qfree2[b] + tok_qk, sem=f"d_at_q{b}")
            tv = em.dma("gpsimd",
                        vth[b].rearrange("p (k c) -> p k c", c=64),
                        self.v_hbm.rearrange("(k p) f -> p k f", p=128)[
                            :, :, hd * 64:(hd + 1) * 64],
                        waits=vfree2[b] + tok_v, sem=f"d_at_v{b}")
            qfree2[b] = []
            vfree2[b] = []

            def dst_fn(g, hd=hd):
                oi = g % 2
                w = ofree[oi]
                ofree[oi] = []
                return ost[oi][:], w

            def done_fn(g, oe, hd=hd):
                oi = g % 2
                td = em.dma("sync", self.o_hbm[hd * 64:(hd + 1) * 64,
                                               g * G * 128:(g + 1) * G * 128],
                            ost[oi][:], waits=[oe], sem=f"d_o{oi}")
                ofree[oi] = [td]

            qk_cons, v_cons = self.attn_core2(
                qh[b], kh[b], vth[b], SCALE_A, dst_fn, done_fn, tq, tk, tv)
            qfree2[b] += qk_cons
            vfree2[b] += v_cons
        tok_o = [em.stream_tok(f"d_o{i}") for i in range(2)]
        self.tap_hbm("tap_o", self.o_hbm, tok_o)

        self.new_phase()
        wcpT = self.sbp("wcpT", 128, 8 * D, BF16)
        bcp_cols = self.sbp("bcp_cols", 128, 8)
        tok_wcp = self.gen_weight_fam("cp", D, D, wcpT, waits=self.fence)
        tok_bcp = self.gen_bias_cols("cp", D, bcp_cols, self.fence)
        wcp_done = tok_wcp[-1]
        self.stb = self.sbp("c1stb", 128, 8 * 512, BF16)
        self.stq = self.sbp("c1stq", 128, 8 * 512, BF16)
        self.stf = self.sbp("c1stf", 128, 512)
        ot_b = [self.sbp(f"c1o{i}", 128, 8 * 512, BF16) for i in range(2)]
        x_b = [self.sbp(f"c1x{i}", 128, 8 * 512) for i in range(2)]
        ocp_b = self.sbp("c1ocp", 128, 512)
        r_b = [self.sbp(f"c1r{i}", 128, 8 * 512) for i in range(2)]
        nsc1 = self.sbp("c1nsc", 128, 512)
        ofree1 = {0: [], 1: []}
        xfree1 = {0: [], 1: []}
        rfree1 = {0: [], 1: []}
        prev_rep = []
        for tt in range(TT):
            b = tt % 2
            ob = ot_b[b]
            xb = x_b[b]
            rb = r_b[b]
            tok_o_ld = em.dma("sync", ob.rearrange("p (c t) -> p c t", c=8),
                              self.o_hbm.rearrange("(c p) t -> p c t", p=128)[
                                  :, :, tt * 512:(tt + 1) * 512],
                              waits=ofree1[b] + tok_o + self.fence, sem=f"d_c1o{b}")
            tok_x_ld = em.dma("sync", xb.rearrange("p (c t) -> p c t", c=8),
                              self.x_hbm.rearrange("(c p) t -> p c t", p=128)[
                                  :, :, tt * 512:(tt + 1) * 512],
                              waits=xfree1[b] + self.tok_x_ready + self.fence,
                              sem=f"d_c1x{b}")
            rtoks = self.row_math_outer(tt * 512, 512, [self.tok_stats_ready] + prev_rep)
            Aps, Bps, rep = self.rep_rows(self.rA[0:1, 0:512], self.rB[0:1, 0:512],
                                          512, rtoks, banks=(4, 5))
            prev_rep = rep
            rstat_toks = []
            for oc in range(8):
                bank = oc % 2
                pb, bw = self.ps.get(bank)
                mm = None
                for ic in range(8):
                    mm = self.t(lambda h, ic=ic, oc=oc, ob=ob, pb=pb,
                                       st=(ic == 0), sp=(ic == 7):
                                h.matmul(pb[:, 0:512],
                                         wcpT[:, ic * D + oc * 128: ic * D + oc * 128 + 128],
                                         ob[:, ic * 512:(ic + 1) * 512],
                                         start=st, stop=sp),
                                waits=[tok_o_ld, wcp_done] + bw)
                ev = self.s(lambda h, pb=pb, oc=oc: h.activation(
                    ocp_b[:], pb[:, 0:512], AF.Identity, bias=bcp_cols[:, oc:oc + 1]),
                    waits=[mm] + tok_bcp + getattr(self, "_ocp_prev", []))
                self.ps.consumed(bank, ev)
                u1 = self.v(lambda h, oc=oc, xb=xb: h.scalar_tensor_tensor(
                    nsc1[:, 0:512], xb[:, oc * 512:(oc + 1) * 512], 1.0,
                    Aps[:, 0:512], ALU.mult, ALU.mult),
                    waits=[tok_x_ld] + rep + rfree1[b])
                u2 = self.v(lambda h: h.scalar_tensor_tensor(
                    nsc1[:, 0:512], nsc1[:, 0:512], 1.0, Bps[:, 0:512],
                    ALU.mult, ALU.add))
                u3 = self.v(lambda h, oc=oc, rb=rb: h.tensor_add(
                    rb[:, oc * 512:(oc + 1) * 512], nsc1[:, 0:512], ocp_b[:]),
                    waits=[ev])
                self._ocp_prev = [u3]
                u4 = self.v(lambda h, oc=oc, xb=xb, rb=rb: h.tensor_add(
                    xb[:, oc * 512:(oc + 1) * 512], xb[:, oc * 512:(oc + 1) * 512],
                    rb[:, oc * 512:(oc + 1) * 512]))
                self.ps.consumed(4, u2)
                self.ps.consumed(5, u2)
                rstat_toks.append(u4)
            td_r = em.dma("sync", self.r_hbm.rearrange("(c p) t -> p c t", p=128)[
                :, :, tt * 512:(tt + 1) * 512],
                rb.rearrange("p (c t) -> p c t", c=8), waits=rstat_toks, sem=f"d_r{b}")
            td_x2 = em.dma("sync", self.x2_hbm.rearrange("(c p) t -> p c t", p=128)[
                :, :, tt * 512:(tt + 1) * 512],
                xb.rearrange("p (c t) -> p c t", c=8), waits=rstat_toks,
                sem=f"d_x2{b}")
            rows, casts = self.stats_tile(
                lambda ic, rb=rb: rb[:, ic * 512:(ic + 1) * 512], 512, tt * 512,
                rstat_toks)
            ofree1[b] = rstat_toks
            xfree1[b] = rstat_toks + [td_x2]
            rfree1[b] = casts + [td_r]
        tok_r = [em.stream_tok("d_r0"), em.stream_tok("d_r1")]
        tok_x2 = [em.stream_tok("d_x20"), em.stream_tok("d_x21")]
        self.tap_hbm("tap_r", self.r_hbm, tok_r)

        self.new_phase()
        wcfT = self.sbp("wcfT", 128, 8 * FI, BF16)
        self.nrmA = self.sbp("nrmA_5", 128, 512)
        self.nrmB = self.sbp("nrmB_5", 128, 512)
        bcf_cols = self.sbp("bcf_cols", 128, 32)
        ln2g = self.sbp("ln2g", 128, 8)
        ln2b = self.sbp("ln2b", 128, 8)
        tok_wcf = self.gen_weight_fam("cf", FI, D, wcfT, waits=self.fence)
        tok_bcf = self.gen_bias_cols("cf", FI, bcf_cols, self.fence)
        tok_g2 = self.load_cols(self.inp["L0_attn_a_ln2_g"], 8, ln2g, self.fence)
        tok_b2 = self.load_cols(self.inp["L0_attn_a_ln2_b"], 8, ln2b, self.fence)
        ln2_tok = em.stream_tok("d_small")
        wcf_done = tok_wcf[-1]
        r5 = [self.sbp(f"a5r{i}", 128, 8 * 512) for i in range(2)]
        h2 = [self.sbp(f"a5h{i}", 128, 8 * 512, BF16) for i in range(2)]
        nsc5 = self.sbp("a5nsc", 128, 512)
        gsc1 = self.sbp("a5g1", 128, 512)
        gsc2 = self.sbp("a5g2", 128, 512)
        gst = [self.sbp(f"a5gs{i}", 128, 512, BF16) for i in range(4)]
        rfree5 = {0: [], 1: []}
        hfree5 = {0: [], 1: []}
        gfree5 = {i: [] for i in range(4)}
        prev_rep = []
        for tt in range(TT):
            b = tt % 2
            rb = r5[b]
            hb = h2[b]
            tok_r_ld = em.dma("sync", rb.rearrange("p (c t) -> p c t", c=8),
                              self.r_hbm.rearrange("(c p) t -> p c t", p=128)[
                                  :, :, tt * 512:(tt + 1) * 512],
                              waits=rfree5[b] + tok_r + self.fence, sem=f"d_a5r{b}")
            rtoks = self.row_math_outer(tt * 512, 512, [self.tok_stats_ready] + prev_rep)
            Aps, Bps, rep = self.rep_rows(self.rA[0:1, 0:512], self.rB[0:1, 0:512],
                                          512, rtoks, banks=(4, 5))
            prev_rep = rep
            htoks = self.normalize_tile(
                lambda ic, rb=rb: rb[:, ic * 512:(ic + 1) * 512],
                lambda ic, hb=hb: hb[:, ic * 512:(ic + 1) * 512],
                512, Aps, Bps, rep, (4, 5),
                waits=[tok_r_ld, ln2_tok] + hfree5[b], gcol=ln2g, bcol=ln2b,
                scratch=nsc5)
            rfree5[b] = htoks
            hfree5[b] = []
            for oc in range(32):
                bank = oc % 2
                pb, bw = self.ps.get(bank)
                mm = None
                for ic in range(8):
                    mm = self.t(lambda h, ic=ic, oc=oc, hb=hb, pb=pb,
                                       st=(ic == 0), sp=(ic == 7):
                                h.matmul(pb[:, 0:512],
                                         wcfT[:, ic * FI + oc * 128: ic * FI + oc * 128 + 128],
                                         hb[:, ic * 512:(ic + 1) * 512],
                                         start=st, stop=sp),
                                waits=htoks + bw + [wcf_done])
                hfree5[b].append(mm)
                e0 = self.s(lambda h, pb=pb, oc=oc: h.activation(
                    gsc1[:], pb[:, 0:512], AF.Identity, bias=bcf_cols[:, oc:oc + 1]),
                    waits=[mm] + tok_bcf + getattr(self, "_g_prev", []))
                self.ps.consumed(bank, e0)
                e1 = self.s(lambda h: h.activation(gsc2[:], gsc1[:], AF.Square))
                e2 = self.v(lambda h: h.tensor_mul(gsc2[:], gsc2[:], gsc1[:]),
                            waits=[e1])
                e3 = self.v(lambda h: h.scalar_tensor_tensor(
                    gsc2[:], gsc2[:], GELU_A, gsc1[:], ALU.mult, ALU.add))
                e4 = self.s(lambda h: h.activation(gsc2[:], gsc2[:], AF.Tanh,
                                                   scale=GELU_C), waits=[e3])
                e5 = self.v(lambda h: h.tensor_scalar(gsc2[:], gsc2[:], 0.5, 0.5,
                                                      ALU.mult, ALU.add), waits=[e4])
                gi = oc % 4
                e6 = self.v(lambda h, gi=gi: h.tensor_mul(gst[gi][:], gsc1[:],
                                                          gsc2[:]),
                            waits=gfree5[gi])
                self._g_prev = [e6]
                td = em.dma("sync", self.g_hbm[oc * 128:(oc + 1) * 128,
                                               tt * 512:(tt + 1) * 512],
                            gst[gi][:], waits=[e6], sem=f"d_g{gi}")
                gfree5[gi] = [td]
        tok_gh = [em.stream_tok(f"d_g{i}") for i in range(4)]

        self.new_phase()
        wcmT = self.sbp("wcmT", 128, 32 * D, BF16)
        bcm_cols = self.sbp("bcm_cols", 128, 8)
        tok_wcm = self.gen_weight_fam("cm", D, FI, wcmT, waits=self.fence)
        tok_bcm = self.gen_bias_cols("cm", D, bcm_cols, self.fence)
        wcm_done = tok_wcm[-1]
        self.stb = self.sbp("b5stb", 128, 8 * 512, BF16)
        self.stq = self.sbp("b5stq", 128, 8 * 512, BF16)
        self.stf = self.sbp("b5stf", 128, 512)
        g5 = [self.sbp("b5g0", 128, 32 * 512, BF16)]
        x25 = [self.sbp("b5x20", 128, 8 * 512)]
        msc = self.sbp("b5msc", 128, 512)
        gfree = {0: []}
        x2free = {0: []}
        for tt in range(TT):
            b = 0
            gb = g5[b]
            x2b = x25[b]
            xnb = x2b
            tok_g_ld = em.dma("sync", gb.rearrange("p (c t) -> p c t", c=32),
                              self.g_hbm.rearrange("(c p) t -> p c t", p=128)[
                                  :, :, tt * 512:(tt + 1) * 512],
                              waits=gfree[b] + tok_gh + self.fence, sem="d_b5g")
            tok_x2_ld = em.dma("sync", x2b.rearrange("p (c t) -> p c t", c=8),
                               self.x2_hbm.rearrange("(c p) t -> p c t", p=128)[
                                   :, :, tt * 512:(tt + 1) * 512],
                               waits=x2free[b] + tok_x2 + self.fence, sem="d_b5x")
            xn_toks = []
            for oc in range(8):
                bank = oc % 2
                pb, bw = self.ps.get(bank)
                mm = None
                for ic in range(32):
                    mm = self.t(lambda h, ic=ic, oc=oc, gb=gb, pb=pb,
                                       st=(ic == 0), sp=(ic == 31):
                                h.matmul(pb[:, 0:512],
                                         wcmT[:, ic * D + oc * 128: ic * D + oc * 128 + 128],
                                         gb[:, ic * 512:(ic + 1) * 512],
                                         start=st, stop=sp),
                                waits=[tok_g_ld, wcm_done] + bw)
                gfree[b].append(mm)
                e0 = self.s(lambda h, pb=pb, oc=oc: h.activation(
                    msc[:], pb[:, 0:512], AF.Identity, bias=bcm_cols[:, oc:oc + 1]),
                    waits=[mm] + tok_bcm + getattr(self, "_m_prev", []))
                self.ps.consumed(bank, e0)
                e1 = self.v(lambda h, oc=oc, x2b=x2b, xnb=xnb: h.tensor_add(
                    xnb[:, oc * 512:(oc + 1) * 512], msc[:],
                    x2b[:, oc * 512:(oc + 1) * 512]),
                    waits=[e0, tok_x2_ld] + getattr(self, "_xn_prev", []))
                self._m_prev = [e1]
                xn_toks.append(e1)
            td = em.dma("sync", self.x_hbm.rearrange("(c p) t -> p c t", p=128)[
                :, :, tt * 512:(tt + 1) * 512],
                xnb.rearrange("p (c t) -> p c t", c=8), waits=xn_toks, sem="d_xn")
            rows, casts = self.stats_tile(
                lambda ic, xnb=xnb: xnb[:, ic * 512:(ic + 1) * 512], 512, tt * 512,
                xn_toks)
            x2free[b] = xn_toks + casts + [td]
            self._xn_prev = casts + [td]
        self.tok_x_ready = [em.stream_tok("d_xn")]
        self.tok_stats_ready = em.stream_tok("c_v")

    def p_attn_b(self):
        em = self.em
        T = self.T
        TT = self.TT
        SLC = T // 16

        self.new_phase()
        wqkv = self.sbp("bq_w", 128, 8 * I3, BF16)
        wout = self.sbp("bq_wo", 128, 8 * D, BF16)
        bq_cols = self.sbp("bq_bc", 128, 24)
        bo_cols = self.sbp("bq_bo", 128, 8)
        t_wq = em.dma("gpsimd", wqkv.rearrange("p (c o) -> p c o", c=8),
                      self.inp["L0_attn_b_W_qkv"].rearrange("(c p) o -> p c o", p=128),
                      waits=self.fence, sem="d_bqw")
        t_wo = em.dma("gpsimd", wout.rearrange("p (c o) -> p c o", c=8),
                      self.inp["L0_attn_b_W_out"].rearrange("(c p) o -> p c o", p=128),
                      waits=self.fence, sem="d_bqw")
        tok_bq = self.load_cols(self.inp["L0_attn_b_b_qkv"], 24, bq_cols, self.fence)
        tok_bo = self.load_cols(self.inp["L0_attn_b_b_out"], 8, bo_cols, self.fence)
        tok_bias = em.stream_tok("d_small")

        G = 2
        self.at_sbufs = 1
        self.attn_alloc(G, "b")
        self.stb = self.sbp("bqstb", 128, 8 * 256, BF16)
        self.stq = self.sbp("bqstq", 128, 8 * 256, BF16)
        self.stf = self.sbp("bqstf", 128, 256)
        qp = self.sbp("bq_qp", 64, T, BF16)
        kp = self.sbp("bq_kp", 64, T, BF16)
        vf = self.sbp("bq_vf", 64, T, BF16)
        vtk = self.sbp("bq_vtk", 128, (T // 128) * 64, BF16)
        ohb = vf
        xs = self.sbp("bq_xs", 128, 8 * SLC)
        xns = self.sbp("bq_xns", 128, 8 * SLC, BF16)
        nscb = self.sbp("bq_nsc", 128, SLC)
        qkvs = [self.sbp(f"bq_qkvs{i}", 128, SLC, BF16) for i in range(2)]
        sasl = self.sbp("bq_sa", 128, 8 * SLC, BF16)
        xnew = xs
        mscb = nscb

        xsfree = list(self.fence)
        xnsfree = list(self.fence)
        qkvsfree = list(self.fence)
        qpfree = list(self.fence)
        kpfree = list(self.fence)
        vffree = list(self.fence)
        vtkfree = list(self.fence)
        ohfree = {g: list(self.fence) for g in range(2)}
        safree = list(self.fence)
        xnewfree = list(self.fence)
        prev_rep = []

        def prep_slice(hd):
            nonlocal xsfree, xnsfree, qkvsfree, qpfree, kpfree, vffree, \
                vtkfree, prev_rep
            col0 = hd * SLC
            tok_x = em.dma("sync", xs.rearrange("p (c t) -> p c t", c=8),
                           self.x_hbm.rearrange("(c p) t -> p c t", p=128)[
                               :, :, col0:col0 + SLC],
                           waits=xsfree + self.tok_x_ready, sem="d_bqx")
            rtoks = self.row_math_outer(col0, SLC, [self.tok_stats_ready] + prev_rep)
            Aps, Bps, rep = self.rep_rows(self.rA[0:1, 0:SLC], self.rB[0:1, 0:SLC],
                                          SLC, rtoks, banks=(4, 5))
            prev_rep = rep
            xn_toks = []
            for ic in range(8):
                u1 = self.v(lambda h, ic=ic: h.scalar_tensor_tensor(
                    nscb[:, 0:SLC], xs[:, ic * SLC:(ic + 1) * SLC], 1.0,
                    Aps[:, 0:SLC], ALU.mult, ALU.mult),
                    waits=[tok_x] + rep)
                u2 = self.v(lambda h, ic=ic: h.scalar_tensor_tensor(
                    xns[:, ic * SLC:(ic + 1) * SLC], nscb[:, 0:SLC], 1.0,
                    Bps[:, 0:SLC], ALU.mult, ALU.add), waits=xnsfree)
                xn_toks.append(u2)
            xnsfree = []
            for bk in (4, 5):
                self.ps.consumed(bk, xn_toks[-1])
            cp_toks = []
            qfree_in = {0: list(qkvsfree), 1: list(qkvsfree)}
            for oc in range(24):
                bank = oc % 2
                pb, bw = self.ps.get(bank)
                mm = None
                for ic in range(8):
                    mm = self.t(lambda h, ic=ic, oc=oc, pb=pb,
                                       st=(ic == 0), sp=(ic == 7):
                                h.matmul(pb[:, 0:SLC],
                                         wqkv[:, ic * I3 + oc * 128: ic * I3 + oc * 128 + 128],
                                         xns[:, ic * SLC:(ic + 1) * SLC],
                                         start=st, stop=sp),
                                waits=xn_toks + bw + [t_wq])
                qb = qkvs[oc % 2]
                ev = self.s(lambda h, pb=pb, oc=oc, qb=qb: h.activation(
                    qb[:], pb[:, 0:SLC], AF.Identity,
                    bias=bq_cols[:, oc:oc + 1]),
                    waits=[mm] + tok_bq + qfree_in[oc % 2])
                self.ps.consumed(bank, ev)
                qfree_in[oc % 2] = []
                for h64 in range(2):
                    c = 128 * oc + 64 * h64
                    j, r = divmod(c, 192)
                    part = r // 64
                    dstbuf = (qp, kp, vf)[part]
                    sfree = (qpfree if part == 0 else kpfree if part == 1
                             else vffree)
                    dap = dstbuf[0:64, j: j + 16 * (SLC - 1) + 1:16]
                    tcp = self.v(lambda h, h64=h64, dap=dap, qb=qb:
                                 h.tensor_copy(dap, qb[64 * h64:64 * h64 + 64, :]),
                                 waits=[ev] + sfree)
                    sfree.clear()
                    cp_toks.append(tcp)
                    qfree_in[oc % 2].append(tcp)
            qpfree = []
            kpfree = []
            vffree = []
            vtr_toks = []
            for kt in range(T // 128):
                seq = len(self.at_bfseq)
                pbk = self.ps.t["bf" if seq % 2 == 0 else "bf2"]
                stag = (self.at_bfw0 if seq < 2 else [self.at_bfseq[seq - 2]])
                ttr = self.t(lambda h, kt=kt, pbk=pbk: h.transpose(
                    pbk[:, 0:64],
                    vf[0:64, kt * 128:(kt + 1) * 128],
                    self.ident_bf[0:64, 0:64]),
                    waits=cp_toks + stag + [self.tok_const])
                tev = self.v(lambda h, kt=kt, pbk=pbk:
                             h.tensor_copy(vtk[:, kt * 64:(kt + 1) * 64],
                                           pbk[:, 0:64]),
                             waits=[ttr] + vtkfree)
                vtkfree = []
                self.at_bfseq.append(tev)
                vtr_toks.append(tev)
            xsfree = xn_toks
            self._bq_x_tok = tok_x
            return cp_toks, vtr_toks, tok_x

        def post_slice(hd, oh, x_tok):
            nonlocal safree, xnewfree, xsfree
            col0 = hd * SLC
            sa_toks = []
            for j in range(16):
                cfull = 64 * j
                oc, off = divmod(cfull, 128)
                tcp = self.v(lambda h, oc=oc, off=off, j=j, oh=oh:
                             h.tensor_copy(sasl[off:off + 64,
                                                oc * SLC:(oc + 1) * SLC],
                                           oh[0:64, j: j + 16 * (SLC - 1) + 1:16]),
                             waits=safree)
                sa_toks.append(tcp)
            safree = []
            out_toks = []
            for oc in range(8):
                bank = oc % 2
                pb, bw = self.ps.get(bank)
                mm = None
                for ic in range(8):
                    mm = self.t(lambda h, ic=ic, oc=oc, pb=pb,
                                       st=(ic == 0), sp=(ic == 7):
                                h.matmul(pb[:, 0:SLC],
                                         wout[:, ic * D + oc * 128: ic * D + oc * 128 + 128],
                                         sasl[:, ic * SLC:(ic + 1) * SLC],
                                         start=st, stop=sp),
                                waits=sa_toks + bw + [t_wo])
                e0 = self.s(lambda h, pb=pb, oc=oc: h.activation(
                    mscb[:], pb[:, 0:SLC], AF.Identity, bias=bo_cols[:, oc:oc + 1]),
                    waits=[mm] + tok_bo + getattr(self, "_bq_m_prev", []))
                self.ps.consumed(bank, e0)
                e1 = self.v(lambda h, oc=oc: h.tensor_add(
                    xnew[:, oc * SLC:(oc + 1) * SLC], mscb[:],
                    xs[:, oc * SLC:(oc + 1) * SLC]),
                    waits=[e0, x_tok] + xnewfree)
                self._bq_m_prev = [e1]
                out_toks.append(e1)
            xnewfree = []
            td = em.dma("sync", self.x_hbm.rearrange("(c p) t -> p c t", p=128)[
                :, :, col0:col0 + SLC],
                xnew.rearrange("p (c t) -> p c t", c=8), waits=out_toks,
                sem="d_bqo")
            if not getattr(self, "SKIP_BQ_STATS", False):
                rows, casts = self.stats_tile(
                    lambda ic: xnew[:, ic * SLC:(ic + 1) * SLC], SLC, col0, out_toks)
                xnewfree = casts + [td]
            else:
                xnewfree = out_toks + [td]
            xsfree = xsfree + out_toks + list(xnewfree)
            return sa_toks

        for hd in range(H):
            cp_toks, vtr_toks, x_tok = prep_slice(hd)
            tok_qk = ("c_v", em.cnt["c_v"])
            tok_v2 = ("c_v", em.cnt["c_v"])

            def dst_fn(g):
                w = ohfree[g % 2] + safree
                ohfree[g % 2] = []
                return ohb[0:64, g * G * 128:(g + 1) * G * 128], w

            def done_fn(g, oe):
                ohfree[g % 2].append(oe)

            qk_cons, v_cons = self.attn_core2(
                qp, kp, vtk, SCALE_B, dst_fn, done_fn, tok_qk, tok_qk, tok_v2)
            qpfree += qk_cons
            kpfree += qk_cons
            vtkfree = vtkfree + v_cons
            tx = em.dma("sync", xs.rearrange("p (c t) -> p c t", c=8),
                        self.x_hbm.rearrange("(c p) t -> p c t", p=128)[
                            :, :, hd * SLC:(hd + 1) * SLC],
                        waits=xsfree + self.tok_x_ready, sem="d_bqx")
            xsfree = []
            sa = post_slice(hd, ohb, tx)
            safree = list(sa)
        self.tok_x_ready = [em.stream_tok("d_bqo")]
        self.tok_stats_ready = em.stream_tok("c_v")

    def p_ffn(self, idx):
        em = self.em
        TT = self.TT
        T = self.T
        sfx = f"f{idx}"

        self.new_phase()
        w_em = self.sbp(f"{sfx}_wem", NE, T, BF16)


        self.push_phase()
        xf = [self.sbp(f"{sfx}_x{i}", 128, 8 * 512) for i in range(2)]
        xnb = [self.sbp(f"{sfx}_xn{i}", 128, 8 * 512, BF16) for i in range(2)]
        xsc = self.sbp(f"{sfx}_xsc", 128, 512)
        xsc2 = self.sbp(f"{sfx}_xsc2", 128, 512)
        wg = self.sbp(f"{sfx}_wg", 128, 64)
        gt1 = self.sbp(f"{sfx}_gt1", 128, 8)
        gt2 = self.sbp(f"{sfx}_gt2", 128, 8)
        gm1 = self.sbp(f"{sfx}_gm1", 128, 1)
        gm2 = self.sbp(f"{sfx}_gm2", 128, 1)
        gsum = self.sbp(f"{sfx}_gs", 128, 1)
        weq = self.sbp(f"{sfx}_weq", 128, 8)
        tok_wg = em.dma("scalar", wg.rearrange("p (c e) -> p c e", e=NE),
                        self.inp["L0_ffn_Wg"].rearrange("(c p) e -> p c e", p=128),
                        waits=self.fence, sem="d_small")
        xfree = {0: [], 1: []}
        xnfree = {0: list(self.fence), 1: list(self.fence)}
        prev_rep = []
        for tt in range(TT):
            b = tt % 2
            x = xf[b]
            xn = xnb[b]
            tok_x = em.dma("sync", x.rearrange("p (c t) -> p c t", c=8),
                           self.x_hbm.rearrange("(c p) t -> p c t", p=128)[
                               :, :, tt * 512:(tt + 1) * 512],
                           waits=xfree[b] + self.tok_x_ready + self.fence,
                           sem=f"d_{sfx}x{b}")
            rtoks = self.row_math_outer(tt * 512, 512,
                                        [self.tok_stats_ready] + prev_rep)
            Aps, Bps, rep = self.rep_rows(self.rA[0:1, 0:512], self.rB[0:1, 0:512],
                                          512, rtoks, banks=(4, 5))
            prev_rep = rep
            gp = {}
            gpw = {}
            for sub in range(4):
                gp[sub], gpw[sub] = self.ps.get(sub)
            g_mms = {sub: None for sub in range(4)}
            xn_toks = []
            for ic in range(8):
                u1 = self.v(lambda h, ic=ic, x=x: h.scalar_tensor_tensor(
                    xsc[:, 0:512], x[:, ic * 512:(ic + 1) * 512], 1.0,
                    Aps[:, 0:512], ALU.mult, ALU.mult),
                    waits=[tok_x] + rep)
                u2 = self.v(lambda h: h.scalar_tensor_tensor(
                    xsc2[:, 0:512], xsc[:, 0:512], 1.0, Bps[:, 0:512],
                    ALU.mult, ALU.add))
                cv = self.s(lambda h, ic=ic, xn=xn: h.activation(
                    xn[:, ic * 512:(ic + 1) * 512], xsc2[:, 0:512], AF.Copy),
                    waits=[u2] + xnfree[b])
                xn_toks.append(cv)
                for sub in range(4):
                    g_mms[sub] = self.t(lambda h, ic=ic, sub=sub, gp=gp:
                                        h.matmul(gp[sub][:, 0:8],
                                                 xsc2[:, sub * 128:(sub + 1) * 128],
                                                 wg[:, ic * 8:(ic + 1) * 8],
                                                 start=(ic == 0), stop=(ic == 7)),
                                        waits=[u2, tok_wg] + (gpw[sub] if ic == 0 else []))
                self.em.last["vector"] = u2
                if ic < 7:
                    for s in range(4):
                        self.em.wait("vector", g_mms[s])
                    self.em.wait("vector", cv)
            td_xn = em.dma("sync", self.xn_hbm.rearrange("(c p) t -> p c t", p=128)[
                :, :, tt * 512:(tt + 1) * 512],
                xn.rearrange("p (c t) -> p c t", c=8), waits=xn_toks,
                sem=f"d_{sfx}xn{b}")
            xnfree[b] = [td_xn]
            xfree[b] = xn_toks
            for sub in range(4):
                pbk = gp[sub]
                e0 = self.s(lambda h, pbk=pbk: h.activation(
                    gt1[:], pbk[:, 0:8], AF.Exp, scale=-1.0),
                    waits=[g_mms[sub]] + getattr(self, "_gate_prev", []))
                e1 = self.v(lambda h: h.tensor_scalar(gt1[:], gt1[:], 1.0, None,
                                                      ALU.add), waits=[e0])
                e2 = self.v(lambda h: h.reciprocal(gt1[:], gt1[:]))
                e3 = self.v(lambda h: h.reduce_max(gm1[:], gt1[:], axis=AX.X))
                e4 = self.v(lambda h: h.tensor_single_scalar(weq[:], gt1[:], gm1[:],
                                                             ALU.is_equal))
                e5 = self.v(lambda h: h.scalar_tensor_tensor(
                    gt2[:], weq[:], -10000.0, gt1[:], ALU.mult, ALU.add))
                e6 = self.v(lambda h: h.reduce_max(gm2[:], gt2[:], axis=AX.X))
                e7 = self.v(lambda h: h.tensor_add(gsum[:], gm1[:], gm2[:]))
                e8 = self.v(lambda h: h.reciprocal(gsum[:], gsum[:]))
                e11 = self.v(lambda h: h.tensor_single_scalar(gt2[:], gt2[:], gm2[:],
                                                              ALU.is_equal))
                e9 = self.v(lambda h: h.tensor_mul(gm1[:], gm1[:], gsum[:]))
                e10 = self.v(lambda h: h.tensor_mul(gm2[:], gm2[:], gsum[:]))
                e12 = self.v(lambda h: h.tensor_scalar(weq[:], weq[:], gm1[:], None,
                                                       ALU.mult))
                e13 = self.v(lambda h: h.scalar_tensor_tensor(
                    weq[:], gt2[:], gm2[:], weq[:], ALU.mult, ALU.add))
                self.ps.consumed(sub, e0)
                tpb2, tbw2 = self.ps.get(sub)
                ttr = self.t(lambda h, tpb2=tpb2: h.transpose(
                    tpb2[0:8, 0:128], weq[:, 0:8], self.ident_f[:]),
                    waits=[e13, self.tok_const] + tbw2)
                dst = w_em[0:8, tt * 512 + sub * 128: tt * 512 + (sub + 1) * 128]
                tev = self.s(lambda h, tpb2=tpb2, dst=dst: h.activation(
                    dst, tpb2[0:8, 0:128], AF.Copy),
                    waits=[ttr] + getattr(self, "_wem_prev", self.fence))
                self.ps.consumed(sub, tev)
                self._gate_prev = [e13, tev]
                self._wem_prev = []
        tok_xn = [em.stream_tok(f"d_{sfx}xn{i}") for i in range(2)]
        tok_wem = em.stream_tok("c_s")
        self.tap_hbm(f"tap_xn{idx}", self.xn_hbm, tok_xn)

        self.pop_phase()
        self.push_phase()
        ohu = self.sbp(f"{sfx}_ohu", NE, 128, BF16)
        ohu_free = list(self.fence)
        w1 = self.sbp(f"{sfx}_w1", 128, 8 * FF, BF16)
        w3 = self.sbp(f"{sfx}_w3", 128, 8 * FF, BF16)
        w2 = self.sbp(f"{sfx}_w2", 128, 16 * D, BF16)
        xnu = self.sbp(f"{sfx}_xnu", 128, 8 * 512, BF16)
        xnw = self.sbp(f"{sfx}_xnw", 128, 8 * 512, BF16)
        hst = self.sbp(f"{sfx}_h", 128, 16 * 512, BF16)
        usg = self.sbp(f"{sfx}_u", 128, 512, BF16)
        vsg = self.sbp(f"{sfx}_v", 128, 512, BF16)
        acs = self.sbp(f"{sfx}_acs", 128, 4 * 512)
        wrp = self.sbp(f"{sfx}_wrp", 128, 512, BF16)
        w1free = list(self.fence)
        w3free = list(self.fence)
        w2free = list(self.fence)
        xnufree = list(self.fence)
        xnwfree = list(self.fence)
        hfree = list(self.fence)
        acsfree = list(self.fence)
        wrpfree = list(self.fence)
        self._ffn_uprev = list(self.fence)
        self._ffn_vprev = list(self.fence)
        acc_tok = {tt: None for tt in range(TT)}
        units = [("s", None)] + [("e", e) for e in range(NE)]
        for ui, (kind, e) in enumerate(units):
            if kind == "s":
                src1 = self.inp["L0_ffn_Ws1"]
                src3 = self.inp["L0_ffn_Ws3"]
                src2 = self.inp["L0_ffn_Ws2"]
            else:
                src1 = self.inp["L0_ffn_We1"][e]
                src3 = self.inp["L0_ffn_We3"][e]
                src2 = self.inp["L0_ffn_We2"][e]
            if kind == "e":
                t_oh = em.dma("gpsimd", ohu[:], self.inp["c_onehot"][:, e * 128:(e + 1) * 128],
                              waits=ohu_free, sem=f"d_{sfx}oh")
            t1 = em.dma("gpsimd", w1.rearrange("p (c o) -> p c o", c=8),
                        src1.rearrange("(c p) o -> p c o", p=128),
                        waits=w1free, sem=f"d_{sfx}w1")
            t3 = em.dma("gpsimd", w3.rearrange("p (c o) -> p c o", c=8),
                        src3.rearrange("(c p) o -> p c o", p=128),
                        waits=w3free, sem=f"d_{sfx}w3")
            t2 = em.dma("gpsimd", w2.rearrange("p (c o) -> p c o", c=16),
                        src2.rearrange("(c p) o -> p c o", p=128),
                        waits=w2free, sem=f"d_{sfx}w2")
            w1free, w3free, w2free = [], [], []
            for tt in range(TT):
                tok_xn_ld = em.dma("sync", xnu.rearrange("p (c t) -> p c t", c=8),
                                   self.xn_hbm.rearrange("(c p) t -> p c t", p=128)[
                                       :, :, tt * 512:(tt + 1) * 512],
                                   waits=xnufree + tok_xn, sem=f"d_{sfx}xnu")
                xin = xnu
                xin_tok = [tok_xn_ld]
                if kind == "e":
                    rpb, rbw = self.ps.get(5)
                    rm = self.t(lambda h, e=e, tt=tt, rpb=rpb: h.matmul(
                        rpb[:, 0:512], ohu[:, 0:128],
                        w_em[0:8, tt * 512:(tt + 1) * 512], start=True, stop=True),
                        waits=[t_oh, tok_wem] + rbw)
                    re = self.s(lambda h, rpb=rpb: h.activation(
                        wrp[:], rpb[:, 0:512], AF.Copy), waits=[rm] + wrpfree)
                    self.ps.consumed(5, re)
                    xw_toks = []
                    for ic in range(8):
                        xw = self.v(lambda h, ic=ic: h.tensor_mul(
                            xnw[:, ic * 512:(ic + 1) * 512],
                            xnu[:, ic * 512:(ic + 1) * 512], wrp[:]),
                            waits=[re, tok_xn_ld] + xnwfree)
                        xw_toks.append(xw)
                    xnwfree = []
                    wrpfree = xw_toks
                    xin = xnw
                    xin_tok = xw_toks
                h_toks = []
                for oc in range(16):
                    bu = oc % 2
                    bv = 2 + oc % 2
                    pu, puw = self.ps.get(bu)
                    pv, pvw = self.ps.get(bv)
                    mmu = None
                    mmv = None
                    for ic in range(8):
                        mmu = self.t(lambda h, ic=ic, oc=oc, pu=pu,
                                            st=(ic == 0), sp=(ic == 7):
                                     h.matmul(pu[:, 0:512],
                                              w1[:, ic * FF + oc * 128: ic * FF + oc * 128 + 128],
                                              xnu[:, ic * 512:(ic + 1) * 512],
                                              start=st, stop=sp),
                                     waits=[t1, tok_xn_ld] + (puw if ic == 0 else []))
                        mmv = self.t(lambda h, ic=ic, oc=oc, pv=pv, xin=xin,
                                            st=(ic == 0), sp=(ic == 7):
                                     h.matmul(pv[:, 0:512],
                                              w3[:, ic * FF + oc * 128: ic * FF + oc * 128 + 128],
                                              xin[:, ic * 512:(ic + 1) * 512],
                                              start=st, stop=sp),
                                     waits=[t3] + xin_tok + (pvw if ic == 0 else []))
                    eu = self.s(lambda h, pu=pu: h.activation(usg[:], pu[:, 0:512],
                                                              AF.Silu),
                                waits=[mmu] + getattr(self, "_ffn_uprev", []))
                    self.ps.consumed(bu, eu)
                    ev = self.v(lambda h, pv=pv: h.tensor_copy(vsg[:], pv[:, 0:512]),
                                waits=[mmv] + getattr(self, "_ffn_vprev", []))
                    self.ps.consumed(bv, ev)
                    hh = self.v(lambda h, oc=oc: h.tensor_mul(
                        hst[:, oc * 512:(oc + 1) * 512], usg[:], vsg[:]),
                        waits=[eu] + hfree)
                    self._ffn_uprev = [hh]
                    self._ffn_vprev = [hh]
                    h_toks.append(hh)
                hfree = []
                xnufree = h_toks
                if kind == "e":
                    xnwfree = h_toks
                accw = [acc_tok[tt]] if acc_tok[tt] is not None else list(self.fence)
                for half in range(2):
                    ac_toks = []
                    for oc4 in range(4):
                        oc = half * 4 + oc4
                        bk = oc % 2
                        po, pow_ = self.ps.get(bk)
                        mmo = None
                        for ic in range(16):
                            mmo = self.t(lambda h, ic=ic, oc=oc, po=po,
                                                st=(ic == 0), sp=(ic == 15):
                                         h.matmul(po[:, 0:512],
                                                  w2[:, ic * D + oc * 128: ic * D + oc * 128 + 128],
                                                  hst[:, ic * 512:(ic + 1) * 512],
                                                  start=st, stop=sp),
                                         waits=[t2] + h_toks + (pow_ if ic == 0 else []))
                        hfree.append(mmo)
                        ea = self.s(lambda h, po=po, oc4=oc4: h.activation(
                            acs[:, oc4 * 512:(oc4 + 1) * 512], po[:, 0:512], AF.Copy),
                            waits=[mmo] + acsfree)
                        self.ps.consumed(bk, ea)
                        ac_toks.append(ea)
                    acsfree = []
                    td = em.dma("gpsimd",
                                self.acc_hbm.rearrange("(c p) t -> p c t", p=128)[
                                    :, half * 4:(half + 1) * 4,
                                    tt * 512:(tt + 1) * 512],
                                acs.rearrange("p (c t) -> p c t", c=4),
                                waits=ac_toks + accw, sem=f"d_{sfx}acc",
                                **({} if ui == 0 else {"accum_op": ALU.add}))
                    accw = [td]
                    acsfree = [td]
                acc_tok[tt] = td
            w1free = [td]
            w3free = [td]
            w2free = [td]
            if kind == "e":
                ohu_free = [rm]
        self.tap_hbm(f"tap_acc{idx}", self.acc_hbm,
                     [em.stream_tok(f"d_{sfx}acc")])
        if f"tap_wem{idx}" in self.taps:
            em.dma("sync", self.taps[f"tap_wem{idx}"], w_em[0:NE, 0:T],
                   waits=[tok_wem], sem="d_tap")
        self.pop_phase()
        self.push_phase()
        self.stb = self.sbp(f"{sfx}_stb", 128, 8 * 512, BF16)
        self.stq = self.sbp(f"{sfx}_stq", 128, 8 * 512, BF16)
        self.stf = self.sbp(f"{sfx}_stf", 128, 512)
        af = [self.sbp(f"{sfx}_af{i}", 128, 8 * 512) for i in range(2)]
        xpf = [self.sbp(f"{sfx}_xpf{i}", 128, 8 * 512) for i in range(2)]
        affree = {0: [], 1: []}
        xpffree = {0: [], 1: []}
        is_last = (idx == 2)
        for tt in range(TT):
            b = tt % 2
            ab = af[b]
            xb = xpf[b]
            tok_a = em.dma("sync", ab.rearrange("p (c t) -> p c t", c=8),
                           self.acc_hbm.rearrange("(c p) t -> p c t", p=128)[
                               :, :, tt * 512:(tt + 1) * 512],
                           waits=affree[b] + [acc_tok[tt],
                                              em.stream_tok(f"d_{sfx}acc")] + self.fence,
                           sem=f"d_{sfx}a{b}")
            tok_x = em.dma("sync", xb.rearrange("p (c t) -> p c t", c=8),
                           self.x_hbm.rearrange("(c p) t -> p c t", p=128)[
                               :, :, tt * 512:(tt + 1) * 512],
                           waits=xpffree[b] + self.tok_x_ready + self.fence,
                           sem=f"d_{sfx}px{b}")
            out_toks = []
            for ic in range(8):
                u1 = self.v(lambda h, ic=ic, ab=ab, xb=xb: h.scalar_tensor_tensor(
                    ab[:, ic * 512:(ic + 1) * 512], ab[:, ic * 512:(ic + 1) * 512],
                    1.0 / 3.0, xb[:, ic * 512:(ic + 1) * 512], ALU.mult, ALU.add),
                    waits=[tok_a, tok_x])
                out_toks.append(u1)
            td = em.dma("sync", self.x_hbm.rearrange("(c p) t -> p c t", p=128)[
                :, :, tt * 512:(tt + 1) * 512],
                ab.rearrange("p (c t) -> p c t", c=8), waits=out_toks,
                sem=f"d_{sfx}o{b}")
            affree[b] = [td]
            xpffree[b] = out_toks
            if not is_last:
                rows, casts = self.stats_tile(
                    lambda ic, ab=ab: ab[:, ic * 512:(ic + 1) * 512], 512, tt * 512,
                    out_toks)
                affree[b] = affree[b] + casts
        self.tok_x_ready = [em.stream_tok(f"d_{sfx}o0"), em.stream_tok(f"d_{sfx}o1")]
        self.tok_stats_ready = em.stream_tok("c_v")
        self.pop_phase()
        self.pop_phase()
        self.push_phase()

    def p0_ingest(self):
        em = self.em
        self.new_phase()
        self.p0_tm = [self.sbp(f"p0_tm{i}", 128, 4 * 1024) for i in range(2)]
        self.p0_x = [self.sbp(f"p0_x{i}", 128, 8 * 512) for i in range(2)]
        self.stb = self.sbp("p0stb", 128, 8 * 512, BF16)
        self.stq = self.sbp("p0stq", 128, 8 * 512, BF16)
        self.stf = self.sbp("p0stf", 128, 512)
        tmfree = {0: [], 1: []}
        xfree = {0: [], 1: []}
        for tt in range(self.TT):
            b = tt % 2
            tm = self.p0_tm[b]
            xf = self.p0_x[b]
            tok_ld = em.dma("sync",
                            tm.rearrange("p (s f) -> p s f", s=4),
                            self.hs[tt * 512:(tt + 1) * 512, :].rearrange(
                                "(s p) f -> p s f", p=128),
                            waits=tmfree[b], sem=f"d_p0_{b}")
            ev_toks = []
            tr_all = []
            for idx in range(32):
                s4, ic = idx // 8, idx % 8
                bank = 3 + (idx % 2)
                pb, bw = self.ps.get(bank)
                src = tm[:, s4 * 1024 + ic * 128: s4 * 1024 + ic * 128 + 128]
                tr = self.t(
                    lambda h, s=src, p=pb[:, 0:128]:
                    h.transpose(p, s, self.ident_f[:]),
                    waits=[tok_ld, self.tok_const] + bw)
                tr_all.append(tr)
                dst = xf[:, ic * 512 + s4 * 128: ic * 512 + s4 * 128 + 128]
                srcp = pb[:, 0:128]
                if idx % 2 == 0:
                    tev = self.s(lambda h, p=srcp, d=dst: h.activation(d, p, AF.Copy),
                                 waits=[tr] + xfree[b])
                else:
                    tev = self.v(lambda h, p=srcp, d=dst: h.tensor_copy(d, p),
                                 waits=[tr] + xfree[b])
                ev_toks.append(tev)
                self.ps.consumed(bank, tev)
            tmfree[b] = tr_all
            tok_st = em.dma("sync",
                            self.x_hbm.rearrange("(c p) t -> p c t", p=128)[
                                :, :, tt * 512:(tt + 1) * 512],
                            xf.rearrange("p (c t) -> p c t", c=8),
                            waits=ev_toks, sem=f"d_xst{b}")
            rows, casts = self.stats_tile(
                lambda ic, xf=xf: xf[:, ic * 512:(ic + 1) * 512], 512, tt * 512,
                ev_toks)
            xfree[b] = casts + [tok_st]
        self.tok_x_ready = [em.stream_tok("d_xst0"), em.stream_tok("d_xst1")]
        self.tok_stats_ready = em.stream_tok("c_v")

    def finish(self, ready_tok=None):
        em = self.em
        self.new_phase()
        fstage = [self.sbp(f"fin_x{i}", 128, 8 * 512) for i in range(2)]
        ostage = [self.sbp(f"fin_o{i}", 128, 4 * 1024) for i in range(2)]
        ffree = {0: [], 1: []}
        ofree = {0: [], 1: []}
        waits0 = list(ready_tok) if ready_tok is not None else list(self.tok_x_ready)
        for tt in range(self.TT):
            b = tt % 2
            xf = fstage[b]
            om = ostage[b]
            tok_ld = em.dma("sync",
                            xf.rearrange("p (c t) -> p c t", c=8),
                            self.x_hbm.rearrange("(c p) t -> p c t", p=128)[
                                :, :, tt * 512:(tt + 1) * 512],
                            waits=ffree[b] + waits0 + self.fence, sem=f"d_fin{b}")
            ev_toks = []
            tr_all = []
            for idx in range(32):
                ic, s4 = idx // 4, idx % 4
                bank = 3 + (idx % 2)
                pb, bw = self.ps.get(bank)
                src = xf[:, ic * 512 + s4 * 128: ic * 512 + s4 * 128 + 128]
                tr = self.t(
                    lambda h, s=src, p=pb[:, 0:128]:
                    h.transpose(p, s, self.ident_f[:]),
                    waits=[tok_ld, self.tok_const] + bw)
                tr_all.append(tr)
                dst = om[:, s4 * 1024 + ic * 128: s4 * 1024 + ic * 128 + 128]
                srcp = pb[:, 0:128]
                if idx % 2 == 0:
                    tev = self.s(lambda h, p=srcp, d=dst: h.activation(d, p, AF.Copy),
                                 waits=[tr] + ofree[b])
                else:
                    tev = self.v(lambda h, p=srcp, d=dst: h.tensor_copy(d, p),
                                 waits=[tr] + ofree[b])
                ev_toks.append(tev)
                self.ps.consumed(bank, tev)
            ffree[b] = tr_all
            tok_st = em.dma("sync",
                            self.out[tt * 512:(tt + 1) * 512, :].rearrange(
                                "(s p) f -> p s f", p=128),
                            om.rearrange("p (s f) -> p s f", s=4),
                            waits=ev_toks, sem=f"d_out{b}")
            ofree[b] = [tok_st]
        em.wait("sync", em.stream_tok("d_out0"))
        em.wait("sync", em.stream_tok("d_out1"))

    def build_phases(self):
        self.phase_stack = []
        self.fence = []
        self.p0_ingest()
        if self.n_sub >= 1:
            self.p_attn_a()
        if self.n_sub >= 2:
            self.p_ffn(1)
        if self.n_sub >= 3:
            self.p_attn_b()
        if self.n_sub >= 4:
            self.p_ffn(2)
        self.finish()

    def build(self, detect_races=True):
        self.nc = bass.Bass("TRN2", target_bir_lowering=False,
                            detect_race_conditions=detect_races)
        self.es = ExitStack()
        self.inp = {}
        self.declare()
        with self.es:
            self.em = Em(self.nc, self.es)
            self.ps = Psum(self.nc, self.es)
            self.alloc_persistent()
            self.build_phases()
            while self.phase_stack:
                self.pop_phase()
            self.flush_block()
        return self.nc

    def _replayer(self, eng):
        q = list(self.em.q[eng])

        def run(h):
            for fn in q:
                fn(h)
        return run


def make_in_maps(inputs, T):
    consts = host_constants()
    full = inputs["L0_attn_a_hs"]
    B = full.shape[0]
    in_maps = []
    for b in range(B):
        m = {k: np.ascontiguousarray(v) for k, v in inputs.items()
             if k != "L0_attn_a_hs"}
        m["L0_attn_a_hs"] = np.ascontiguousarray(full[b, :T])
        m.update(consts)
        in_maps.append(m)
    return in_maps


def kernel(**inputs):
    T = 4096
    bld = Builder(T=T)
    nc = bld.build()
    in_maps = make_in_maps(inputs, T)
    res = run_bass_kernel_spmd(nc, in_maps, core_ids=list(range(len(in_maps))))
    return np.stack([np.asarray(r["out"]) for r in res.results], axis=0)
